# version 03
# baseline (speedup 1.0000x reference)
import math
from contextlib import ExitStack

import numpy as np
import ml_dtypes

import concourse.bass as bass
import concourse.mybir as mybir
from concourse.bass_utils import run_bass_kernel_spmd

F32 = mybir.dt.float32
BF16 = mybir.dt.bfloat16
AF = mybir.ActivationFunctionType
ALU = mybir.AluOpType
AX = mybir.AxisListType
BF = ml_dtypes.bfloat16

T, D = 4096, 1024
TT = T // 128
NCORE = 8
E0 = 16
CAP = math.ceil(NCORE * T * 2 / E0 * 1.25)
E1 = 8
H0 = 2048
F1 = 4096
NEG = -1e30
N_BIS = 34
BIS_BOUND = 1e5

ENGS = ("sync", "tensor", "vector", "scalar", "gpsimd")
CHAIN = ("vector", "scalar", "gpsimd")


def _aslist(x):
    if x is None:
        return []
    if isinstance(x, list):
        return [t for t in x if t is not None]
    return [x]


class Sem:
    __slots__ = ("h", "c", "name")

    def __init__(self, h, name):
        self.h = h
        self.c = 0
        self.name = name


class Sched:
    def __init__(self, nc):
        self.nc = nc
        self.items = {e: [] for e in ENGS}
        self._phase_idx = 0
        self.all_sems = []
        self.chain = {e: self.sem(f"ch_{e}") for e in CHAIN}
        self._bar = self.sem("bar")

    def sem(self, name):
        s = Sem(self.nc.alloc_semaphore(name), name)
        self.all_sems.append(s)
        return s

    def op(self, eng, fn, waits=(), inc=None, dma=False):
        ws = []
        for w in waits:
            ws.extend(_aslist(w))
        if eng in CHAIN and not dma:
            assert inc is None
            cs = self.chain[eng]
            if cs.c > 0:
                ws.append((cs, cs.c))
            cs.c += 1
            self.items[eng].append((fn, tuple(ws), cs, 1))
            return (cs, cs.c)
        amt = 16 if dma else 1
        tok = None
        if inc is not None:
            inc.c += amt
            tok = (inc, inc.c)
        self.items[eng].append((fn, tuple(ws), inc, amt))
        return tok

    def barrier(self):
        bs = self._bar
        ex = tuple((s, s.c) for s in self.all_sems if s.c > 0 and s is not bs)
        for e in ENGS:
            self.items[e].append(("drain", ex, bs, 1))
            bs.c += 1
        tgt = bs.c
        for e in ENGS:
            self.items[e].append((None, ((bs, tgt),), None, 0))

    def emit_phase(self, name):
        items = self.items
        self.items = {e: [] for e in ENGS}
        self._phase_idx += 1
        with self.nc.Block(f"ph{self._phase_idx}_{name}") as block:
            for e in ENGS:
                lst = items[e]
                if not lst:
                    continue

                def body(eng, lst=lst):
                    water = {}
                    for fn, waits, inc, amt in lst:
                        for (s, v) in waits:
                            if water.get(id(s), -1) < v:
                                eng.wait_ge(s.h, v)
                                water[id(s)] = v
                        ins = None
                        if fn == "drain":
                            ins = eng.drain()
                        elif fn is not None:
                            ins = fn(eng)
                        if inc is not None:
                            if ins is None:
                                ins = eng.nop()
                            ins.then_inc(inc.h, amt)

                getattr(block, e)(body)



def emit_k1(nc, sc, px, x_d, wt_d, gw_d, gb_d, idb_d, idf_d, cm_d,
            x1p_d, xnq_d, wmf_d, wmt_d, x0xn_d):
    with ExitStack() as ctx:
        en = ctx.enter_context
        K_fm = en(nc.sbuf_tensor(px + "K_fm", [128, 8 * T], BF16)).ap()
        V_tm = en(nc.sbuf_tensor(px + "V_tm", [128, TT * 1024], BF16)).ap()
        sidb = en(nc.sbuf_tensor(px + "sidb", [128, 128], BF16)).ap()
        sidf = en(nc.sbuf_tensor(px + "sidf", [128, 128], F32)).ap()
        scm = en(nc.sbuf_tensor(px + "scm", [128, 128], F32)).ap()
        sgw = en(nc.sbuf_tensor(px + "sgw", [128, 8 * E0], BF16)).ap()
        sgb = en(nc.sbuf_tensor(px + "sgb", [128, E0], F32)).ap()

        s_t = sc.sem(px + "t")
        s_c = sc.sem(px + "c")
        s_gw = sc.sem(px + "gwl")
        s_wt = sc.sem(px + "wtl")

        sc.op("sync", lambda e: e.dma_start(out=sidb[:], in_=idb_d[:]), inc=s_c, dma=True)
        sc.op("sync", lambda e: e.dma_start(out=sidf[:], in_=idf_d[:]), inc=s_c, dma=True)
        sc.op("sync", lambda e: e.dma_start(out=scm[:], in_=cm_d[:]), inc=s_c, dma=True)
        sc.op("sync", lambda e: e.dma_start(out=sgb[:], in_=gb_d.broadcast_to([128, E0])), inc=s_c, dma=True)
        t_gw = sc.op("gpsimd", lambda e: e.dma_start(out=sgw.rearrange("p (c e) -> p c e", c=8),
                                                     in_=gw_d.rearrange("(c p) e -> p c e", p=128)),
                     inc=s_gw, dma=True)
        t_c = (s_c, s_c.c)

        with ExitStack() as ca:
            en2 = ca.enter_context
            sx = [en2(nc.sbuf_tensor(f"{px}sx{j}", [128, D], F32)).ap() for j in range(2)]
            sx0xn = [en2(nc.sbuf_tensor(f"{px}sx0xn{j}", [128, D], F32)).ap() for j in range(2)]
            sxh = [en2(nc.sbuf_tensor(f"{px}sxh{j}", [128, D], BF16)).ap() for j in range(2)]
            stg = [en2(nc.sbuf_tensor(f"{px}stg{j}", [128, D], BF16)).ap() for j in range(2)]
            swmt = [en2(nc.sbuf_tensor(f"{px}swmt{j}", [128, E0], F32)).ap() for j in range(2)]
            sxn = en2(nc.sbuf_tensor(px + "sxn", [128, D], F32)).ap()
            sqscr = en2(nc.sbuf_tensor(px + "sqscr", [128, D], F32)).ap()
            swmf = en2(nc.sbuf_tensor(px + "swmf", [128, T], F32)).ap()
            seps = en2(nc.sbuf_tensor(px + "seps", [128, 1], F32)).ap()
            ssum = en2(nc.sbuf_tensor(px + "ssum", [128, 1], F32)).ap()
            ssumsq = en2(nc.sbuf_tensor(px + "ssumsq", [128, 1], F32)).ap()
            smean = en2(nc.sbuf_tensor(px + "smean", [128, 1], F32)).ap()
            sm2t = en2(nc.sbuf_tensor(px + "sm2t", [128, 1], F32)).ap()
            svar = en2(nc.sbuf_tensor(px + "svar", [128, 1], F32)).ap()
            srstd = en2(nc.sbuf_tensor(px + "srstd", [128, 1], F32)).ap()
            slog = en2(nc.sbuf_tensor(px + "slog", [128, E0], F32)).ap()
            snegmx = en2(nc.sbuf_tensor(px + "snegmx", [128, 1], F32)).ap()
            sexp = en2(nc.sbuf_tensor(px + "sexp", [128, E0], F32)).ap()
            sesum = en2(nc.sbuf_tensor(px + "sesum", [128, 1], F32)).ap()
            srcp = en2(nc.sbuf_tensor(px + "srcp", [128, 1], F32)).ap()
            sscor = en2(nc.sbuf_tensor(px + "sscor", [128, E0], F32)).ap()
            sm1 = en2(nc.sbuf_tensor(px + "sm1", [128, 1], F32)).ap()
            seq1 = en2(nc.sbuf_tensor(px + "seq1", [128, E0], F32)).ap()
            smk = en2(nc.sbuf_tensor(px + "smk", [128, E0], F32)).ap()
            sm2 = en2(nc.sbuf_tensor(px + "sm2", [128, 1], F32)).ap()
            seq2 = en2(nc.sbuf_tensor(px + "seq2", [128, E0], F32)).ap()
            sasg = en2(nc.sbuf_tensor(px + "sasg", [128, E0], F32)).ap()
            st1 = en2(nc.sbuf_tensor(px + "st1", [128, E0], F32)).ap()
            st2 = en2(nc.sbuf_tensor(px + "st2", [128, E0], F32)).ap()
            ptp = [en2(nc.psum_tensor(f"{px}ptp{j}", [128, 128], BF16)).ap() for j in range(2)]
            plog = en2(nc.psum_tensor(px + "plog", [128, E0], F32)).ap()
            pwm = en2(nc.psum_tensor(px + "pwm", [E0, 128], F32)).ap()

            s_x = [sc.sem(f"{px}x{j}") for j in range(2)]
            s_sx0 = [sc.sem(f"{px}sx0_{j}") for j in range(2)]
            s_sq = [sc.sem(f"{px}sq{j}") for j in range(2)]
            s_swm = [sc.sem(f"{px}swm{j}") for j in range(2)]
            s_swf = sc.sem(px + "swf")

            t_eps = sc.op("vector", lambda e: e.memset(seps[:], 1e-5))
            free_sx = [None, None]
            free_x0 = [None, None]
            free_xh = [None, None]
            free_stg = [None, None]
            free_wmt = [None, None]
            free_ptp = [None, None]
            free_plog = None
            free_pwm = None
            last_wmf_cp = None
            prev_var_read = None
            prev_xn = None
            for i in range(TT):
                b = i % 2
                r0 = i * 128
                tok_x = sc.op("sync", lambda e, b=b, r0=r0: e.dma_start(out=sx[b][:], in_=x_d[r0:r0 + 128, :]),
                              waits=[free_sx[b]], inc=s_x[b], dma=True)
                sc.op("vector", lambda e, b=b: e.reduce_sum(ssum[:], sx[b][:], axis=AX.X), waits=[tok_x])
                tok_sq = sc.op("scalar", lambda e, b=b: e.activation(out=sqscr[:], in_=sx[b][:], func=AF.Square,
                                                                    accum_out=ssumsq[:]),
                               waits=[tok_x, prev_var_read])
                tok_mean = sc.op("vector", lambda e: e.tensor_scalar_mul(smean[:], ssum[:], 1.0 / D))
                sc.op("vector", lambda e: e.tensor_mul(sm2t[:], smean[:], smean[:]))
                prev_var_read = sc.op("vector", lambda e: e.tensor_scalar_mul(svar[:], ssumsq[:], 1.0 / D),
                                      waits=[tok_sq])
                tok_var = sc.op("vector", lambda e: e.tensor_sub(svar[:], svar[:], sm2t[:]))
                sc.op("scalar", lambda e: e.activation(out=srstd[:], in_=svar[:], func=AF.Ln, bias=seps[:, 0:1]),
                      waits=[tok_var, t_eps, prev_xn])
                tok_rstd = sc.op("scalar", lambda e: e.activation(out=srstd[:], in_=srstd[:], func=AF.Exp, scale=-0.5))
                tok_xn = sc.op("vector", lambda e, b=b: e.tensor_scalar(out=sxn[:], in0=sx[b][:], scalar1=smean[:, 0:1],
                                                                       scalar2=srstd[:, 0:1], op0=ALU.subtract,
                                                                       op1=ALU.mult),
                               waits=[tok_mean, tok_rstd])
                prev_xn = tok_xn
                tok_x0xn = sc.op("vector", lambda e, b=b: e.tensor_add(sx0xn[b][:], sx[b][:], sxn[:]),
                                 waits=[free_x0[b]])
                free_sx[b] = tok_x0xn
                free_x0[b] = sc.op("sync", lambda e, b=b, r0=r0: e.dma_start(out=x0xn_d[r0:r0 + 128, :],
                                                                            in_=sx0xn[b][:]),
                                   waits=[tok_x0xn], inc=s_sx0[b], dma=True)
                tok_xh = sc.op("vector", lambda e, b=b: e.tensor_copy(sxh[b][:], sxn[:]), waits=[free_xh[b]])
                cp_toks = []
                wstg = free_stg[b]
                tok_tr = None
                for c in range(8):
                    tok_tr = sc.op("tensor", lambda e, b=b, c=c: e.transpose(ptp[c % 2][:],
                                                                             sxh[b][:, c * 128:(c + 1) * 128],
                                                                             sidb[:]),
                                   waits=[tok_xh, t_c, free_ptp[c % 2]], inc=s_t)
                    tok_cp = sc.op("vector", lambda e, b=b, c=c: e.tensor_copy(stg[b][:, c * 128:(c + 1) * 128],
                                                                               ptp[c % 2][:]),
                                   waits=[tok_tr, wstg])
                    free_ptp[c % 2] = tok_cp
                    cp_toks.append(tok_cp)
                free_xh[b] = tok_tr
                tok_stg = cp_toks[-1]
                tok_xnq = sc.op("sync", lambda e, b=b, i=i: e.dma_start(
                    out=xnq_d[i].rearrange("c p t -> p c t"),
                    in_=stg[b].rearrange("p (c t) -> p c t", c=8)),
                    waits=[tok_stg], inc=s_sq[b], dma=True)
                mm_tok = None
                for c in range(8):
                    mm_tok = sc.op("tensor", lambda e, b=b, c=c: e.matmul(plog[:], stg[b][:, c * 128:(c + 1) * 128],
                                                                          sgw[:, c * E0:(c + 1) * E0],
                                                                          start=(c == 0), stop=(c == 7)),
                                   waits=[tok_stg, t_gw, free_plog] if c == 0 else [],
                                   inc=s_t if c == 7 else None)
                free_stg[b] = [tok_xnq, mm_tok]
                tok_logcp = sc.op("vector", lambda e: e.tensor_copy(slog[:], plog[:]), waits=[mm_tok])
                free_plog = tok_logcp
                tok_ngx = sc.op("vector", lambda e: e.reduce_max(snegmx[:], slog[:], axis=AX.X, negate=True))
                tok_exp = sc.op("scalar", lambda e: e.activation(out=sexp[:], in_=slog[:], func=AF.Exp,
                                                                bias=snegmx[:, 0:1], accum_out=sesum[:]),
                                waits=[tok_ngx, tok_logcp])
                tok_rcp = sc.op("vector", lambda e: e.reciprocal(srcp[:], sesum[:]), waits=[tok_exp])
                sc.op("vector", lambda e: e.scalar_tensor_tensor(out=sscor[:], in0=sexp[:], scalar=srcp[:, 0:1],
                                                                 in1=sgb[:], op0=ALU.mult, op1=ALU.add),
                      waits=[tok_rcp, t_c])
                tok_m1 = sc.op("vector", lambda e: e.reduce_max(sm1[:], sscor[:], axis=AX.X))
                sc.op("vector", lambda e: e.tensor_scalar(out=seq1[:], in0=sscor[:], scalar1=sm1[:, 0:1],
                                                          scalar2=None, op0=ALU.is_equal),
                      waits=[tok_m1])
                sc.op("vector", lambda e: e.scalar_tensor_tensor(out=smk[:], in0=seq1[:], scalar=NEG,
                                                                 in1=sscor[:], op0=ALU.mult, op1=ALU.add))
                tok_m2 = sc.op("vector", lambda e: e.reduce_max(sm2[:], smk[:], axis=AX.X))
                sc.op("vector", lambda e: e.tensor_scalar(out=seq2[:], in0=smk[:], scalar1=sm2[:, 0:1],
                                                          scalar2=None, op0=ALU.is_equal),
                      waits=[tok_m2])
                sc.op("vector", lambda e: e.tensor_add(sasg[:], seq1[:], seq2[:]))
                sc.op("vector", lambda e: e.tensor_mul(st1[:], slog[:], sasg[:]))
                sc.op("vector", lambda e: e.tensor_scalar(out=st2[:], in0=sasg[:], scalar1=1.0, scalar2=-NEG,
                                                          op0=ALU.subtract, op1=ALU.mult))
                tok_wmt = sc.op("vector", lambda e, b=b: e.tensor_add(swmt[b][:], st1[:], st2[:]),
                                waits=[free_wmt[b]])
                tok_wmt_st = sc.op("sync", lambda e, b=b, r0=r0: e.dma_start(out=wmt_d[r0:r0 + 128, :],
                                                                            in_=swmt[b][:]),
                                   waits=[tok_wmt], inc=s_swm[b], dma=True)
                tok_wmT = sc.op("tensor", lambda e, b=b: e.transpose(pwm[:], swmt[b][:], sidf[:]),
                                waits=[tok_wmt, t_c, free_pwm], inc=s_t)
                free_wmt[b] = [tok_wmt_st, tok_wmT]
                tok_wmfcp = sc.op("vector", lambda e, i=i: e.tensor_copy(swmf[0:E0, i * 128:(i + 1) * 128], pwm[:]),
                                  waits=[tok_wmT])
                free_pwm = tok_wmfcp
                last_wmf_cp = tok_wmfcp
            sc.op("sync", lambda e: e.dma_start(out=wmf_d[:], in_=swmf[0:E0, :]), waits=[last_wmf_cp],
                  inc=s_swf, dma=True)
            sc.barrier()
            sc.emit_phase("A_ln0")

        with ExitStack() as cb:
            en2 = cb.enter_context
            sWT = en2(nc.sbuf_tensor(px + "sWT", [128, 8 * 2 * D], BF16)).ap()
            sxnb = [en2(nc.sbuf_tensor(f"{px}sxnb{j}", [128, 8 * 512], BF16)).ap() for j in range(2)]
            pkv = [en2(nc.psum_tensor(f"{px}pkv{j}", [128, 512], F32)).ap() for j in range(2)]

            s_nb = [sc.sem(f"{px}nb{j}") for j in range(2)]
            t_wt = sc.op("gpsimd", lambda e: e.dma_start(out=sWT.rearrange("p (c k) -> p c k", c=8),
                                                         in_=wt_d.rearrange("(c p) k -> p c k", p=128)),
                         inc=s_wt, dma=True)
            free_xnb = [None, None]
            free_pkv = [None, None]
            pi = 0
            for tc in range(8):
                b = tc % 2
                tok_xn = None
                for q4 in range(4):
                    tok_xn = sc.op("sync", lambda e, b=b, tc=tc, q4=q4: e.dma_start(
                        out=sxnb[b].rearrange("p (c q t) -> p c q t", c=8, q=4)[:, :, q4, :],
                        in_=xnq_d[tc * 4 + q4].rearrange("c p t -> p c t")),
                        waits=[free_xnb[b]] if q4 == 0 else [], inc=s_nb[b], dma=True)
                last_mm = None
                for kk in range(8):
                    pb = pi % 2
                    pi += 1
                    mm = None
                    for c in range(8):
                        mm = sc.op("tensor", lambda e, b=b, kk=kk, c=c, pb=pb: e.matmul(
                            pkv[pb][:], sWT[:, c * 2048 + kk * 128: c * 2048 + kk * 128 + 128],
                            sxnb[b][:, c * 512:(c + 1) * 512], start=(c == 0), stop=(c == 7)),
                            waits=[tok_xn, t_wt, free_pkv[pb]] if c == 0 else [],
                            inc=s_t if c == 7 else None)
                    tok_ev = sc.op("vector", lambda e, kk=kk, tc=tc, pb=pb: e.tensor_copy(
                        K_fm[:, kk * T + tc * 512: kk * T + (tc + 1) * 512], pkv[pb][:]), waits=[mm])
                    free_pkv[pb] = tok_ev
                    last_mm = mm
                for q4 in range(4):
                    qt = tc * 4 + q4
                    for dvh in range(2):
                        pb = pi % 2
                        pi += 1
                        mm = None
                        for c in range(8):
                            mm = sc.op("tensor", lambda e, b=b, q4=q4, dvh=dvh, c=c, pb=pb: e.matmul(
                                pkv[pb][:], sxnb[b][:, c * 512 + q4 * 128: c * 512 + q4 * 128 + 128],
                                sWT[:, c * 2048 + D + dvh * 512: c * 2048 + D + dvh * 512 + 512],
                                start=(c == 0), stop=(c == 7)),
                                waits=[free_pkv[pb]] if c == 0 else [],
                                inc=s_t if c == 7 else None)
                        tok_ev = sc.op("vector", lambda e, qt=qt, dvh=dvh, pb=pb: e.tensor_copy(
                            V_tm[:, qt * 1024 + dvh * 512: qt * 1024 + (dvh + 1) * 512], pkv[pb][:]),
                            waits=[mm])
                        free_pkv[pb] = tok_ev
                        last_mm = mm
                free_xnb[b] = last_mm
            sc.barrier()
            sc.emit_phase("B_kv")

        with ExitStack() as cc:
            en2 = cc.enter_context
            sxq = [en2(nc.sbuf_tensor(f"{px}sxq{j}", [128, D], BF16)).ap() for j in range(2)]
            sx0 = [en2(nc.sbuf_tensor(f"{px}sx0{j}", [128, D], F32)).ap() for j in range(2)]
            sout = [en2(nc.sbuf_tensor(f"{px}sout{j}", [128, 512], F32)).ap() for j in range(2)]
            sprT = [en2(nc.sbuf_tensor(f"{px}sprT{j}", [128, 128], BF16)).ap() for j in range(2)]
            ssc = en2(nc.sbuf_tensor(px + "ssc", [128, T], F32)).ap()
            sprob = en2(nc.sbuf_tensor(px + "sprob", [128, T], BF16)).ap()
            sneg = en2(nc.sbuf_tensor(px + "sneg", [128, 1], F32)).ap()
            ssume = en2(nc.sbuf_tensor(px + "ssume", [128, 1], F32)).ap()
            srcpo = en2(nc.sbuf_tensor(px + "srcpo", [128, 1], F32)).ap()
            psc = [en2(nc.psum_tensor(f"{px}psc{j}", [128, 512], F32)).ap() for j in range(2)]
            ptr = [en2(nc.psum_tensor(f"{px}ptr{j}", [128, 128], BF16)).ap() for j in range(2)]
            pav = en2(nc.psum_tensor(px + "pav", [128, 1024], F32)).ap()

            s_q = [sc.sem(f"{px}q{j}") for j in range(2)]
            s_x0c = [sc.sem(f"{px}x0c{j}") for j in range(2)]
            s_so = [sc.sem(f"{px}so{j}") for j in range(2)]

            free_xq = [None, None]
            free_x0b = [None, None]
            free_out = [None, None]
            free_psc = [None, None]
            free_ptr = [None, None]
            free_prT = [None, None]
            free_pav = None
            prev_prob = None
            prev_last_tr = None
            for qi in range(TT):
                b = qi % 2
                r0 = qi * 128
                nk = qi + 1
                L = nk * 128
                nch = (L + 511) // 512
                tok_q = sc.op("sync", lambda e, b=b, qi=qi: e.dma_start(
                    out=sxq[b].rearrange("p (c t) -> p c t", c=8),
                    in_=xnq_d[qi].rearrange("c p t -> p c t")),
                    waits=[free_xq[b]], inc=s_q[b], dma=True)
                tok_x0 = sc.op("sync", lambda e, b=b, r0=r0: e.dma_start(out=sx0[b][:], in_=x0xn_d[r0:r0 + 128, :]),
                               waits=[free_x0b[b]], inc=s_x0c[b], dma=True)
                ev_toks = []
                last_score_mm = None
                for ch in range(nch):
                    w = min(512, L - ch * 512)
                    pb = ch % 2
                    mm = None
                    for c in range(8):
                        mm = sc.op("tensor", lambda e, b=b, ch=ch, c=c, pb=pb, w=w: e.matmul(
                            psc[pb][:, 0:w], sxq[b][:, c * 128:(c + 1) * 128],
                            K_fm[:, c * T + ch * 512: c * T + ch * 512 + w],
                            start=(c == 0), stop=(c == 7)),
                            waits=[tok_q, free_psc[pb]] if c == 0 else [],
                            inc=s_t if c == 7 else None)
                    last_score_mm = mm
                    if ch == nch - 1:
                        off = r0 - ch * 512
                        if off > 0:
                            sc.op("vector", lambda e, ch=ch, pb=pb, off=off: e.tensor_copy(
                                ssc[:, ch * 512: ch * 512 + off], psc[pb][:, 0:off]), waits=[mm])
                        tok_ev = sc.op("vector", lambda e, ch=ch, pb=pb, off=off: e.tensor_add(
                            ssc[:, ch * 512 + off: ch * 512 + off + 128], psc[pb][:, off:off + 128], scm[:]),
                            waits=[mm, t_c])
                    else:
                        tok_ev = sc.op("vector", lambda e, ch=ch, pb=pb, w=w: e.tensor_copy(
                            ssc[:, ch * 512: ch * 512 + w], psc[pb][:, 0:w]), waits=[mm])
                    free_psc[pb] = tok_ev
                    ev_toks.append(tok_ev)
                tok_ngs = sc.op("vector", lambda e, L=L: e.reduce_max(sneg[:], ssc[:, 0:L], axis=AX.X, negate=True),
                                waits=[ev_toks[-1]])
                tok_prob = sc.op("scalar", lambda e, L=L: e.activation(out=sprob[:, 0:L], in_=ssc[:, 0:L],
                                                                      func=AF.Exp, bias=sneg[:, 0:1],
                                                                      accum_out=ssume[:]),
                                 waits=[tok_ngs, prev_prob, prev_last_tr])
                prev_prob = tok_prob
                tok_rcpo = sc.op("vector", lambda e: e.reciprocal(srcpo[:], ssume[:]), waits=[tok_prob])
                av_last = None
                tok_tr = None
                for k in range(nk):
                    tb = k % 2
                    tok_tr = sc.op("tensor", lambda e, k=k, tb=tb: e.transpose(ptr[tb][:],
                                                                               sprob[:, k * 128:(k + 1) * 128],
                                                                               sidb[:]),
                                   waits=([tok_prob] if k == 0 else []) + [free_ptr[tb]], inc=s_t)
                    tok_cp = sc.op("vector", lambda e, tb=tb: e.tensor_copy(sprT[tb][:], ptr[tb][:]),
                                   waits=[tok_tr, free_prT[tb]])
                    free_ptr[tb] = tok_cp
                    mm2 = None
                    for half in range(2):
                        mm2 = sc.op("tensor", lambda e, k=k, tb=tb, half=half, nk=nk: e.matmul(
                            pav[:, half * 512:(half + 1) * 512], sprT[tb][:],
                            V_tm[:, k * 1024 + half * 512: k * 1024 + half * 512 + 512],
                            start=(k == 0), stop=(k == nk - 1)),
                            waits=[tok_cp, free_pav] if (k == 0 and half == 0) else [tok_cp],
                            inc=s_t if (k == nk - 1 and half == 1) else None)
                    free_prT[tb] = mm2 if mm2 is not None else free_prT[tb]
                    if k == nk - 1:
                        av_last = mm2
                prev_last_tr = tok_tr
                out_toks = []
                for half in range(2):
                    ob = half
                    tok_o = sc.op("vector", lambda e, b=b, half=half, ob=ob: e.scalar_tensor_tensor(
                        out=sout[ob][:], in0=pav[:, half * 512:(half + 1) * 512], scalar=srcpo[:, 0:1],
                        in1=sx0[b][:, half * 512:(half + 1) * 512], op0=ALU.mult, op1=ALU.add),
                        waits=[av_last, tok_rcpo, tok_x0, free_out[ob]])
                    tok_os = sc.op("sync", lambda e, r0=r0, half=half, ob=ob: e.dma_start(
                        out=x1p_d[r0:r0 + 128, half * 512:(half + 1) * 512], in_=sout[ob][:]),
                        waits=[tok_o], inc=s_so[ob], dma=True)
                    free_out[ob] = tok_os
                    out_toks.append(tok_o)
                free_pav = out_toks
                free_xq[b] = last_score_mm
                free_x0b[b] = out_toks[-1]
            sc.barrier()
            sc.emit_phase("C_attn")


def build_k1(debug=False):
    nc = bass.Bass("TRN2", target_bir_lowering=False)
    x_d = nc.dram_tensor("x", [T, D], F32, kind="ExternalInput").ap()
    wt_d = nc.dram_tensor("wt", [D, 2 * D], F32, kind="ExternalInput").ap()
    gw_d = nc.dram_tensor("gw", [D, E0], F32, kind="ExternalInput").ap()
    gb_d = nc.dram_tensor("gb", [1, E0], F32, kind="ExternalInput").ap()
    idb_d = nc.dram_tensor("idb", [128, 128], BF16, kind="ExternalInput").ap()
    idf_d = nc.dram_tensor("idf", [128, 128], F32, kind="ExternalInput").ap()
    cm_d = nc.dram_tensor("cm", [128, 128], F32, kind="ExternalInput").ap()
    x1p_d = nc.dram_tensor("x1p", [T, D], F32, kind="ExternalOutput").ap()
    xnq_d = nc.dram_tensor("xnq", [TT, 8, 128, 128], BF16, kind="ExternalOutput").ap()
    wmf_d = nc.dram_tensor("wmf", [E0, T], F32, kind="ExternalOutput").ap()
    wmt_d = nc.dram_tensor("wmt", [T, E0], F32, kind="ExternalOutput").ap()
    x0xn_d = nc.dram_tensor("x0xn", [T, D], F32,
                            kind="ExternalOutput" if debug else "Internal").ap()
    sc = Sched(nc)
    emit_k1(nc, sc, "a_", x_d, wt_d, gw_d, gb_d, idb_d, idf_d, cm_d,
            x1p_d, xnq_d, wmf_d, wmt_d, x0xn_d)
    return nc



def emit_k2(nc, sc, px, xnq_d, wmt_d, wgf_d, x1p_d, ew1_d, ew3_d, ew2_d,
            sw1_d, sw3_d, sw2_d, ssel_d, stsel_d, rsel_d, ones_d, idb_d, idf_d,
            x1_d, wfin_d, ew1b_d, ew3b_d, ew2b_d, sw1b_d, sw3b_d, sw2b_d,
            experts=None, dump_acc=False, use_scratch=True, acc_dbg=None,
            wgf_wait=None):
    with ExitStack() as ctx:
        en = ctx.enter_context
        sidb = en(nc.sbuf_tensor(px + "sidb", [128, 128], BF16)).ap()
        sidf = en(nc.sbuf_tensor(px + "sidf", [128, 128], F32)).ap()

        s_t = sc.sem(px + "t")
        s_c = sc.sem(px + "c")
        s_cast = sc.sem(px + "cast")
        s_st = sc.sem(px + "st")

        sc.op("sync", lambda e: e.dma_start(out=sidb[:], in_=idb_d[:]), inc=s_c, dma=True)
        sc.op("sync", lambda e: e.dma_start(out=sidf[:], in_=idf_d[:]), inc=s_c, dma=True)

        if use_scratch:
            for e_i in range(E0):
                for src, dst in ((ew1_d, ew1b_d), (ew3_d, ew3b_d), (ew2_d, ew2b_d)):
                    sc.op("gpsimd", lambda e, s=src, d_=dst, e_i=e_i: e.dma_start(out=d_[e_i], in_=s[e_i]),
                          inc=s_cast, dma=True)
            for s_i in range(2):
                for src, dst in ((sw1_d, sw1b_d), (sw3_d, sw3b_d), (sw2_d, sw2b_d)):
                    sc.op("gpsimd", lambda e, s=src, d_=dst, s_i=s_i: e.dma_start(out=d_[s_i], in_=s[s_i]),
                          inc=s_cast, dma=True)
        tok_cast_all = (s_cast, s_cast.c) if use_scratch else None

        with ExitStack() as cd:
            en2 = cd.enter_context
            swg = en2(nc.sbuf_tensor(px + "swg", [128, T], F32)).ap()
            sgnscr = en2(nc.sbuf_tensor(px + "sgnscr", [128, T], BF16)).ap()
            sS = en2(nc.sbuf_tensor(px + "sS", [128, E0], F32)).ap()
            sST = en2(nc.sbuf_tensor(px + "sST", [E0, 128], F32)).ap()
            sR = en2(nc.sbuf_tensor(px + "sR", [128, E0], F32)).ap()
            sone = en2(nc.sbuf_tensor(px + "sone", [1, 128], F32)).ap()
            sb29 = en2(nc.sbuf_tensor(px + "sb29", [128, 1], F32)).ap()
            ssgn = en2(nc.sbuf_tensor(px + "ssgn", [128, 1], F32)).ap()
            scnt = en2(nc.sbuf_tensor(px + "scnt", [128, 1], F32)).ap()
            scntE = en2(nc.sbuf_tensor(px + "scntE", [E0, 1], F32)).ap()
            stgt16 = en2(nc.sbuf_tensor(px + "stgt16", [E0, 1], F32)).ap()
            slo = en2(nc.sbuf_tensor(px + "slo", [128, 1], F32)).ap()
            shi = en2(nc.sbuf_tensor(px + "shi", [128, 1], F32)).ap()
            stmp = en2(nc.sbuf_tensor(px + "stmp", [128, 1], F32)).ap()
            smid = en2(nc.sbuf_tensor(px + "smid", [128, 1], F32)).ap()
            snmid = en2(nc.sbuf_tensor(px + "snmid", [128, 1], F32)).ap()
            spredE = en2(nc.sbuf_tensor(px + "spredE", [E0, 1], F32)).ap()
            spred = en2(nc.sbuf_tensor(px + "spred", [128, 1], F32)).ap()
            sd1 = en2(nc.sbuf_tensor(px + "sd1", [128, 1], F32)).ap()
            sd2 = en2(nc.sbuf_tensor(px + "sd2", [128, 1], F32)).ap()
            strow = en2(nc.sbuf_tensor(px + "strow", [1, E0], F32)).ap()
            sthr = en2(nc.sbuf_tensor(px + "sthr", [128, E0], F32)).ap()
            swmtt = [en2(nc.sbuf_tensor(f"{px}swmtt{j}", [128, E0], F32)).ap() for j in range(2)]
            sselm = en2(nc.sbuf_tensor(px + "sselm", [128, E0], F32)).ap()
            swfin = [en2(nc.sbuf_tensor(f"{px}swfin{j}", [128, E0], F32)).ap() for j in range(2)]
            swf = en2(nc.sbuf_tensor(px + "swf", [128, T], F32)).ap()
            pcnt = en2(nc.psum_tensor(px + "pcnt", [E0, 1], F32)).ap()
            ppred = en2(nc.psum_tensor(px + "ppred", [128, 1], F32)).ap()
            ptrow = en2(nc.psum_tensor(px + "ptrow", [1, E0], F32)).ap()
            ptbc = en2(nc.psum_tensor(px + "ptbc", [128, E0], F32)).ap()
            pwt = [en2(nc.psum_tensor(f"{px}pwt{j}", [E0, 128], F32)).ap() for j in range(2)]

            s_wm = [sc.sem(f"{px}wm{j}") for j in range(2)]
            s_wf = sc.sem(px + "wfst")

            sc.op("sync", lambda e: e.dma_start(out=swg[:], in_=wgf_d[:]), waits=[wgf_wait], inc=s_c, dma=True)
            sc.op("sync", lambda e: e.dma_start(out=sS[:], in_=ssel_d[:]), inc=s_c, dma=True)
            sc.op("sync", lambda e: e.dma_start(out=sST[:], in_=stsel_d[:]), inc=s_c, dma=True)
            sc.op("sync", lambda e: e.dma_start(out=sR[:], in_=rsel_d[:]), inc=s_c, dma=True)
            sc.op("sync", lambda e: e.dma_start(out=sone[:], in_=ones_d[:]), inc=s_c, dma=True)
            t_c = (s_c, s_c.c)

            sc.op("vector", lambda e: e.memset(sb29[:], 1e29))
            tok_mx0 = None
            for hfi in range(2):
                sc.op("vector", lambda e, hfi=hfi: e.tensor_scalar(
                    out=sgnscr.bitcast(F32)[:, 0:T // 2], in0=swg[:, hfi * (T // 2):(hfi + 1) * (T // 2)],
                    scalar1=-1e29, scalar2=None, op0=ALU.is_gt), waits=[t_c])
                sc.op("vector", lambda e, hfi=hfi: e.tensor_mul(
                    sgnscr.bitcast(F32)[:, 0:T // 2], sgnscr.bitcast(F32)[:, 0:T // 2],
                    swg[:, hfi * (T // 2):(hfi + 1) * (T // 2)]))
                tok_mx0 = sc.op("vector", lambda e, hfi=hfi: e.reduce_max(
                    smid[:] if hfi == 0 else snmid[:], sgnscr.bitcast(F32)[:, 0:T // 2], axis=AX.X,
                    apply_absolute_value=True))
            tok_mx0 = sc.op("vector", lambda e: e.tensor_max(shi[:], smid[:], snmid[:]))
            mmx = sc.op("tensor", lambda e: e.matmul(pcnt[:], sS[:], shi[:], start=True, stop=True),
                        waits=[t_c, tok_mx0], inc=s_t)
            tok_hx = sc.op("vector", lambda e: e.tensor_copy(scntE[:], pcnt[:]), waits=[mmx])
            sc.op("vector", lambda e: e.tensor_scalar_add(scntE[:], scntE[:], 1.0))
            mmx2 = sc.op("tensor", lambda e: e.matmul(ppred[:], sST[:], scntE[:], start=True, stop=True),
                         inc=s_t)
            tok_hb = sc.op("vector", lambda e: e.tensor_copy(shi[:], ppred[:]), waits=[mmx2])
            tok_ms = sc.op("vector", lambda e: e.tensor_scalar_mul(slo[:], shi[:], -1.0))
            tok_sgn0 = sc.op("scalar", lambda e: e.activation(out=sgnscr[:], in_=swg[:], func=AF.Sign,
                                                             bias=sb29[:, 0:1], accum_out=ssgn[:]),
                             waits=[t_c, tok_ms])
            tok_cnt0 = sc.op("vector", lambda e: e.tensor_scalar(out=scnt[:], in0=ssgn[:], scalar1=float(T),
                                                                scalar2=0.5, op0=ALU.add, op1=ALU.mult),
                             waits=[tok_sgn0])
            mmc = sc.op("tensor", lambda e: e.matmul(pcnt[:], sS[:], scnt[:], start=True, stop=True),
                        waits=[t_c, tok_cnt0, tok_hx], inc=s_t)
            tok_cE = sc.op("vector", lambda e: e.tensor_copy(scntE[:], pcnt[:]), waits=[mmc])
            tok_tgt = sc.op("vector", lambda e: e.tensor_scalar_min(stgt16[:], scntE[:], float(CAP)))
            free_pcnt = tok_cE
            tok_upd = tok_tgt
            free_ppred = None
            for it in range(N_BIS):
                sc.op("vector", lambda e: e.tensor_add(stmp[:], slo[:], shi[:]), waits=[tok_upd])
                sc.op("vector", lambda e: e.tensor_scalar_mul(smid[:], stmp[:], 0.5))
                tok_nm = sc.op("vector", lambda e: e.tensor_scalar_mul(snmid[:], stmp[:], -0.5))
                tok_sgn = sc.op("scalar", lambda e: e.activation(out=sgnscr[:], in_=swg[:], func=AF.Sign,
                                                                bias=snmid[:, 0:1], accum_out=ssgn[:]),
                                waits=[tok_nm])
                tok_cnt = sc.op("vector", lambda e: e.tensor_scalar(out=scnt[:], in0=ssgn[:], scalar1=float(T),
                                                                   scalar2=0.5, op0=ALU.add, op1=ALU.mult),
                               waits=[tok_sgn])
                mmc = sc.op("tensor", lambda e: e.matmul(pcnt[:], sS[:], scnt[:], start=True, stop=True),
                            waits=[tok_cnt, free_pcnt], inc=s_t)
                tok_cE = sc.op("vector", lambda e: e.tensor_copy(scntE[:], pcnt[:]), waits=[mmc])
                free_pcnt = tok_cE
                tok_pE = sc.op("vector", lambda e: e.tensor_tensor(spredE[:], scntE[:], stgt16[:], ALU.is_ge))
                mmp = sc.op("tensor", lambda e: e.matmul(ppred[:], sST[:], spredE[:], start=True, stop=True),
                            waits=[tok_pE, free_ppred if free_ppred is not None else tok_hb], inc=s_t)
                tok_pr = sc.op("vector", lambda e: e.tensor_copy(spred[:], ppred[:]), waits=[mmp])
                free_ppred = tok_pr
                sc.op("vector", lambda e: e.tensor_sub(sd1[:], smid[:], slo[:]))
                sc.op("vector", lambda e: e.tensor_mul(sd1[:], spred[:], sd1[:]))
                sc.op("vector", lambda e: e.tensor_add(slo[:], slo[:], sd1[:]))
                sc.op("vector", lambda e: e.tensor_sub(sd2[:], shi[:], smid[:]))
                sc.op("vector", lambda e: e.tensor_mul(sd2[:], spred[:], sd2[:]))
                tok_upd = sc.op("vector", lambda e: e.tensor_add(shi[:], smid[:], sd2[:]))
            mmr = sc.op("tensor", lambda e: e.matmul(ptrow[:], slo[:], sR[:], start=True, stop=True),
                        waits=[tok_upd], inc=s_t)
            tok_tr = sc.op("vector", lambda e: e.tensor_copy(strow[:], ptrow[:]), waits=[mmr])
            mmb = sc.op("tensor", lambda e: e.matmul(ptbc[:], sone[:], strow[:], start=True, stop=True),
                        waits=[tok_tr], inc=s_t)
            tok_thr = sc.op("vector", lambda e: e.tensor_copy(sthr[:], ptbc[:]), waits=[mmb])
            free_wmtt = [None, None]
            free_wfin = [None, None]
            free_pwt = [None, None]
            last_wf = None
            for i in range(TT):
                b = i % 2
                tok_w = sc.op("sync", lambda e, b=b, i=i: e.dma_start(out=swmtt[b][:],
                                                                     in_=wmt_d[i * 128:(i + 1) * 128, :]),
                              waits=[free_wmtt[b]], inc=s_wm[b], dma=True)
                sc.op("vector", lambda e, b=b: e.tensor_tensor(sselm[:], swmtt[b][:], sthr[:], ALU.is_ge),
                      waits=[tok_w, tok_thr])
                tok_wf = sc.op("vector", lambda e, b=b: e.tensor_mul(swfin[b][:], sselm[:], swmtt[b][:]),
                               waits=[free_wfin[b]])
                free_wmtt[b] = tok_wf
                tok_T = sc.op("tensor", lambda e, b=b: e.transpose(pwt[b][:], swfin[b][:], sidf[:]),
                              waits=[tok_wf, t_c, free_pwt[b]], inc=s_t)
                free_wfin[b] = tok_T
                tok_cp = sc.op("vector", lambda e, i=i, b=b: e.tensor_copy(swf[0:E0, i * 128:(i + 1) * 128],
                                                                           pwt[b][:]),
                               waits=[tok_T])
                free_pwt[b] = tok_cp
                last_wf = tok_cp
            sc.op("sync", lambda e: e.dma_start(out=wfin_d[:], in_=swf[0:E0, :]), waits=[last_wf],
                  inc=s_wf, dma=True)
            sc.barrier()
            sc.emit_phase("D_bisect")

        with ExitStack() as cf:
            en2 = cf.enter_context
            sacc = en2(nc.sbuf_tensor(px + "sacc", [128, 8 * T], BF16)).ap()
            cf2 = en2(ExitStack())
            en3 = cf2.enter_context
            sw1s = en3(nc.sbuf_tensor(px + "sw1s", [128, 8 * H0], BF16)).ap()
            sw3s = en3(nc.sbuf_tensor(px + "sw3s", [128, 8 * H0], BF16)).ap()
            sw2s = en3(nc.sbuf_tensor(px + "sw2s", [128, 16 * D], BF16)).ap()
            sxnb = [en3(nc.sbuf_tensor(f"{px}sxnb{j}", [128, 8 * 512], BF16)).ap() for j in range(2)]
            sbc = [en3(nc.sbuf_tensor(f"{px}sbc{j}", [128, 512], F32)).ap() for j in range(2)]
            sbc05 = en3(nc.sbuf_tensor(px + "sbc05", [128, 512], F32)).ap()
            sG = en3(nc.sbuf_tensor(px + "sG", [128, 16 * 512], BF16)).ap()
            sg1 = en3(nc.sbuf_tensor(px + "sg1", [128, 512], BF16)).ap()
            sg3 = en3(nc.sbuf_tensor(px + "sg3", [128, 512], BF16)).ap()
            ph = [en3(nc.psum_tensor(f"{px}ph{j}", [128, 512], F32)).ap() for j in range(2)]
            po = [en3(nc.psum_tensor(f"{px}po{j}", [128, 512], F32)).ap() for j in range(2)]

            s_nb = [sc.sem(f"{px}nb{j}") for j in range(2)]
            s_bc = [sc.sem(f"{px}bc{j}") for j in range(2)]
            s_w1 = sc.sem(px + "w1")
            s_w3 = sc.sem(px + "w3")
            s_w2 = sc.sem(px + "w2")

            tok_acc0 = sc.op("vector", lambda e: e.memset(sacc[:], 0.0))
            sc.op("vector", lambda e: e.memset(sbc05[:], 0.5))
            free_xnb = [None, None]
            free_bc = [None, None]
            free_ph = [None, None]
            free_po = [None, None]
            free_w = None
            acc_tok = tok_acc0
            phi = 0
            expert_list = list(range(E0 + 2)) if experts is None else experts
            for ei in expert_list:
                routed = ei < E0
                if use_scratch:
                    w1src = ew1b_d[ei] if routed else sw1b_d[ei - E0]
                    w3src = ew3b_d[ei] if routed else sw3b_d[ei - E0]
                    w2src = ew2b_d[ei] if routed else sw2b_d[ei - E0]
                else:
                    w1src = ew1_d[ei] if routed else sw1_d[ei - E0]
                    w3src = ew3_d[ei] if routed else sw3_d[ei - E0]
                    w2src = ew2_d[ei] if routed else sw2_d[ei - E0]
                t_w1 = sc.op("gpsimd", lambda e, s=w1src: e.dma_start(
                    out=sw1s.rearrange("p (c h) -> p c h", c=8), in_=s.rearrange("(c p) h -> p c h", p=128)),
                    waits=[tok_cast_all, free_w], inc=s_w1, dma=True)
                t_w3 = sc.op("gpsimd", lambda e, s=w3src: e.dma_start(
                    out=sw3s.rearrange("p (c h) -> p c h", c=8), in_=s.rearrange("(c p) h -> p c h", p=128)),
                    inc=s_w3, dma=True)
                t_w2 = sc.op("gpsimd", lambda e, s=w2src: e.dma_start(
                    out=sw2s.rearrange("p (h o) -> p h o", h=16), in_=s.rearrange("(h p) o -> p h o", p=128)),
                    inc=s_w2, dma=True)
                last_mm_e = None
                for tc in range(8):
                    bx = tc % 2
                    tok_xn = None
                    for q4 in range(4):
                        tok_xn = sc.op("sync", lambda e, bx=bx, tc=tc, q4=q4: e.dma_start(
                            out=sxnb[bx].rearrange("p (c q t) -> p c q t", c=8, q=4)[:, :, q4, :],
                            in_=xnq_d[tc * 4 + q4].rearrange("c p t -> p c t")),
                            waits=[free_xnb[bx]] if q4 == 0 else [], inc=s_nb[bx], dma=True)
                    if routed:
                        tok_bc = sc.op("sync", lambda e, bx=bx, ei=ei, tc=tc: e.dma_start(
                            out=sbc[bx][:],
                            in_=wfin_d[ei:ei + 1, tc * 512:(tc + 1) * 512].broadcast_to([128, 512])),
                            waits=[free_bc[bx]], inc=s_bc[bx], dma=True)
                        bc_ap = sbc[bx]
                    else:
                        tok_bc = None
                        bc_ap = sbc05
                    g_toks = []
                    prev_G = None
                    for h in range(16):
                        pb = phi % 2
                        phi += 1
                        mm1 = None
                        for c in range(8):
                            mm1 = sc.op("tensor", lambda e, h=h, c=c, pb=pb, bx=bx: e.matmul(
                                ph[pb][:], sw1s[:, c * H0 + h * 128: c * H0 + h * 128 + 128],
                                sxnb[bx][:, c * 512:(c + 1) * 512], start=(c == 0), stop=(c == 7)),
                                waits=[t_w1, tok_xn, free_ph[pb]] if c == 0 else [],
                                inc=s_t if c == 7 else None)
                        tok_g1 = sc.op("scalar", lambda e, pb=pb: e.activation(out=sg1[:], in_=ph[pb][:],
                                                                              func=AF.Silu),
                                       waits=[mm1, prev_G])
                        free_ph[pb] = tok_g1
                        pb3 = phi % 2
                        phi += 1
                        mm3 = None
                        for c in range(8):
                            mm3 = sc.op("tensor", lambda e, h=h, c=c, pb3=pb3, bx=bx: e.matmul(
                                ph[pb3][:], sw3s[:, c * H0 + h * 128: c * H0 + h * 128 + 128],
                                sxnb[bx][:, c * 512:(c + 1) * 512], start=(c == 0), stop=(c == 7)),
                                waits=[t_w3, free_ph[pb3]] if c == 0 else [],
                                inc=s_t if c == 7 else None)
                        tok_g3 = sc.op("vector", lambda e, pb3=pb3, bc_ap=bc_ap: e.tensor_tensor(
                            sg3[:], ph[pb3][:], bc_ap[:], ALU.mult),
                            waits=[mm3, tok_bc])
                        free_ph[pb3] = tok_g3
                        tok_G = sc.op("vector", lambda e, h=h: e.tensor_mul(sG[:, h * 512:(h + 1) * 512],
                                                                            sg1[:], sg3[:]),
                                      waits=[tok_g1])
                        prev_G = tok_G
                        g_toks.append(tok_G)
                    for oc in range(8):
                        pb = phi % 2
                        phi += 1
                        mm2 = None
                        for h in range(16):
                            mm2 = sc.op("tensor", lambda e, oc=oc, h=h, pb=pb: e.matmul(
                                po[pb][:], sw2s[:, h * D + oc * 128: h * D + oc * 128 + 128],
                                sG[:, h * 512:(h + 1) * 512], start=(h == 0), stop=(h == 15)),
                                waits=[t_w2, g_toks[-1], free_po[pb]] if h == 0 else [],
                                inc=s_t if h == 15 else None)
                        acc_tok = sc.op("vector", lambda e, oc=oc, tc=tc, pb=pb: e.tensor_tensor(
                            sacc[:, oc * T + tc * 512: oc * T + (tc + 1) * 512],
                            sacc[:, oc * T + tc * 512: oc * T + (tc + 1) * 512], po[pb][:], ALU.add),
                            waits=[mm2, acc_tok])
                        free_po[pb] = acc_tok
                        last_mm_e = mm2
                    free_xnb[bx] = last_mm_e
                    free_bc[bx] = acc_tok
                free_w = last_mm_e
            if dump_acc:
                sc.op("sync", lambda e: e.dma_start(out=acc_dbg[:], in_=sacc[:]), waits=[acc_tok],
                      inc=s_st, dma=True)
            sc.barrier()
            sc.emit_phase("F_experts")
            cf2.close()

            with ExitStack() as cg:
                en4 = cg.enter_context
                sx1p = [en4(nc.sbuf_tensor(f"{px}sx1p{j}", [128, D], F32)).ap() for j in range(2)]
                sx1o = [en4(nc.sbuf_tensor(f"{px}sx1o{j}", [128, D], F32)).ap() for j in range(2)]
                pacc = [en4(nc.psum_tensor(f"{px}pacc{j}", [128, 128], BF16)).ap() for j in range(2)]
                s_xp = [sc.sem(f"{px}xp{j}") for j in range(2)]
                s_xo = [sc.sem(f"{px}xo{j}") for j in range(2)]
                free_p = [None, None]
                free_o = [None, None]
                free_i = [None, None]
                for i in range(TT):
                    b = i % 2
                    tok_x1p = sc.op("sync", lambda e, b=b, i=i: e.dma_start(out=sx1p[b][:],
                                                                           in_=x1p_d[i * 128:(i + 1) * 128, :]),
                                    waits=[free_i[b]], inc=s_xp[b], dma=True)
                    add_toks = []
                    for c in range(8):
                        pb = c % 2
                        tok_T = sc.op("tensor", lambda e, i=i, c=c, pb=pb: e.transpose(
                            pacc[pb][:], sacc[:, c * T + i * 128: c * T + i * 128 + 128], sidb[:]),
                            waits=[(s_c, 32), free_p[pb]] + ([acc_tok] if i == 0 and c == 0 else []), inc=s_t)
                        tok_add = sc.op("vector", lambda e, b=b, c=c, pb=pb: e.tensor_tensor(
                            sx1o[b][:, c * 128:(c + 1) * 128], sx1p[b][:, c * 128:(c + 1) * 128],
                            pacc[pb][:], ALU.add),
                            waits=[tok_T, tok_x1p] + ([free_o[b]] if c == 0 else []))
                        free_p[pb] = tok_add
                        add_toks.append(tok_add)
                    tok_st2 = sc.op("sync", lambda e, b=b, i=i: e.dma_start(out=x1_d[i * 128:(i + 1) * 128, :],
                                                                           in_=sx1o[b][:]),
                                    waits=[add_toks[-1]], inc=s_xo[b], dma=True)
                    free_o[b] = tok_st2
                    free_i[b] = add_toks[-1]
                sc.barrier()
                sc.emit_phase("G_final")


def build_k2(experts=None, dump_acc=False, use_scratch=True):
    nc = bass.Bass("TRN2", target_bir_lowering=False)
    xnq_d = nc.dram_tensor("xnq", [TT, 8, 128, 128], BF16, kind="ExternalInput").ap()
    wmt_d = nc.dram_tensor("wmtd", [T, E0], F32, kind="ExternalInput").ap()
    wgf_d = nc.dram_tensor("wgf", [128, T], F32, kind="ExternalInput").ap()
    x1p_d = nc.dram_tensor("x1p", [T, D], F32, kind="ExternalInput").ap()
    ew1_d = nc.dram_tensor("ew1", [E0, D, H0], F32, kind="ExternalInput").ap()
    ew3_d = nc.dram_tensor("ew3", [E0, D, H0], F32, kind="ExternalInput").ap()
    ew2_d = nc.dram_tensor("ew2", [E0, H0, D], F32, kind="ExternalInput").ap()
    sw1_d = nc.dram_tensor("sw1", [2, D, H0], F32, kind="ExternalInput").ap()
    sw3_d = nc.dram_tensor("sw3", [2, D, H0], F32, kind="ExternalInput").ap()
    sw2_d = nc.dram_tensor("sw2", [2, H0, D], F32, kind="ExternalInput").ap()
    ssel_d = nc.dram_tensor("Ssel", [128, E0], F32, kind="ExternalInput").ap()
    stsel_d = nc.dram_tensor("STsel", [E0, 128], F32, kind="ExternalInput").ap()
    rsel_d = nc.dram_tensor("Rsel", [128, E0], F32, kind="ExternalInput").ap()
    ones_d = nc.dram_tensor("ones_f", [1, 128], F32, kind="ExternalInput").ap()
    idb_d = nc.dram_tensor("idb", [128, 128], BF16, kind="ExternalInput").ap()
    idf_d = nc.dram_tensor("idf", [128, 128], F32, kind="ExternalInput").ap()
    x1_d = nc.dram_tensor("x1", [T, D], F32, kind="ExternalOutput").ap()
    wfin_d = nc.dram_tensor("wfin", [E0, T], F32, kind="ExternalOutput").ap()
    acc_dbg = nc.dram_tensor("accdbg", [128, 8 * T], BF16, kind="ExternalOutput").ap() if dump_acc else None
    ew1b_d = nc.dram_tensor("ew1b", [E0, D, H0], BF16, kind="Internal").ap()
    ew3b_d = nc.dram_tensor("ew3b", [E0, D, H0], BF16, kind="Internal").ap()
    ew2b_d = nc.dram_tensor("ew2b", [E0, H0, D], BF16, kind="Internal").ap()
    sw1b_d = nc.dram_tensor("sw1b", [2, D, H0], BF16, kind="Internal").ap()
    sw3b_d = nc.dram_tensor("sw3b", [2, D, H0], BF16, kind="Internal").ap()
    sw2b_d = nc.dram_tensor("sw2b", [2, H0, D], BF16, kind="Internal").ap()
    sc = Sched(nc)
    emit_k2(nc, sc, "b_", xnq_d, wmt_d, wgf_d, x1p_d, ew1_d, ew3_d, ew2_d,
            sw1_d, sw3_d, sw2_d, ssel_d, stsel_d, rsel_d, ones_d, idb_d, idf_d,
            x1_d, wfin_d, ew1b_d, ew3b_d, ew2b_d, sw1b_d, sw3b_d, sw2b_d,
            experts=experts, dump_acc=dump_acc, use_scratch=use_scratch,
            acc_dbg=acc_dbg)
    return nc


def k2_inputs(inputs, res1):
    idb, idf, cm = _host_consts()
    S = np.zeros((128, E0), np.float32)
    for r in range(NCORE):
        for e_i in range(E0):
            S[r * E0 + e_i, e_i] = 1.0
    ST = np.ascontiguousarray(S.T)
    ones_f = np.ones((1, 128), np.float32)
    wgf = np.concatenate([np.asarray(res1.results[b]["wmf"]) for b in range(NCORE)], axis=0)
    maps = []
    for b in range(NCORE):
        R = np.zeros((128, E0), np.float32)
        for e_i in range(E0):
            R[b * E0 + e_i, e_i] = 1.0
        maps.append({
            "xnq": np.asarray(res1.results[b]["xnq"]),
            "wmtd": np.asarray(res1.results[b]["wmt"]),
            "wgf": wgf,
            "x1p": np.asarray(res1.results[b]["x1p"]),
            "ew1": inputs["L0_ffn_ew1"], "ew3": inputs["L0_ffn_ew3"], "ew2": inputs["L0_ffn_ew2"],
            "sw1": inputs["L0_ffn_sw1"], "sw3": inputs["L0_ffn_sw3"], "sw2": inputs["L0_ffn_sw2"],
            "Ssel": S, "STsel": ST, "Rsel": R, "ones_f": ones_f, "idb": idb, "idf": idf,
        })
    return maps



JTOT = 1367
JPAD = 1408
NH = 16
DH = 64
SCALE = float(D) ** -0.5


def _lq(qi):
    win0 = (qi * 128) // 3 + 2
    return min(JTOT, win0 + 64), win0


def emit_k3(nc, sc, px, x1_d, wr_d, wqT_d, wcp_d, bcv_d, nlk_d, nlv_d,
            woT_d, bo_d, cml_d, idb_d, idf_d, x2p_d, xnq1_d, rsc_d, qq_d, attn_d):
    with ExitStack() as ctx:
        en = ctx.enter_context
        sidb = en(nc.sbuf_tensor(px + "sidb", [128, 128], BF16)).ap()
        sidf = en(nc.sbuf_tensor(px + "sidf", [128, 128], F32)).ap()
        kc_fm = en(nc.sbuf_tensor(px + "kc_fm", [128, 8 * JPAD], BF16)).ap()
        vc_tm = en(nc.sbuf_tensor(px + "vc_tm", [128, 11 * 1024], BF16)).ap()
        swcp = en(nc.sbuf_tensor(px + "swcp", [128, 3 * 8 * DH], BF16)).ap()
        sbcv = en(nc.sbuf_tensor(px + "sbcv", [128, 8], F32)).ap()
        snlk = en(nc.sbuf_tensor(px + "snlk", [128, 8], BF16)).ap()
        snlv = en(nc.sbuf_tensor(px + "snlv", [128, 8], BF16)).ap()

        s_t = sc.sem(px + "t")
        s_c = sc.sem(px + "c")

        sc.op("sync", lambda e: e.dma_start(out=sidb[:], in_=idb_d[:]), inc=s_c, dma=True)
        sc.op("sync", lambda e: e.dma_start(out=sidf[:], in_=idf_d[:]), inc=s_c, dma=True)
        sc.op("sync", lambda e: e.dma_start(out=sbcv[:], in_=bcv_d[:]), inc=s_c, dma=True)
        sc.op("gpsimd", lambda e: e.dma_start(out=swcp[:], in_=wcp_d[:]), inc=s_c, dma=True)
        sc.op("gpsimd", lambda e: e.dma_start(out=snlk[:], in_=nlk_d[:]), inc=s_c, dma=True)
        sc.op("gpsimd", lambda e: e.dma_start(out=snlv[:], in_=nlv_d[:]), inc=s_c, dma=True)
        t_c = (s_c, s_c.c)

        with ExitStack() as ch:
            en2 = ch.enter_context
            sx = [en2(nc.sbuf_tensor(f"{px}sx{j}", [128, D], F32)).ap() for j in range(2)]
            sxh = [en2(nc.sbuf_tensor(f"{px}sxh{j}", [128, D], BF16)).ap() for j in range(2)]
            stg = [en2(nc.sbuf_tensor(f"{px}stg{j}", [128, D], BF16)).ap() for j in range(2)]
            srt = [en2(nc.sbuf_tensor(f"{px}srt{j}", [128, E1], F32)).ap() for j in range(2)]
            swr = en2(nc.sbuf_tensor(px + "swr", [128, 8 * E1], BF16)).ap()
            srf = en2(nc.sbuf_tensor(px + "srf", [128, T], F32)).ap()
            sxn = en2(nc.sbuf_tensor(px + "sxn", [128, D], F32)).ap()
            sqscr = en2(nc.sbuf_tensor(px + "sqscr", [128, D], F32)).ap()
            seps = en2(nc.sbuf_tensor(px + "seps", [128, 1], F32)).ap()
            ssum = en2(nc.sbuf_tensor(px + "ssum", [128, 1], F32)).ap()
            ssumsq = en2(nc.sbuf_tensor(px + "ssumsq", [128, 1], F32)).ap()
            smean = en2(nc.sbuf_tensor(px + "smean", [128, 1], F32)).ap()
            sm2t = en2(nc.sbuf_tensor(px + "sm2t", [128, 1], F32)).ap()
            svar = en2(nc.sbuf_tensor(px + "svar", [128, 1], F32)).ap()
            srstd = en2(nc.sbuf_tensor(px + "srstd", [128, 1], F32)).ap()
            slog = en2(nc.sbuf_tensor(px + "slog", [128, E1], F32)).ap()
            smr = en2(nc.sbuf_tensor(px + "smr", [128, 1], F32)).ap()
            sexp = en2(nc.sbuf_tensor(px + "sexp", [128, E1], F32)).ap()
            sesum = en2(nc.sbuf_tensor(px + "sesum", [128, 1], F32)).ap()
            srcp = en2(nc.sbuf_tensor(px + "srcp", [128, 1], F32)).ap()
            seqr = en2(nc.sbuf_tensor(px + "seqr", [128, E1], F32)).ap()
            ptp = [en2(nc.psum_tensor(f"{px}ptp{j}", [128, 128], BF16)).ap() for j in range(2)]
            plog = en2(nc.psum_tensor(px + "plog", [128, E1], F32)).ap()
            pwm = en2(nc.psum_tensor(px + "pwm", [E1, 128], F32)).ap()

            s_x = [sc.sem(f"{px}x{j}") for j in range(2)]
            s_sq = [sc.sem(f"{px}sq{j}") for j in range(2)]
            s_rf = sc.sem(px + "rf")
            s_wrl = sc.sem(px + "wrl")

            t_wr = sc.op("gpsimd", lambda e: e.dma_start(out=swr.rearrange("p (c e) -> p c e", c=8),
                                                         in_=wr_d.rearrange("(c p) e -> p c e", p=128)),
                         inc=s_wrl, dma=True)
            t_eps = sc.op("vector", lambda e: e.memset(seps[:], 1e-5))
            free_sx = [None, None]
            free_xh = [None, None]
            free_stg = [None, None]
            free_rt = [None, None]
            free_ptp = [None, None]
            free_plog = None
            free_pwm = None
            last_rf = None
            prev_var_read = None
            prev_xn = None
            for i in range(TT):
                b = i % 2
                r0 = i * 128
                tok_x = sc.op("sync", lambda e, b=b, r0=r0: e.dma_start(out=sx[b][:], in_=x1_d[r0:r0 + 128, :]),
                              waits=[free_sx[b]], inc=s_x[b], dma=True)
                sc.op("vector", lambda e, b=b: e.reduce_sum(ssum[:], sx[b][:], axis=AX.X), waits=[tok_x])
                tok_sq = sc.op("scalar", lambda e, b=b: e.activation(out=sqscr[:], in_=sx[b][:], func=AF.Square,
                                                                    accum_out=ssumsq[:]),
                               waits=[tok_x, prev_var_read])
                tok_mean = sc.op("vector", lambda e: e.tensor_scalar_mul(smean[:], ssum[:], 1.0 / D))
                sc.op("vector", lambda e: e.tensor_mul(sm2t[:], smean[:], smean[:]))
                prev_var_read = sc.op("vector", lambda e: e.tensor_scalar_mul(svar[:], ssumsq[:], 1.0 / D),
                                      waits=[tok_sq])
                tok_var = sc.op("vector", lambda e: e.tensor_sub(svar[:], svar[:], sm2t[:]))
                sc.op("scalar", lambda e: e.activation(out=srstd[:], in_=svar[:], func=AF.Ln, bias=seps[:, 0:1]),
                      waits=[tok_var, t_eps, prev_xn])
                tok_rstd = sc.op("scalar", lambda e: e.activation(out=srstd[:], in_=srstd[:], func=AF.Exp, scale=-0.5))
                tok_xn = sc.op("vector", lambda e, b=b: e.tensor_scalar(out=sxn[:], in0=sx[b][:],
                                                                       scalar1=smean[:, 0:1],
                                                                       scalar2=srstd[:, 0:1], op0=ALU.subtract,
                                                                       op1=ALU.mult),
                               waits=[tok_mean, tok_rstd])
                prev_xn = tok_xn
                free_sx[b] = tok_xn
                tok_xh = sc.op("vector", lambda e, b=b: e.tensor_copy(sxh[b][:], sxn[:]), waits=[free_xh[b]])
                cp_toks = []
                wstg = free_stg[b]
                tok_tr = None
                for c in range(8):
                    tok_tr = sc.op("tensor", lambda e, b=b, c=c: e.transpose(ptp[c % 2][:],
                                                                             sxh[b][:, c * 128:(c + 1) * 128],
                                                                             sidb[:]),
                                   waits=[tok_xh, t_c, free_ptp[c % 2]], inc=s_t)
                    tok_cp = sc.op("vector", lambda e, b=b, c=c: e.tensor_copy(stg[b][:, c * 128:(c + 1) * 128],
                                                                               ptp[c % 2][:]),
                                   waits=[tok_tr, wstg])
                    free_ptp[c % 2] = tok_cp
                    cp_toks.append(tok_cp)
                free_xh[b] = tok_tr
                tok_stg = cp_toks[-1]
                tok_xnq = sc.op("sync", lambda e, b=b, i=i: e.dma_start(
                    out=xnq1_d[i].rearrange("c p t -> p c t"),
                    in_=stg[b].rearrange("p (c t) -> p c t", c=8)),
                    waits=[tok_stg], inc=s_sq[b], dma=True)
                mm_tok = None
                for c in range(8):
                    mm_tok = sc.op("tensor", lambda e, b=b, c=c: e.matmul(plog[:], stg[b][:, c * 128:(c + 1) * 128],
                                                                          swr[:, c * E1:(c + 1) * E1],
                                                                          start=(c == 0), stop=(c == 7)),
                                   waits=[tok_stg, t_wr, free_plog] if c == 0 else [],
                                   inc=s_t if c == 7 else None)
                free_stg[b] = [tok_xnq, mm_tok]
                tok_logcp = sc.op("vector", lambda e: e.tensor_copy(slog[:], plog[:]), waits=[mm_tok])
                free_plog = tok_logcp
                tok_mr = sc.op("vector", lambda e: e.reduce_max(smr[:], slog[:], axis=AX.X, negate=True))
                tok_exp = sc.op("scalar", lambda e: e.activation(out=sexp[:], in_=slog[:], func=AF.Exp,
                                                                bias=smr[:, 0:1], accum_out=sesum[:]),
                                waits=[tok_mr, tok_logcp])
                tok_rcp = sc.op("vector", lambda e: e.reciprocal(srcp[:], sesum[:]), waits=[tok_exp])
                sc.op("vector", lambda e: e.tensor_scalar(out=seqr[:], in0=sexp[:], scalar1=1.0, scalar2=None,
                                                          op0=ALU.is_ge))
                tok_rt = sc.op("vector", lambda e, b=b: e.tensor_scalar(out=srt[b][:], in0=seqr[:],
                                                                       scalar1=srcp[:, 0:1], scalar2=None,
                                                                       op0=ALU.mult),
                               waits=[tok_rcp, free_rt[b]])
                tok_rT = sc.op("tensor", lambda e, b=b: e.transpose(pwm[:], srt[b][:], sidf[:]),
                               waits=[tok_rt, t_c, free_pwm], inc=s_t)
                free_rt[b] = tok_rT
                tok_rcpc = sc.op("vector", lambda e, i=i: e.tensor_copy(srf[0:E1, i * 128:(i + 1) * 128], pwm[:]),
                                 waits=[tok_rT])
                free_pwm = tok_rcpc
                last_rf = tok_rcpc
            sc.op("sync", lambda e: e.dma_start(out=rsc_d[:], in_=srf[0:E1, :]), waits=[last_rf],
                  inc=s_rf, dma=True)
            sc.barrier()
            sc.emit_phase("H_ln1")

        for pno, part in enumerate(("q", "k", "v")):
            with ExitStack() as ci:
                en2 = ci.enter_context
                swq = en2(nc.sbuf_tensor(f"{px}swq_{part}", [128, 8 * D], BF16)).ap()
                sxnb = [en2(nc.sbuf_tensor(f"{px}sxnb_{part}{j}", [128, 8 * 512], BF16)).ap() for j in range(2)]
                sev = [en2(nc.sbuf_tensor(f"{px}sev_{part}{j}", [128, 512], BF16)).ap() for j in range(2)]
                pkv = [en2(nc.psum_tensor(f"{px}pkv_{part}{j}", [128, 512], F32)).ap() for j in range(2)]
                if part != "q":
                    kvf = en2(nc.sbuf_tensor(f"{px}kvf_{part}", [128, 8 * 4104], BF16)).ap()

                s_nb = [sc.sem(f"{px}nb_{part}{j}") for j in range(2)]
                s_ev = [sc.sem(f"{px}ev_{part}{j}") for j in range(2)]
                s_wq = sc.sem(f"{px}wq_{part}")
                t_wq = sc.op("gpsimd", lambda e, pno=pno: e.dma_start(
                    out=swq.rearrange("p (c k) -> p c k", c=8),
                    in_=wqT_d[:, pno * D:(pno + 1) * D].rearrange("(c p) k -> p c k", p=128)),
                    inc=s_wq, dma=True)
                if part != "q":
                    tok_pad = sc.op("vector", lambda e: e.memset(kvf[:], 0.0))
                free_xnb = [None, None]
                free_ev = [None, None]
                free_pkv = [None, None]
                pi = 0
                last_proj_v = None
                for tc in range(8):
                    b = tc % 2
                    tok_xn = None
                    for q4 in range(4):
                        tok_xn = sc.op("sync", lambda e, b=b, tc=tc, q4=q4: e.dma_start(
                            out=sxnb[b].rearrange("p (c q t) -> p c q t", c=8, q=4)[:, :, q4, :],
                            in_=xnq1_d[tc * 4 + q4].rearrange("c p t -> p c t")),
                            waits=[free_xnb[b]] if q4 == 0 else [], inc=s_nb[b], dma=True)
                    last_mm = None
                    for kk in range(8):
                        pb = pi % 2
                        pi += 1
                        mm = None
                        for c in range(8):
                            mm = sc.op("tensor", lambda e, b=b, kk=kk, c=c, pb=pb: e.matmul(
                                pkv[pb][:], swq[:, c * D + kk * 128: c * D + kk * 128 + 128],
                                sxnb[b][:, c * 512:(c + 1) * 512], start=(c == 0), stop=(c == 7)),
                                waits=[tok_xn, t_wq, free_pkv[pb]] if c == 0 else [],
                                inc=s_t if c == 7 else None)
                        last_mm = mm
                        if part == "q":
                            eb = pb
                            tok_e = sc.op("vector", lambda e, pb=pb, eb=eb: e.tensor_copy(sev[eb][:], pkv[pb][:]),
                                          waits=[mm, free_ev[eb]])
                            free_pkv[pb] = tok_e
                            tok_st = sc.op("sync", lambda e, tc=tc, kk=kk, eb=eb: e.dma_start(
                                out=qq_d[tc * 4:(tc + 1) * 4, kk].rearrange("q p t -> p q t"),
                                in_=sev[eb].rearrange("p (q t) -> p q t", q=4)),
                                waits=[tok_e], inc=s_ev[eb], dma=True)
                            free_ev[eb] = tok_st
                        else:
                            tok_e = sc.op("vector", lambda e, kk=kk, tc=tc, pb=pb: e.tensor_copy(
                                kvf[:, kk * 4104 + 2 + tc * 512: kk * 4104 + 2 + (tc + 1) * 512], pkv[pb][:]),
                                waits=[mm] + ([tok_pad] if tc == 0 and kk == 0 else []))
                            free_pkv[pb] = tok_e
                            last_proj_v = tok_e
                    free_xnb[b] = last_mm
                if part != "q":
                    with ExitStack() as cj:
                        en3 = cj.enter_context
                        svt = en3(nc.sbuf_tensor(f"{px}svt_{part}", [128, 128], BF16)).ap()
                        pcm = [en3(nc.psum_tensor(f"{px}pcm_{part}{j}", [64, 128], F32)).ap() for j in range(2)]
                        ptv = [en3(nc.psum_tensor(f"{px}ptv_{part}{j}", [128, 64], BF16)).ap() for j in range(2)]
                        free_pcm = [None, None]
                        free_ptv = [None, None]
                        free_svt = None
                        if part == "k":
                            tok_nl = sc.op("vector", lambda e: e.tensor_copy(
                                kc_fm.rearrange("p (cc j) -> p cc j", cc=8)[:, :, 0:1].squeeze(2), snlk[:]),
                                waits=[t_c])
                        else:
                            tok_nl = None
                        cidx = 0
                        for h in range(NH):
                            base = (h % 2) * 64
                            hp = h // 2
                            for jc in range(11):
                                jlo = jc * 128 - 1
                                wfull = min(128, 1366 - jlo)
                                o0 = 1 if jc == 0 else 0
                                jj0 = max(jlo, 0)
                                wv = wfull - o0
                                pb = cidx % 2
                                mmc_ = None
                                for tap in range(3):
                                    col0 = 3 * jj0 + tap
                                    mmc_ = sc.op("tensor", lambda e, h=h, hp=hp, base=base, tap=tap, jj0=jj0,
                                                 wv=wv, o0=o0, pb=pb, col0=col0: e.matmul(
                                        pcm[pb][:, o0:o0 + wv],
                                        swcp[base:base + 64, tap * 512 + hp * 64: tap * 512 + hp * 64 + 64],
                                        kvf[base:base + 64, hp * 4104 + col0: hp * 4104 + col0 + 3 * wv: 3],
                                        start=(tap == 0), stop=(tap == 2)),
                                        waits=[last_proj_v, t_c, free_pcm[pb]] if tap == 0 else [],
                                        inc=s_t if tap == 2 else None)
                                if part == "k":
                                    tok_ev2 = sc.op("vector", lambda e, h=h, hp=hp, base=base, jc=jc, o0=o0,
                                                    wv=wv, pb=pb: e.tensor_scalar(
                                        out=kc_fm[base:base + 64,
                                                  hp * JPAD + jc * 128 + o0: hp * JPAD + jc * 128 + o0 + wv],
                                        in0=pcm[pb][:, o0:o0 + wv],
                                        scalar1=sbcv[base:base + 64, hp:hp + 1], scalar2=None, op0=ALU.add),
                                        waits=[mmc_])
                                    free_pcm[pb] = tok_ev2
                                else:
                                    tok_b = sc.op("vector", lambda e, base=base, hp=hp, o0=o0, wv=wv, pb=pb: e.tensor_scalar(
                                        out=svt[base:base + 64, o0:o0 + wv], in0=pcm[pb][:, o0:o0 + wv],
                                        scalar1=sbcv[base:base + 64, hp:hp + 1], scalar2=None, op0=ALU.add),
                                        waits=[mmc_, free_svt])
                                    free_pcm[pb] = tok_b
                                    if jc == 0:
                                        tok_b = sc.op("vector", lambda e, base=base, hp=hp: e.tensor_copy(
                                            svt[base:base + 64, 0:1], snlv[base:base + 64, hp:hp + 1]),
                                            waits=[tok_b, t_c])
                                    tb = cidx % 2
                                    wtot = o0 + wv if jc == 0 else wv
                                    tok_T = sc.op("tensor", lambda e, base=base, tb=tb, wtot=wtot: e.transpose(
                                        ptv[tb][0:wtot, :], svt[base:base + 64, 0:wtot],
                                        sidb[base:base + 64, base:base + 64]),
                                        waits=[tok_b, free_ptv[tb]], inc=s_t)
                                    free_svt = tok_T
                                    tok_vc = sc.op("vector", lambda e, h=h, jc=jc, tb=tb, wtot=wtot: e.tensor_copy(
                                        vc_tm[0:wtot, jc * 1024 + h * 64: jc * 1024 + h * 64 + 64]
                                        if False else
                                        vc_tm.rearrange("p (jcb d) -> p jcb d", jcb=11)[0:wtot, jc, h * 64:h * 64 + 64],
                                        ptv[tb][0:wtot, :]),
                                        waits=[tok_T])
                                    free_ptv[tb] = tok_vc
                                cidx += 1
                sc.barrier()
                sc.emit_phase(f"I_{part}")

        with ExitStack() as cj2:
            en2 = cj2.enter_context
            sq = [en2(nc.sbuf_tensor(f"{px}sq{j}", [128, D], BF16)).ap() for j in range(2)]
            ssc1 = en2(nc.sbuf_tensor(px + "ssc1", [128, JPAD], F32)).ap()
            sexp1 = en2(nc.sbuf_tensor(px + "sexp1", [128, JPAD], F32)).ap()
            sprob1 = en2(nc.sbuf_tensor(px + "sprob1", [128, JPAD], BF16)).ap()
            sprT = [en2(nc.sbuf_tensor(f"{px}sprT{j}", [128, 128], BF16)).ap() for j in range(2)]
            scml = [en2(nc.sbuf_tensor(f"{px}scml{j}", [128, 64], F32)).ap() for j in range(2)]
            sneg = en2(nc.sbuf_tensor(px + "sneg", [128, 1], F32)).ap()
            snegs = en2(nc.sbuf_tensor(px + "snegs", [128, 1], F32)).ap()
            ssume = en2(nc.sbuf_tensor(px + "ssume", [128, 1], F32)).ap()
            srcpo = en2(nc.sbuf_tensor(px + "srcpo", [128, 1], F32)).ap()
            psc = [en2(nc.psum_tensor(f"{px}psc{j}", [128, 512], F32)).ap() for j in range(2)]
            ptr = [en2(nc.psum_tensor(f"{px}ptr{j}", [128, 128], BF16)).ap() for j in range(2)]
            pout = [en2(nc.psum_tensor(f"{px}pout{j}", [64, 128], F32)).ap() for j in range(2)]

            satt = [en2(nc.sbuf_tensor(f"{px}satt{j}", [128, 8 * 128], BF16)).ap() for j in range(2)]
            s_q = [sc.sem(f"{px}qj{j}") for j in range(2)]
            s_cm = [sc.sem(f"{px}cm{j}") for j in range(2)]
            s_at = [sc.sem(f"{px}at{j}") for j in range(2)]

            free_q = [None, None]
            free_cm = [None, None]
            free_att = [None, None]
            free_psc = [None, None]
            free_ptr = [None, None]
            free_prT = [None, None]
            free_pout = [None, None]
            prev_prob = None
            prev_last_tr = None
            oidx = 0
            for qi in range(TT):
                bq = qi % 2
                Lq, win0 = _lq(qi)
                wm = Lq - win0
                nblk = (Lq + 127) // 128
                tok_q = sc.op("sync", lambda e, bq=bq, qi=qi: e.dma_start(
                    out=sq[bq].rearrange("p (c t) -> p c t", c=8),
                    in_=qq_d[qi].rearrange("c p t -> p c t")),
                    waits=[free_q[bq]], inc=s_q[bq], dma=True)
                tok_cmq = sc.op("sync", lambda e, bq=bq, qi=qi: e.dma_start(out=scml[bq][:], in_=cml_d[qi]),
                                waits=[free_cm[bq]], inc=s_cm[bq], dma=True)
                last_cm_read = None
                for h in range(NH):
                    base = (h % 2) * 64
                    hp = h // 2
                    nch = (Lq + 511) // 512
                    ev_toks = []
                    for chk in range(nch):
                        wch = min(512, Lq - chk * 512)
                        pb = chk % 2
                        mm = sc.op("tensor", lambda e, base=base, hp=hp, bq=bq, chk=chk, wch=wch, pb=pb: e.matmul(
                            psc[pb][:, 0:wch], sq[bq][base:base + 64, hp * 128:hp * 128 + 128]
                            if False else sq[bq].rearrange("p (c t) -> p c t", c=8)[base:base + 64, hp, :],
                            kc_fm[base:base + 64, hp * JPAD + chk * 512: hp * JPAD + chk * 512 + wch],
                            start=True, stop=True),
                            waits=[tok_q, free_psc[pb]], inc=s_t)
                        lo = chk * 512
                        hi = lo + wch
                        if lo < win0:
                            w0 = min(hi, win0) - lo
                            sc.op("vector", lambda e, lo=lo, w0=w0, pb=pb: e.tensor_copy(
                                ssc1[:, lo:lo + w0], psc[pb][:, 0:w0]), waits=[mm])
                        if hi > win0:
                            m0 = max(lo, win0)
                            wmm = hi - m0
                            tok_ev = sc.op("vector", lambda e, m0=m0, wmm=wmm, lo=lo, pb=pb, bq=bq, win0=win0: e.tensor_add(
                                ssc1[:, m0:m0 + wmm], psc[pb][:, m0 - lo:m0 - lo + wmm],
                                scml[bq][:, m0 - win0:m0 - win0 + wmm]),
                                waits=[mm, tok_cmq])
                            last_cm_read = tok_ev
                        else:
                            tok_ev = sc.op("vector", lambda e: e.engine_nop(), waits=[mm])
                        free_psc[pb] = tok_ev
                        ev_toks.append(tok_ev)
                    tok_ngs = sc.op("vector", lambda e, Lq=Lq: e.reduce_max(sneg[:], ssc1[:, 0:Lq], axis=AX.X,
                                                                           negate=True),
                                    waits=[ev_toks[-1]])
                    tok_ngss = sc.op("vector", lambda e: e.tensor_scalar_mul(snegs[:], sneg[:], SCALE))
                    tok_exp = sc.op("scalar", lambda e, Lq=Lq: e.activation(out=sexp1[:, 0:Lq], in_=ssc1[:, 0:Lq],
                                                                           func=AF.Exp, bias=snegs[:, 0:1],
                                                                           scale=SCALE, accum_out=ssume[:]),
                                    waits=[tok_ngss, prev_prob])
                    tok_rcp = sc.op("vector", lambda e: e.reciprocal(srcpo[:], ssume[:]), waits=[tok_exp])
                    tok_pr = sc.op("vector", lambda e, Lq=Lq: e.tensor_scalar(out=sprob1[:, 0:Lq],
                                                                             in0=sexp1[:, 0:Lq],
                                                                             scalar1=srcpo[:, 0:1], scalar2=None,
                                                                             op0=ALU.mult),
                                   waits=[tok_rcp, prev_last_tr])
                    prev_prob = tok_pr
                    out_last = None
                    tok_tr = None
                    for jc in range(nblk):
                        wj = min(128, Lq - jc * 128)
                        tb = jc % 2
                        ob = oidx % 2
                        tok_tr = sc.op("tensor", lambda e, jc=jc, tb=tb, wj=wj: e.transpose(
                            ptr[tb][0:wj, :], sprob1[:, jc * 128: jc * 128 + wj], sidb[:]),
                            waits=([tok_pr] if jc == 0 else []) + [free_ptr[tb]], inc=s_t)
                        tok_cp = sc.op("vector", lambda e, tb=tb, wj=wj: e.tensor_copy(sprT[tb][0:wj, :],
                                                                                      ptr[tb][0:wj, :]),
                                       waits=[tok_tr, free_prT[tb]])
                        free_ptr[tb] = tok_cp
                        mm2 = sc.op("tensor", lambda e, jc=jc, tb=tb, h=h, wj=wj, ob=ob, nblk=nblk: e.matmul(
                            pout[ob][:],
                            vc_tm.rearrange("p (jcb d) -> p jcb d", jcb=11)[0:wj, jc, h * 64:h * 64 + 64]
                            if False else
                            vc_tm[0:wj, jc * 1024 + h * 64: jc * 1024 + h * 64 + 64],
                            sprT[tb][0:wj, :], start=(jc == 0), stop=(jc == nblk - 1)),
                            waits=[tok_cp] + ([free_pout[ob]] if jc == 0 else []),
                            inc=s_t if jc == nblk - 1 else None)
                        free_prT[tb] = mm2
                        out_last = mm2
                    prev_last_tr = tok_tr
                    tok_oc = sc.op("vector", lambda e, base=base, hp=hp, bq=bq, ob=oidx % 2: e.tensor_copy(
                        satt[bq][base:base + 64, hp * 128:(hp + 1) * 128], pout[ob][:]),
                        waits=[out_last] + ([free_att[bq]] if h == 0 else []))
                    free_pout[oidx % 2] = tok_oc
                    last_oc = tok_oc
                    oidx += 1
                free_q[bq] = prev_prob
                free_cm[bq] = last_cm_read
                free_att[bq] = sc.op("sync", lambda e, bq=bq, qi=qi: e.dma_start(
                    out=attn_d[qi].rearrange("c p t -> p c t"),
                    in_=satt[bq].rearrange("p (c t) -> p c t", c=8)),
                    waits=[last_oc], inc=s_at[bq], dma=True)
            sc.barrier()
            sc.emit_phase("J_attn")

        with ExitStack() as ck:
            en2 = ck.enter_context
            swo = en2(nc.sbuf_tensor(px + "swo", [128, 8 * D], BF16)).ap()
            sbo = en2(nc.sbuf_tensor(px + "sbo", [128, D], F32)).ap()
            sx1r = [en2(nc.sbuf_tensor(f"{px}sx1r{j}", [128, D], F32)).ap() for j in range(2)]
            sat2 = [en2(nc.sbuf_tensor(f"{px}sat2{j}", [128, 8 * 128], BF16)).ap() for j in range(2)]
            so2 = [en2(nc.sbuf_tensor(f"{px}so2{j}", [128, 512], F32)).ap() for j in range(2)]
            pwo = [en2(nc.psum_tensor(f"{px}pwo{j}", [128, 512], F32)).ap() for j in range(2)]

            s_wo = sc.sem(px + "wo")
            s_bo = sc.sem(px + "bol")
            s_x1r = [sc.sem(f"{px}x1r{j}") for j in range(2)]
            s_o2 = [sc.sem(f"{px}o2{j}") for j in range(2)]
            s_at2 = [sc.sem(f"{px}at2{j}") for j in range(2)]

            t_wo = sc.op("gpsimd", lambda e: e.dma_start(out=swo.rearrange("p (c k) -> p c k", c=8),
                                                         in_=woT_d.rearrange("(c p) k -> p c k", p=128)),
                         inc=s_wo, dma=True)
            t_bo = sc.op("sync", lambda e: e.dma_start(out=sbo[:], in_=bo_d.broadcast_to([128, D])),
                         inc=s_bo, dma=True)
            free_x1r = [None, None]
            free_o2 = [None, None]
            free_pwo = [None, None]
            free_at2 = [None, None]
            for qi in range(TT):
                b = qi % 2
                r0 = qi * 128
                tok_x1 = sc.op("sync", lambda e, b=b, r0=r0: e.dma_start(out=sx1r[b][:], in_=x1_d[r0:r0 + 128, :]),
                               waits=[free_x1r[b]], inc=s_x1r[b], dma=True)
                tok_at = sc.op("sync", lambda e, b=b, qi=qi: e.dma_start(
                    out=sat2[b].rearrange("p (c t) -> p c t", c=8),
                    in_=attn_d[qi].rearrange("c p t -> p c t")),
                    waits=[free_at2[b]], inc=s_at2[b], dma=True)
                st_toks = []
                for half in range(2):
                    pb = half
                    mm = None
                    for c in range(8):
                        mm = sc.op("tensor", lambda e, b=b, half=half, c=c, pb=pb: e.matmul(
                            pwo[pb][:], sat2[b][:, c * 128:(c + 1) * 128],
                            swo[:, c * D + half * 512: c * D + half * 512 + 512],
                            start=(c == 0), stop=(c == 7)),
                            waits=[t_wo, tok_at, free_pwo[pb]] if c == 0 else [],
                            inc=s_t if c == 7 else None)
                    ob = half
                    tok_o = sc.op("vector", lambda e, b=b, half=half, ob=ob, pb=pb: e.tensor_add(
                        so2[ob][:], pwo[pb][:], sbo[:, half * 512:(half + 1) * 512]),
                        waits=[mm, t_bo, free_o2[ob]])
                    tok_o2 = sc.op("vector", lambda e, b=b, half=half, ob=ob: e.tensor_add(
                        so2[ob][:], so2[ob][:], sx1r[b][:, half * 512:(half + 1) * 512]),
                        waits=[tok_x1])
                    tok_st = sc.op("sync", lambda e, r0=r0, half=half, ob=ob: e.dma_start(
                        out=x2p_d[r0:r0 + 128, half * 512:(half + 1) * 512], in_=so2[ob][:]),
                        waits=[tok_o2], inc=s_o2[ob], dma=True)
                    free_o2[ob] = tok_st
                    free_pwo[pb] = tok_o
                    st_toks.append(tok_o2)
                free_x1r[b] = st_toks[-1]
                free_at2[b] = mm
            sc.barrier()
            sc.emit_phase("K_wout")


def build_k3():
    nc = bass.Bass("TRN2", target_bir_lowering=False)
    x1_d = nc.dram_tensor("x1", [T, D], F32, kind="ExternalInput").ap()
    wr_d = nc.dram_tensor("wr", [D, E1], F32, kind="ExternalInput").ap()
    wqT_d = nc.dram_tensor("wqT", [D, 3 * D], F32, kind="ExternalInput").ap()
    wcp_d = nc.dram_tensor("wcp", [128, 3 * 8 * DH], F32, kind="ExternalInput").ap()
    bcv_d = nc.dram_tensor("bcv", [128, 8], F32, kind="ExternalInput").ap()
    nlk_d = nc.dram_tensor("nlk", [128, 8], F32, kind="ExternalInput").ap()
    nlv_d = nc.dram_tensor("nlv", [128, 8], F32, kind="ExternalInput").ap()
    woT_d = nc.dram_tensor("woT", [D, D], F32, kind="ExternalInput").ap()
    bo_d = nc.dram_tensor("bo", [1, D], F32, kind="ExternalInput").ap()
    cml_d = nc.dram_tensor("cml", [TT, 128, 64], F32, kind="ExternalInput").ap()
    idb_d = nc.dram_tensor("idb", [128, 128], BF16, kind="ExternalInput").ap()
    idf_d = nc.dram_tensor("idf", [128, 128], F32, kind="ExternalInput").ap()
    x2p_d = nc.dram_tensor("x2p", [T, D], F32, kind="ExternalOutput").ap()
    xnq1_d = nc.dram_tensor("xnq1", [TT, 8, 128, 128], BF16, kind="ExternalOutput").ap()
    rsc_d = nc.dram_tensor("rsc", [E1, T], F32, kind="ExternalOutput").ap()
    qq_d = nc.dram_tensor("qq", [TT, 8, 128, 128], BF16, kind="Internal").ap()
    attn_d = nc.dram_tensor("attn", [TT, 8, 128, 128], BF16, kind="Internal").ap()
    sc = Sched(nc)
    emit_k3(nc, sc, "c_", x1_d, wr_d, wqT_d, wcp_d, bcv_d, nlk_d, nlv_d,
            woT_d, bo_d, cml_d, idb_d, idf_d, x2p_d, xnq1_d, rsc_d, qq_d, attn_d)
    return nc


def k3_inputs(inputs, res2):
    idb, idf, cm = _host_consts()
    wqT = np.ascontiguousarray(inputs["L1_attn_w_qkv"].T)
    wc = inputs["L1_attn_w_conv"]
    wcp = np.zeros((128, 3 * 8 * DH), np.float32)
    for h in range(NH):
        base = (h % 2) * 64
        hp = h // 2
        blk = wc[h * DH:(h + 1) * DH]
        for tap in range(3):
            wcp[base:base + 64, tap * 512 + hp * 64: tap * 512 + (hp + 1) * 64] = blk[:, :, tap].T
    bcv = np.ascontiguousarray(inputs["L1_attn_b_conv"].reshape(8, 128).T)
    nlk = np.ascontiguousarray(inputs["L1_attn_null_k"].reshape(8, 128).T)
    nlv = np.ascontiguousarray(inputs["L1_attn_null_v"].reshape(8, 128).T)
    woT = np.ascontiguousarray(inputs["L1_attn_w_out"].T)
    bo = np.ascontiguousarray(inputs["L1_attn_b_out"].reshape(1, D))
    wr = np.ascontiguousarray(inputs["L1_ffn_Wr"])
    cml = np.zeros((TT, 128, 64), np.float32)
    for qi in range(TT):
        Lq, win0 = _lq(qi)
        for r in range(128):
            i = qi * 128 + r
            fm = i // 3 + 2
            for c in range(64):
                j = win0 + c
                if j >= fm:
                    cml[qi, r, c] = NEG
    maps = []
    for b in range(NCORE):
        maps.append({
            "x1": np.asarray(res2.results[b]["x1"]),
            "wr": wr, "wqT": wqT, "wcp": wcp, "bcv": bcv, "nlk": nlk, "nlv": nlv,
            "woT": woT, "bo": bo, "cml": cml, "idb": idb, "idf": idf,
        })
    return maps



def emit_k4(nc, sc, px, xnq1_d, rsc_d, x2p_d, w1_d, b1_d, w2_d, b2_d,
            idb_d, out_d, w1b_d, w2b_d):
    with ExitStack() as ctx:
        en = ctx.enter_context
        sidb = en(nc.sbuf_tensor(px + "sidb", [128, 128], BF16)).ap()
        sacc = en(nc.sbuf_tensor(px + "sacc", [128, 8 * T], BF16)).ap()

        s_t = sc.sem(px + "t")
        s_c = sc.sem(px + "c")
        s_cast = sc.sem(px + "cast")

        sc.op("sync", lambda e: e.dma_start(out=sidb[:], in_=idb_d[:]), inc=s_c, dma=True)
        for e_i in range(E1):
            sc.op("gpsimd", lambda e, e_i=e_i: e.dma_start(out=w1b_d[e_i], in_=w1_d[e_i]), inc=s_cast, dma=True)
            sc.op("gpsimd", lambda e, e_i=e_i: e.dma_start(out=w2b_d[e_i], in_=w2_d[e_i]), inc=s_cast, dma=True)
        tok_cast = (s_cast, s_cast.c)

        with ExitStack() as cf:
            en2 = cf.enter_context
            sw2s = en2(nc.sbuf_tensor(px + "sw2s", [128, 32 * D], BF16)).ap()
            sw1f = [en2(nc.sbuf_tensor(f"{px}sw1f{j}", [128, 8 * 128], BF16)).ap() for j in range(3)]
            sxnb = [en2(nc.sbuf_tensor(f"{px}sxnb{j}", [128, 8 * 512], BF16)).ap() for j in range(2)]
            sbc = [en2(nc.sbuf_tensor(f"{px}sbc{j}", [128, 512], F32)).ap() for j in range(2)]
            sG = en2(nc.sbuf_tensor(px + "sG", [128, 32 * 512], BF16)).ap()
            sb1f = en2(nc.sbuf_tensor(px + "sb1f", [128, 32], F32)).ap()
            stmp5 = en2(nc.sbuf_tensor(px + "stmp5", [128, 512], F32)).ap()
            cb2 = en2(ExitStack())
            sb2r = cb2.enter_context(nc.sbuf_tensor(px + "sb2r", [E1, D], F32)).ap()
            srt = cb2.enter_context(nc.sbuf_tensor(px + "srt", [E1, 512], F32)).ap()
            ph = [en2(nc.psum_tensor(f"{px}ph{j}", [128, 512], F32)).ap() for j in range(2)]
            po = [en2(nc.psum_tensor(f"{px}po{j}", [128, 512], F32)).ap() for j in range(2)]

            s_nb = [sc.sem(f"{px}nb{j}") for j in range(2)]
            s_bc = [sc.sem(f"{px}bc{j}") for j in range(2)]
            s_w1f = [sc.sem(f"{px}w1f{j}") for j in range(3)]
            s_w2l = sc.sem(px + "w2l")
            s_b1l = sc.sem(px + "b1l")

            sc.op("sync", lambda e: e.dma_start(out=sb2r[:], in_=b2_d[:]), inc=s_c, dma=True)
            t_c = (s_c, s_c.c)
            s_rtl = sc.sem(px + "rtl")

            acc_tok = None
            free_pb2 = [None, None]
            free_rt = None
            for tc in range(8):
                tok_rt = sc.op("sync", lambda e, tc=tc: e.dma_start(out=srt[:], in_=rsc_d[:, tc * 512:(tc + 1) * 512]),
                               waits=[free_rt], inc=s_rtl, dma=True)
                last_mm_rt = None
                for oc in range(8):
                    pbi = oc % 2
                    mm = sc.op("tensor", lambda e, oc=oc, tc=tc, pbi=pbi: e.matmul(
                        po[pbi][:], sb2r[:, oc * 128:(oc + 1) * 128], srt[:],
                        start=True, stop=True),
                        waits=[t_c, tok_rt, free_pb2[pbi]], inc=s_t)
                    acc_tok = sc.op("vector", lambda e, oc=oc, tc=tc, pbi=pbi: e.tensor_copy(
                        sacc[:, oc * T + tc * 512: oc * T + (tc + 1) * 512], po[pbi][:]), waits=[mm])
                    free_pb2[pbi] = acc_tok
                    last_mm_rt = mm
                free_rt = last_mm_rt

            sc.barrier()
            sc.emit_phase("L_b2term")
            cb2.close()
            free_xnb = [None, None]
            free_bc = [None, None]
            free_ph = [None, None]
            free_po = [None, None]
            free_w1f = [None, None, None]
            free_w2 = None
            free_b1 = None
            phi = 0
            w1i = 0
            for ei in range(E1):
                t_w2 = sc.op("gpsimd", lambda e, ei=ei: e.dma_start(
                    out=sw2s.rearrange("p (f o) -> p f o", f=32),
                    in_=w2b_d[ei].rearrange("(f p) o -> p f o", p=128)),
                    waits=[tok_cast, free_w2], inc=s_w2l, dma=True)
                t_b1 = sc.op("sync", lambda e, ei=ei: e.dma_start(out=sb1f[:], in_=b1_d[ei]),
                             waits=[free_b1], inc=s_b1l, dma=True)
                last_mm_e = None
                for tc in range(8):
                    bx = tc % 2
                    tok_xn = None
                    for q4 in range(4):
                        tok_xn = sc.op("sync", lambda e, bx=bx, tc=tc, q4=q4: e.dma_start(
                            out=sxnb[bx].rearrange("p (c q t) -> p c q t", c=8, q=4)[:, :, q4, :],
                            in_=xnq1_d[tc * 4 + q4].rearrange("c p t -> p c t")),
                            waits=[free_xnb[bx]] if q4 == 0 else [], inc=s_nb[bx], dma=True)
                    tok_bc = sc.op("sync", lambda e, bx=bx, ei=ei, tc=tc: e.dma_start(
                        out=sbc[bx][:], in_=rsc_d[ei:ei + 1, tc * 512:(tc + 1) * 512].broadcast_to([128, 512])),
                        waits=[free_bc[bx]], inc=s_bc[bx], dma=True)
                    g_toks = []
                    prev_G = None
                    for f in range(32):
                        wslot = w1i % 3
                        w1i += 1
                        t_w1f = sc.op("gpsimd", lambda e, ei=ei, f=f, wslot=wslot: e.dma_start(
                            out=sw1f[wslot].rearrange("p (c t) -> p c t", c=8),
                            in_=w1b_d[ei].rearrange("(c p) f -> p c f", p=128)[:, :, f * 128:(f + 1) * 128]),
                            waits=[tok_cast, free_w1f[wslot]], inc=s_w1f[wslot], dma=True)
                        pb = phi % 2
                        phi += 1
                        mm1 = None
                        for c in range(8):
                            mm1 = sc.op("tensor", lambda e, f=f, c=c, pb=pb, bx=bx, wslot=wslot: e.matmul(
                                ph[pb][:], sw1f[wslot][:, c * 128:(c + 1) * 128],
                                sxnb[bx][:, c * 512:(c + 1) * 512], start=(c == 0), stop=(c == 7)),
                                waits=[t_w1f, tok_xn, free_ph[pb]] if c == 0 else [],
                                inc=s_t if c == 7 else None)
                        free_w1f[wslot] = mm1
                        tok_G = sc.op("scalar", lambda e, f=f, pb=pb: e.activation(
                            out=sG[:, f * 512:(f + 1) * 512], in_=ph[pb][:], func=AF.Gelu,
                            bias=sb1f[:, f:f + 1]),
                            waits=[mm1, t_b1, prev_G])
                        prev_G = tok_G
                        free_ph[pb] = tok_G
                        g_toks.append(tok_G)
                    for oc in range(8):
                        pb = phi % 2
                        phi += 1
                        mm2 = None
                        for f in range(32):
                            mm2 = sc.op("tensor", lambda e, oc=oc, f=f, pb=pb: e.matmul(
                                po[pb][:], sw2s[:, f * D + oc * 128: f * D + oc * 128 + 128],
                                sG[:, f * 512:(f + 1) * 512], start=(f == 0), stop=(f == 31)),
                                waits=[t_w2, g_toks[-1], free_po[pb]] if f == 0 else [],
                                inc=s_t if f == 31 else None)
                        tok_t5 = sc.op("vector", lambda e, pb=pb, bx=bx: e.tensor_tensor(
                            stmp5[:], po[pb][:], sbc[bx][:], ALU.mult), waits=[mm2, tok_bc])
                        acc_tok = sc.op("vector", lambda e, oc=oc, tc=tc: e.tensor_tensor(
                            sacc[:, oc * T + tc * 512: oc * T + (tc + 1) * 512],
                            sacc[:, oc * T + tc * 512: oc * T + (tc + 1) * 512], stmp5[:], ALU.add),
                            waits=[acc_tok])
                        free_po[pb] = tok_t5
                        last_mm_e = mm2
                    free_xnb[bx] = last_mm_e
                    free_bc[bx] = acc_tok
                free_w2 = last_mm_e
                free_b1 = prev_G
            sc.barrier()
            sc.emit_phase("L_ffn")

            with ExitStack() as cg:
                en4 = cg.enter_context
                sx2p = [en4(nc.sbuf_tensor(f"{px}sx2p{j}", [128, D], F32)).ap() for j in range(2)]
                sxo = [en4(nc.sbuf_tensor(f"{px}sxo{j}", [128, D], F32)).ap() for j in range(2)]
                pacc = [en4(nc.psum_tensor(f"{px}pacc{j}", [128, 128], BF16)).ap() for j in range(2)]
                s_xp = [sc.sem(f"{px}xp{j}") for j in range(2)]
                s_xo = [sc.sem(f"{px}xoo{j}") for j in range(2)]
                free_p = [None, None]
                free_o = [None, None]
                free_i = [None, None]
                for i in range(TT):
                    b = i % 2
                    tok_x2p = sc.op("sync", lambda e, b=b, i=i: e.dma_start(out=sx2p[b][:],
                                                                           in_=x2p_d[i * 128:(i + 1) * 128, :]),
                                    waits=[free_i[b]], inc=s_xp[b], dma=True)
                    add_toks = []
                    for c in range(8):
                        pb = c % 2
                        tok_T = sc.op("tensor", lambda e, i=i, c=c, pb=pb: e.transpose(
                            pacc[pb][:], sacc[:, c * T + i * 128: c * T + i * 128 + 128], sidb[:]),
                            waits=[(s_c, 16), free_p[pb]] + ([acc_tok] if i == 0 and c == 0 else []), inc=s_t)
                        tok_add = sc.op("vector", lambda e, b=b, c=c, pb=pb: e.tensor_tensor(
                            sxo[b][:, c * 128:(c + 1) * 128], sx2p[b][:, c * 128:(c + 1) * 128],
                            pacc[pb][:], ALU.add),
                            waits=[tok_T, tok_x2p] + ([free_o[b]] if c == 0 else []))
                        free_p[pb] = tok_add
                        add_toks.append(tok_add)
                    tok_st2 = sc.op("sync", lambda e, b=b, i=i: e.dma_start(out=out_d[i * 128:(i + 1) * 128, :],
                                                                           in_=sxo[b][:]),
                                    waits=[add_toks[-1]], inc=s_xo[b], dma=True)
                    free_o[b] = tok_st2
                    free_i[b] = add_toks[-1]
                sc.barrier()
                sc.emit_phase("M_final")


def build_k4():
    nc = bass.Bass("TRN2", target_bir_lowering=False)
    xnq1_d = nc.dram_tensor("xnq1", [TT, 8, 128, 128], BF16, kind="ExternalInput").ap()
    rsc_d = nc.dram_tensor("rsc", [E1, T], F32, kind="ExternalInput").ap()
    x2p_d = nc.dram_tensor("x2p", [T, D], F32, kind="ExternalInput").ap()
    w1_d = nc.dram_tensor("w1", [E1, D, F1], F32, kind="ExternalInput").ap()
    b1_d = nc.dram_tensor("b1r", [E1, 128, 32], F32, kind="ExternalInput").ap()
    w2_d = nc.dram_tensor("w2", [E1, F1, D], F32, kind="ExternalInput").ap()
    b2_d = nc.dram_tensor("b2", [E1, D], F32, kind="ExternalInput").ap()
    idb_d = nc.dram_tensor("idb", [128, 128], BF16, kind="ExternalInput").ap()
    out_d = nc.dram_tensor("outp", [T, D], F32, kind="ExternalOutput").ap()
    w1b_d = nc.dram_tensor("w1b", [E1, D, F1], BF16, kind="Internal").ap()
    w2b_d = nc.dram_tensor("w2b", [E1, F1, D], BF16, kind="Internal").ap()
    sc = Sched(nc)
    emit_k4(nc, sc, "d_", xnq1_d, rsc_d, x2p_d, w1_d, b1_d, w2_d, b2_d,
            idb_d, out_d, w1b_d, w2b_d)
    return nc


def k4_inputs(inputs, res3):
    idb, idf, cm = _host_consts()
    b1r = np.ascontiguousarray(inputs["L1_ffn_b1"].reshape(E1, 32, 128).transpose(0, 2, 1))
    maps = []
    for b in range(NCORE):
        maps.append({
            "xnq1": np.asarray(res3.results[b]["xnq1"]),
            "rsc": np.asarray(res3.results[b]["rsc"]),
            "x2p": np.asarray(res3.results[b]["x2p"]),
            "w1": inputs["L1_ffn_W1"], "b1r": b1r,
            "w2": inputs["L1_ffn_W2"], "b2": inputs["L1_ffn_b2"],
            "idb": idb,
        })
    return maps



def _host_consts():
    idb = np.eye(128, dtype=np.float32).astype(BF)
    idf = np.eye(128, dtype=np.float32)
    cm = np.triu(np.full((128, 128), NEG, np.float32), 1)
    return idb, idf, cm


def k1_inputs(inputs):
    idb, idf, cm = _host_consts()
    x = np.ascontiguousarray(inputs["L0_attn_x"].astype(np.float32))
    wt = np.ascontiguousarray(inputs["L0_attn_W"].T)
    gw = np.ascontiguousarray(inputs["L0_ffn_gate_w"])
    gb = np.ascontiguousarray(inputs["L0_ffn_gate_b"].reshape(1, E0))
    return [{"x": x[b], "wt": wt, "gw": gw, "gb": gb, "idb": idb, "idf": idf, "cm": cm}
            for b in range(NCORE)]


def build_fused():
    nc = bass.Bass("TRN2", target_bir_lowering=False, num_devices=NCORE)
    I, O, N = "ExternalInput", "ExternalOutput", "Internal"
    dt = nc.dram_tensor
    x_d = dt("x", [T, D], F32, kind=I).ap()
    wt_d = dt("wt", [D, 2 * D], F32, kind=I).ap()
    gw_d = dt("gw", [D, E0], F32, kind=I).ap()
    gb_d = dt("gb", [1, E0], F32, kind=I).ap()
    idb_d = dt("idb", [128, 128], BF16, kind=I).ap()
    idf_d = dt("idf", [128, 128], F32, kind=I).ap()
    cm_d = dt("cm", [128, 128], F32, kind=I).ap()
    ew1_d = dt("ew1", [E0, D, H0], F32, kind=I).ap()
    ew3_d = dt("ew3", [E0, D, H0], F32, kind=I).ap()
    ew2_d = dt("ew2", [E0, H0, D], F32, kind=I).ap()
    sw1_d = dt("sw1", [2, D, H0], F32, kind=I).ap()
    sw3_d = dt("sw3", [2, D, H0], F32, kind=I).ap()
    sw2_d = dt("sw2", [2, H0, D], F32, kind=I).ap()
    ssel_d = dt("Ssel", [128, E0], F32, kind=I).ap()
    stsel_d = dt("STsel", [E0, 128], F32, kind=I).ap()
    rsel_d = dt("Rsel", [128, E0], F32, kind=I).ap()
    ones_d = dt("ones_f", [1, 128], F32, kind=I).ap()
    wr_d = dt("wr", [D, E1], F32, kind=I).ap()
    wqT_d = dt("wqT", [D, 3 * D], F32, kind=I).ap()
    wcp_d = dt("wcp", [128, 3 * 8 * DH], F32, kind=I).ap()
    bcv_d = dt("bcv", [128, 8], F32, kind=I).ap()
    nlk_d = dt("nlk", [128, 8], F32, kind=I).ap()
    nlv_d = dt("nlv", [128, 8], F32, kind=I).ap()
    woT_d = dt("woT", [D, D], F32, kind=I).ap()
    bo_d = dt("bo", [1, D], F32, kind=I).ap()
    cml_d = dt("cml", [TT, 128, 64], F32, kind=I).ap()
    w1_d = dt("w1", [E1, D, F1], F32, kind=I).ap()
    b1_d = dt("b1r", [E1, 128, 32], F32, kind=I).ap()
    w2_d = dt("w2", [E1, F1, D], F32, kind=I).ap()
    b2_d = dt("b2", [E1, D], F32, kind=I).ap()
    out_d = dt("outp", [T, D], F32, kind=O).ap()
    x1p_d = dt("x1p", [T, D], F32, kind=N).ap()
    xnq_d = dt("xnq", [TT, 8, 128, 128], BF16, kind=N).ap()
    wmf_d = dt("wmf", [E0, T], F32, kind=N).ap()
    wmt_d = dt("wmt", [T, E0], F32, kind=N).ap()
    x0xn_d = dt("x0xn", [T, D], F32, kind=N).ap()
    wgf_d = dt("wgf", [128, T], F32, kind=N).ap()
    x1_d = dt("x1", [T, D], F32, kind=N).ap()
    wfin_d = dt("wfin", [E0, T], F32, kind=N).ap()
    ew1b_d = dt("ew1b", [E0, D, H0], BF16, kind=N).ap()
    ew3b_d = dt("ew3b", [E0, D, H0], BF16, kind=N).ap()
    ew2b_d = dt("ew2b", [E0, H0, D], BF16, kind=N).ap()
    sw1b_d = dt("sw1b", [2, D, H0], BF16, kind=N).ap()
    sw3b_d = dt("sw3b", [2, D, H0], BF16, kind=N).ap()
    sw2b_d = dt("sw2b", [2, H0, D], BF16, kind=N).ap()
    xnq1_d = dt("xnq1", [TT, 8, 128, 128], BF16, kind=N).ap()
    rsc_d = dt("rsc", [E1, T], F32, kind=N).ap()
    x2p_d = dt("x2p", [T, D], F32, kind=N).ap()
    qq_d = dt("qq", [TT, 8, 128, 128], BF16, kind=N).ap()
    attn_d = dt("attn", [TT, 8, 128, 128], BF16, kind=N).ap()
    w1b_d = dt("w1b", [E1, D, F1], BF16, kind=N).ap()
    w2b_d = dt("w2b", [E1, F1, D], BF16, kind=N).ap()

    sc = Sched(nc)
    emit_k1(nc, sc, "a_", x_d, wt_d, gw_d, gb_d, idb_d, idf_d, cm_d,
            x1p_d, xnq_d, wmf_d, wmt_d, x0xn_d)
    tok_ag = sc.op("gpsimd", lambda e: e.collective_compute(
        "AllGather", mybir.AluOpType.bypass,
        replica_groups=[list(range(NCORE))],
        ins=[wmf_d.opt()], outs=[wgf_d.opt()]))
    emit_k2(nc, sc, "b_", xnq_d, wmt_d, wgf_d, x1p_d, ew1_d, ew3_d, ew2_d,
            sw1_d, sw3_d, sw2_d, ssel_d, stsel_d, rsel_d, ones_d, idb_d, idf_d,
            x1_d, wfin_d, ew1b_d, ew3b_d, ew2b_d, sw1b_d, sw3b_d, sw2b_d,
            wgf_wait=tok_ag)
    emit_k3(nc, sc, "c_", x1_d, wr_d, wqT_d, wcp_d, bcv_d, nlk_d, nlv_d,
            woT_d, bo_d, cml_d, idb_d, idf_d, x2p_d, xnq1_d, rsc_d, qq_d, attn_d)
    emit_k4(nc, sc, "d_", xnq1_d, rsc_d, x2p_d, w1_d, b1_d, w2_d, b2_d,
            idb_d, out_d, w1b_d, w2b_d)
    return nc


def fused_inputs(inputs):
    idb, idf, cm = _host_consts()
    x = np.ascontiguousarray(inputs["L0_attn_x"].astype(np.float32))
    wt = np.ascontiguousarray(inputs["L0_attn_W"].T)
    gw = np.ascontiguousarray(inputs["L0_ffn_gate_w"])
    gb = np.ascontiguousarray(inputs["L0_ffn_gate_b"].reshape(1, E0))
    S = np.zeros((128, E0), np.float32)
    for r in range(NCORE):
        for e_i in range(E0):
            S[r * E0 + e_i, e_i] = 1.0
    ST = np.ascontiguousarray(S.T)
    ones_f = np.ones((1, 128), np.float32)
    wqT = np.ascontiguousarray(inputs["L1_attn_w_qkv"].T)
    wc = inputs["L1_attn_w_conv"]
    wcp = np.zeros((128, 3 * 8 * DH), np.float32)
    for h in range(NH):
        base = (h % 2) * 64
        hp = h // 2
        blk = wc[h * DH:(h + 1) * DH]
        for tap in range(3):
            wcp[base:base + 64, tap * 512 + hp * 64: tap * 512 + (hp + 1) * 64] = blk[:, :, tap].T
    bcv = np.ascontiguousarray(inputs["L1_attn_b_conv"].reshape(8, 128).T)
    nlk = np.ascontiguousarray(inputs["L1_attn_null_k"].reshape(8, 128).T)
    nlv = np.ascontiguousarray(inputs["L1_attn_null_v"].reshape(8, 128).T)
    woT = np.ascontiguousarray(inputs["L1_attn_w_out"].T)
    bo = np.ascontiguousarray(inputs["L1_attn_b_out"].reshape(1, D))
    wr = np.ascontiguousarray(inputs["L1_ffn_Wr"])
    cml = np.zeros((TT, 128, 64), np.float32)
    for qi in range(TT):
        Lq, win0 = _lq(qi)
        for r in range(128):
            i = qi * 128 + r
            fm = i // 3 + 2
            for c in range(64):
                if win0 + c >= fm:
                    cml[qi, r, c] = NEG
    b1r = np.ascontiguousarray(inputs["L1_ffn_b1"].reshape(E1, 32, 128).transpose(0, 2, 1))
    maps = []
    for b in range(NCORE):
        R = np.zeros((128, E0), np.float32)
        for e_i in range(E0):
            R[b * E0 + e_i, e_i] = 1.0
        maps.append({
            "x": x[b], "wt": wt, "gw": gw, "gb": gb, "idb": idb, "idf": idf, "cm": cm,
            "ew1": inputs["L0_ffn_ew1"], "ew3": inputs["L0_ffn_ew3"], "ew2": inputs["L0_ffn_ew2"],
            "sw1": inputs["L0_ffn_sw1"], "sw3": inputs["L0_ffn_sw3"], "sw2": inputs["L0_ffn_sw2"],
            "Ssel": S, "STsel": ST, "Rsel": R, "ones_f": ones_f,
            "wr": wr, "wqT": wqT, "wcp": wcp, "bcv": bcv, "nlk": nlk, "nlv": nlv,
            "woT": woT, "bo": bo, "cml": cml,
            "w1": inputs["L1_ffn_W1"], "b1r": b1r, "w2": inputs["L1_ffn_W2"],
            "b2": inputs["L1_ffn_b2"],
        })
    return maps


def kernel(**inputs):
    nc = build_fused()
    res = run_bass_kernel_spmd(nc, fused_inputs(inputs), core_ids=list(range(NCORE)))
    return np.stack([np.asarray(res.results[b]["outp"]) for b in range(NCORE)])


def kernel_multilaunch(**inputs):
    nc1 = build_k1()
    res1 = run_bass_kernel_spmd(nc1, k1_inputs(inputs), core_ids=list(range(NCORE)))
    nc2 = build_k2()
    res2 = run_bass_kernel_spmd(nc2, k2_inputs(inputs, res1), core_ids=list(range(NCORE)))
    nc3 = build_k3()
    res3 = run_bass_kernel_spmd(nc3, k3_inputs(inputs, res2), core_ids=list(range(NCORE)))
    nc4 = build_k4()
    res4 = run_bass_kernel_spmd(nc4, k4_inputs(inputs, res3), core_ids=list(range(NCORE)))
    out = np.stack([np.asarray(res4.results[b]["outp"]) for b in range(NCORE)])
    return out


# version 04
# speedup vs baseline: 1.0105x; 1.0105x over previous
import math
from contextlib import ExitStack

import numpy as np
import ml_dtypes

import concourse.bass as bass
import concourse.mybir as mybir
from concourse.bass_utils import run_bass_kernel_spmd

F32 = mybir.dt.float32
BF16 = mybir.dt.bfloat16
AF = mybir.ActivationFunctionType
ALU = mybir.AluOpType
AX = mybir.AxisListType
BF = ml_dtypes.bfloat16

T, D = 4096, 1024
TT = T // 128
NCORE = 8
E0 = 16
CAP = math.ceil(NCORE * T * 2 / E0 * 1.25)
E1 = 8
H0 = 2048
F1 = 4096
NEG = -1e30
N_BIS = 34
BIS_BOUND = 1e5

ENGS = ("sync", "tensor", "vector", "scalar", "gpsimd")
CHAIN = ("vector", "scalar", "gpsimd")


def _aslist(x):
    if x is None:
        return []
    if isinstance(x, list):
        return [t for t in x if t is not None]
    return [x]


class Sem:
    __slots__ = ("h", "c", "name")

    def __init__(self, h, name):
        self.h = h
        self.c = 0
        self.name = name


class Sched:
    def __init__(self, nc):
        self.nc = nc
        self.items = {e: [] for e in ENGS}
        self._phase_idx = 0
        self.all_sems = []
        self.chain = {e: self.sem(f"ch_{e}") for e in CHAIN}
        self._bar = self.sem("bar")

    def sem(self, name):
        s = Sem(self.nc.alloc_semaphore(name), name)
        self.all_sems.append(s)
        return s

    def op(self, eng, fn, waits=(), inc=None, dma=False):
        ws = []
        for w in waits:
            ws.extend(_aslist(w))
        if eng in CHAIN and not dma:
            assert inc is None
            cs = self.chain[eng]
            if cs.c > 0:
                ws.append((cs, cs.c))
            cs.c += 1
            self.items[eng].append((fn, tuple(ws), cs, 1))
            return (cs, cs.c)
        amt = 16 if dma else 1
        tok = None
        if inc is not None:
            inc.c += amt
            tok = (inc, inc.c)
        self.items[eng].append((fn, tuple(ws), inc, amt))
        return tok

    def barrier(self):
        bs = self._bar
        ex = tuple((s, s.c) for s in self.all_sems if s.c > 0 and s is not bs)
        for e in ENGS:
            self.items[e].append(("drain", ex, bs, 1))
            bs.c += 1
        tgt = bs.c
        for e in ENGS:
            self.items[e].append((None, ((bs, tgt),), None, 0))

    def emit_phase(self, name):
        items = self.items
        self.items = {e: [] for e in ENGS}
        self._phase_idx += 1
        with self.nc.Block(f"ph{self._phase_idx}_{name}") as block:
            for e in ENGS:
                lst = items[e]
                if not lst:
                    continue

                def body(eng, lst=lst):
                    water = {}
                    for fn, waits, inc, amt in lst:
                        for (s, v) in waits:
                            if water.get(id(s), -1) < v:
                                eng.wait_ge(s.h, v)
                                water[id(s)] = v
                        ins = None
                        if fn == "drain":
                            ins = eng.drain()
                        elif fn is not None:
                            ins = fn(eng)
                        if inc is not None:
                            if ins is None:
                                ins = eng.nop()
                            ins.then_inc(inc.h, amt)

                getattr(block, e)(body)



def emit_k1(nc, sc, px, x_d, wt_d, gw_d, gb_d, idb_d, idf_d, cm_d,
            x1p_d, xnq_d, wmf_d, wmt_d, x0xn_d):
    with ExitStack() as ctx:
        en = ctx.enter_context
        K_fm = en(nc.sbuf_tensor(px + "K_fm", [128, 8 * T], BF16)).ap()
        V_tm = en(nc.sbuf_tensor(px + "V_tm", [128, TT * 1024], BF16)).ap()
        sidb = en(nc.sbuf_tensor(px + "sidb", [128, 128], BF16)).ap()
        sidf = en(nc.sbuf_tensor(px + "sidf", [128, 128], F32)).ap()
        scm = en(nc.sbuf_tensor(px + "scm", [128, 128], F32)).ap()
        sgw = en(nc.sbuf_tensor(px + "sgw", [128, 8 * E0], BF16)).ap()
        sgb = en(nc.sbuf_tensor(px + "sgb", [128, E0], F32)).ap()

        s_t = sc.sem(px + "t")
        s_c = sc.sem(px + "c")
        s_gw = sc.sem(px + "gwl")
        s_wt = sc.sem(px + "wtl")

        sc.op("sync", lambda e: e.dma_start(out=sidb[:], in_=idb_d[:]), inc=s_c, dma=True)
        sc.op("sync", lambda e: e.dma_start(out=sidf[:], in_=idf_d[:]), inc=s_c, dma=True)
        sc.op("sync", lambda e: e.dma_start(out=scm[:], in_=cm_d[:]), inc=s_c, dma=True)
        sc.op("sync", lambda e: e.dma_start(out=sgb[:], in_=gb_d.broadcast_to([128, E0])), inc=s_c, dma=True)
        t_gw = sc.op("gpsimd", lambda e: e.dma_start(out=sgw.rearrange("p (c e) -> p c e", c=8),
                                                     in_=gw_d.rearrange("(c p) e -> p c e", p=128)),
                     inc=s_gw, dma=True)
        t_c = (s_c, s_c.c)

        with ExitStack() as ca:
            en2 = ca.enter_context
            sx = [en2(nc.sbuf_tensor(f"{px}sx{j}", [128, D], F32)).ap() for j in range(2)]
            sx0xn = [en2(nc.sbuf_tensor(f"{px}sx0xn{j}", [128, D], F32)).ap() for j in range(2)]
            sxh = [en2(nc.sbuf_tensor(f"{px}sxh{j}", [128, D], BF16)).ap() for j in range(2)]
            stg = [en2(nc.sbuf_tensor(f"{px}stg{j}", [128, D], BF16)).ap() for j in range(2)]
            swmt = [en2(nc.sbuf_tensor(f"{px}swmt{j}", [128, E0], F32)).ap() for j in range(2)]
            sxn = en2(nc.sbuf_tensor(px + "sxn", [128, D], F32)).ap()
            sqscr = en2(nc.sbuf_tensor(px + "sqscr", [128, D], F32)).ap()
            swmf = en2(nc.sbuf_tensor(px + "swmf", [128, T], F32)).ap()
            seps = en2(nc.sbuf_tensor(px + "seps", [128, 1], F32)).ap()
            ssum = en2(nc.sbuf_tensor(px + "ssum", [128, 1], F32)).ap()
            ssumsq = en2(nc.sbuf_tensor(px + "ssumsq", [128, 1], F32)).ap()
            smean = en2(nc.sbuf_tensor(px + "smean", [128, 1], F32)).ap()
            sm2t = en2(nc.sbuf_tensor(px + "sm2t", [128, 1], F32)).ap()
            svar = en2(nc.sbuf_tensor(px + "svar", [128, 1], F32)).ap()
            srstd = en2(nc.sbuf_tensor(px + "srstd", [128, 1], F32)).ap()
            slog = en2(nc.sbuf_tensor(px + "slog", [128, E0], F32)).ap()
            snegmx = en2(nc.sbuf_tensor(px + "snegmx", [128, 1], F32)).ap()
            sexp = en2(nc.sbuf_tensor(px + "sexp", [128, E0], F32)).ap()
            sesum = en2(nc.sbuf_tensor(px + "sesum", [128, 1], F32)).ap()
            srcp = en2(nc.sbuf_tensor(px + "srcp", [128, 1], F32)).ap()
            sscor = en2(nc.sbuf_tensor(px + "sscor", [128, E0], F32)).ap()
            sm1 = en2(nc.sbuf_tensor(px + "sm1", [128, 1], F32)).ap()
            seq1 = en2(nc.sbuf_tensor(px + "seq1", [128, E0], F32)).ap()
            smk = en2(nc.sbuf_tensor(px + "smk", [128, E0], F32)).ap()
            sm2 = en2(nc.sbuf_tensor(px + "sm2", [128, 1], F32)).ap()
            seq2 = en2(nc.sbuf_tensor(px + "seq2", [128, E0], F32)).ap()
            sasg = en2(nc.sbuf_tensor(px + "sasg", [128, E0], F32)).ap()
            st1 = en2(nc.sbuf_tensor(px + "st1", [128, E0], F32)).ap()
            st2 = en2(nc.sbuf_tensor(px + "st2", [128, E0], F32)).ap()
            ptp = [en2(nc.psum_tensor(f"{px}ptp{j}", [128, 128], BF16)).ap() for j in range(2)]
            plog = en2(nc.psum_tensor(px + "plog", [128, E0], F32)).ap()
            pwm = en2(nc.psum_tensor(px + "pwm", [E0, 128], F32)).ap()

            s_x = [sc.sem(f"{px}x{j}") for j in range(2)]
            s_sx0 = [sc.sem(f"{px}sx0_{j}") for j in range(2)]
            s_sq = [sc.sem(f"{px}sq{j}") for j in range(2)]
            s_swm = [sc.sem(f"{px}swm{j}") for j in range(2)]
            s_swf = sc.sem(px + "swf")

            t_eps = sc.op("vector", lambda e: e.memset(seps[:], 1e-5))
            free_sx = [None, None]
            free_x0 = [None, None]
            free_xh = [None, None]
            free_stg = [None, None]
            free_wmt = [None, None]
            free_ptp = [None, None]
            free_plog = None
            free_pwm = None
            last_wmf_cp = None
            prev_var_read = None
            prev_xn = None
            for i in range(TT):
                b = i % 2
                r0 = i * 128
                tok_x = sc.op("sync", lambda e, b=b, r0=r0: e.dma_start(out=sx[b][:], in_=x_d[r0:r0 + 128, :]),
                              waits=[free_sx[b]], inc=s_x[b], dma=True)
                sc.op("vector", lambda e, b=b: e.reduce_sum(ssum[:], sx[b][:], axis=AX.X), waits=[tok_x])
                tok_sq = sc.op("scalar", lambda e, b=b: e.activation(out=sqscr[:], in_=sx[b][:], func=AF.Square,
                                                                    accum_out=ssumsq[:]),
                               waits=[tok_x, prev_var_read])
                tok_mean = sc.op("vector", lambda e: e.tensor_scalar_mul(smean[:], ssum[:], 1.0 / D))
                sc.op("vector", lambda e: e.tensor_mul(sm2t[:], smean[:], smean[:]))
                prev_var_read = sc.op("vector", lambda e: e.tensor_scalar_mul(svar[:], ssumsq[:], 1.0 / D),
                                      waits=[tok_sq])
                tok_var = sc.op("vector", lambda e: e.tensor_sub(svar[:], svar[:], sm2t[:]))
                sc.op("scalar", lambda e: e.activation(out=srstd[:], in_=svar[:], func=AF.Ln, bias=seps[:, 0:1]),
                      waits=[tok_var, t_eps, prev_xn])
                tok_rstd = sc.op("scalar", lambda e: e.activation(out=srstd[:], in_=srstd[:], func=AF.Exp, scale=-0.5))
                tok_xn = sc.op("vector", lambda e, b=b: e.tensor_scalar(out=sxn[:], in0=sx[b][:], scalar1=smean[:, 0:1],
                                                                       scalar2=srstd[:, 0:1], op0=ALU.subtract,
                                                                       op1=ALU.mult),
                               waits=[tok_mean, tok_rstd])
                prev_xn = tok_xn
                tok_x0xn = sc.op("vector", lambda e, b=b: e.tensor_add(sx0xn[b][:], sx[b][:], sxn[:]),
                                 waits=[free_x0[b]])
                free_sx[b] = tok_x0xn
                free_x0[b] = sc.op("sync", lambda e, b=b, r0=r0: e.dma_start(out=x0xn_d[r0:r0 + 128, :],
                                                                            in_=sx0xn[b][:]),
                                   waits=[tok_x0xn], inc=s_sx0[b], dma=True)
                tok_xh = sc.op("vector", lambda e, b=b: e.tensor_copy(sxh[b][:], sxn[:]), waits=[free_xh[b]])
                cp_toks = []
                wstg = free_stg[b]
                tok_tr = None
                for c in range(8):
                    tok_tr = sc.op("tensor", lambda e, b=b, c=c: e.transpose(ptp[c % 2][:],
                                                                             sxh[b][:, c * 128:(c + 1) * 128],
                                                                             sidb[:]),
                                   waits=[tok_xh, t_c, free_ptp[c % 2]], inc=s_t)
                    tok_cp = sc.op("vector", lambda e, b=b, c=c: e.tensor_copy(stg[b][:, c * 128:(c + 1) * 128],
                                                                               ptp[c % 2][:]),
                                   waits=[tok_tr, wstg])
                    free_ptp[c % 2] = tok_cp
                    cp_toks.append(tok_cp)
                free_xh[b] = tok_tr
                tok_stg = cp_toks[-1]
                tok_xnq = sc.op("sync", lambda e, b=b, i=i: e.dma_start(
                    out=xnq_d[i].rearrange("c p t -> p c t"),
                    in_=stg[b].rearrange("p (c t) -> p c t", c=8)),
                    waits=[tok_stg], inc=s_sq[b], dma=True)
                mm_tok = None
                for c in range(8):
                    mm_tok = sc.op("tensor", lambda e, b=b, c=c: e.matmul(plog[:], stg[b][:, c * 128:(c + 1) * 128],
                                                                          sgw[:, c * E0:(c + 1) * E0],
                                                                          start=(c == 0), stop=(c == 7)),
                                   waits=[tok_stg, t_gw, free_plog] if c == 0 else [],
                                   inc=s_t if c == 7 else None)
                free_stg[b] = [tok_xnq, mm_tok]
                tok_logcp = sc.op("vector", lambda e: e.tensor_copy(slog[:], plog[:]), waits=[mm_tok])
                free_plog = tok_logcp
                tok_ngx = sc.op("vector", lambda e: e.reduce_max(snegmx[:], slog[:], axis=AX.X, negate=True))
                tok_exp = sc.op("scalar", lambda e: e.activation(out=sexp[:], in_=slog[:], func=AF.Exp,
                                                                bias=snegmx[:, 0:1], accum_out=sesum[:]),
                                waits=[tok_ngx, tok_logcp])
                tok_rcp = sc.op("vector", lambda e: e.reciprocal(srcp[:], sesum[:]), waits=[tok_exp])
                sc.op("vector", lambda e: e.scalar_tensor_tensor(out=sscor[:], in0=sexp[:], scalar=srcp[:, 0:1],
                                                                 in1=sgb[:], op0=ALU.mult, op1=ALU.add),
                      waits=[tok_rcp, t_c])
                tok_m1 = sc.op("vector", lambda e: e.reduce_max(sm1[:], sscor[:], axis=AX.X))
                sc.op("vector", lambda e: e.tensor_scalar(out=seq1[:], in0=sscor[:], scalar1=sm1[:, 0:1],
                                                          scalar2=None, op0=ALU.is_equal),
                      waits=[tok_m1])
                sc.op("vector", lambda e: e.scalar_tensor_tensor(out=smk[:], in0=seq1[:], scalar=NEG,
                                                                 in1=sscor[:], op0=ALU.mult, op1=ALU.add))
                tok_m2 = sc.op("vector", lambda e: e.reduce_max(sm2[:], smk[:], axis=AX.X))
                sc.op("vector", lambda e: e.tensor_scalar(out=seq2[:], in0=smk[:], scalar1=sm2[:, 0:1],
                                                          scalar2=None, op0=ALU.is_equal),
                      waits=[tok_m2])
                sc.op("vector", lambda e: e.tensor_add(sasg[:], seq1[:], seq2[:]))
                sc.op("vector", lambda e: e.tensor_mul(st1[:], slog[:], sasg[:]))
                sc.op("vector", lambda e: e.tensor_scalar(out=st2[:], in0=sasg[:], scalar1=1.0, scalar2=-NEG,
                                                          op0=ALU.subtract, op1=ALU.mult))
                tok_wmt = sc.op("vector", lambda e, b=b: e.tensor_add(swmt[b][:], st1[:], st2[:]),
                                waits=[free_wmt[b]])
                tok_wmt_st = sc.op("sync", lambda e, b=b, r0=r0: e.dma_start(out=wmt_d[r0:r0 + 128, :],
                                                                            in_=swmt[b][:]),
                                   waits=[tok_wmt], inc=s_swm[b], dma=True)
                tok_wmT = sc.op("tensor", lambda e, b=b: e.transpose(pwm[:], swmt[b][:], sidf[:]),
                                waits=[tok_wmt, t_c, free_pwm], inc=s_t)
                free_wmt[b] = [tok_wmt_st, tok_wmT]
                tok_wmfcp = sc.op("vector", lambda e, i=i: e.tensor_copy(swmf[0:E0, i * 128:(i + 1) * 128], pwm[:]),
                                  waits=[tok_wmT])
                free_pwm = tok_wmfcp
                last_wmf_cp = tok_wmfcp
            sc.op("sync", lambda e: e.dma_start(out=wmf_d[:], in_=swmf[0:E0, :]), waits=[last_wmf_cp],
                  inc=s_swf, dma=True)
            sc.barrier()
            sc.emit_phase("A_ln0")

        with ExitStack() as cb:
            en2 = cb.enter_context
            sWT = en2(nc.sbuf_tensor(px + "sWT", [128, 8 * 2 * D], BF16)).ap()
            sxnb = [en2(nc.sbuf_tensor(f"{px}sxnb{j}", [128, 8 * 512], BF16)).ap() for j in range(2)]
            pkv = [en2(nc.psum_tensor(f"{px}pkv{j}", [128, 512], F32)).ap() for j in range(2)]

            s_nb = [sc.sem(f"{px}nb{j}") for j in range(2)]
            t_wt = sc.op("gpsimd", lambda e: e.dma_start(out=sWT.rearrange("p (c k) -> p c k", c=8),
                                                         in_=wt_d.rearrange("(c p) k -> p c k", p=128)),
                         inc=s_wt, dma=True)
            free_xnb = [None, None]
            free_pkv = [None, None]
            pi = 0
            for tc in range(8):
                b = tc % 2
                tok_xn = None
                for q4 in range(4):
                    tok_xn = sc.op("sync", lambda e, b=b, tc=tc, q4=q4: e.dma_start(
                        out=sxnb[b].rearrange("p (c q t) -> p c q t", c=8, q=4)[:, :, q4, :],
                        in_=xnq_d[tc * 4 + q4].rearrange("c p t -> p c t")),
                        waits=[free_xnb[b]] if q4 == 0 else [], inc=s_nb[b], dma=True)
                last_mm = None
                for kk in range(8):
                    pb = pi % 2
                    pi += 1
                    mm = None
                    for c in range(8):
                        mm = sc.op("tensor", lambda e, b=b, kk=kk, c=c, pb=pb: e.matmul(
                            pkv[pb][:], sWT[:, c * 2048 + kk * 128: c * 2048 + kk * 128 + 128],
                            sxnb[b][:, c * 512:(c + 1) * 512], start=(c == 0), stop=(c == 7)),
                            waits=[tok_xn, t_wt, free_pkv[pb]] if c == 0 else [],
                            inc=s_t if c == 7 else None)
                    tok_ev = sc.op("vector", lambda e, kk=kk, tc=tc, pb=pb: e.tensor_copy(
                        K_fm[:, kk * T + tc * 512: kk * T + (tc + 1) * 512], pkv[pb][:]), waits=[mm])
                    free_pkv[pb] = tok_ev
                    last_mm = mm
                for q4 in range(4):
                    qt = tc * 4 + q4
                    for dvh in range(2):
                        pb = pi % 2
                        pi += 1
                        mm = None
                        for c in range(8):
                            mm = sc.op("tensor", lambda e, b=b, q4=q4, dvh=dvh, c=c, pb=pb: e.matmul(
                                pkv[pb][:], sxnb[b][:, c * 512 + q4 * 128: c * 512 + q4 * 128 + 128],
                                sWT[:, c * 2048 + D + dvh * 512: c * 2048 + D + dvh * 512 + 512],
                                start=(c == 0), stop=(c == 7)),
                                waits=[free_pkv[pb]] if c == 0 else [],
                                inc=s_t if c == 7 else None)
                        tok_ev = sc.op("vector", lambda e, qt=qt, dvh=dvh, pb=pb: e.tensor_copy(
                            V_tm[:, qt * 1024 + dvh * 512: qt * 1024 + (dvh + 1) * 512], pkv[pb][:]),
                            waits=[mm])
                        free_pkv[pb] = tok_ev
                        last_mm = mm
                free_xnb[b] = last_mm
            sc.barrier()
            sc.emit_phase("B_kv")

        with ExitStack() as cc:
            en2 = cc.enter_context
            sxq = [en2(nc.sbuf_tensor(f"{px}sxq{j}", [128, D], BF16)).ap() for j in range(2)]
            sx0 = [en2(nc.sbuf_tensor(f"{px}sx0{j}", [128, D], F32)).ap() for j in range(2)]
            sout = [en2(nc.sbuf_tensor(f"{px}sout{j}", [128, 512], F32)).ap() for j in range(2)]
            sprT = [en2(nc.sbuf_tensor(f"{px}sprT{j}", [128, 128], BF16)).ap() for j in range(2)]
            ssc = en2(nc.sbuf_tensor(px + "ssc", [128, T], F32)).ap()
            sprob = en2(nc.sbuf_tensor(px + "sprob", [128, T], BF16)).ap()
            sneg = en2(nc.sbuf_tensor(px + "sneg", [128, 1], F32)).ap()
            ssume = en2(nc.sbuf_tensor(px + "ssume", [128, 1], F32)).ap()
            srcpo = en2(nc.sbuf_tensor(px + "srcpo", [128, 1], F32)).ap()
            psc = [en2(nc.psum_tensor(f"{px}psc{j}", [128, 512], F32)).ap() for j in range(2)]
            ptr = [en2(nc.psum_tensor(f"{px}ptr{j}", [128, 128], BF16)).ap() for j in range(2)]
            pav = en2(nc.psum_tensor(px + "pav", [128, 1024], F32)).ap()

            s_q = [sc.sem(f"{px}q{j}") for j in range(2)]
            s_x0c = [sc.sem(f"{px}x0c{j}") for j in range(2)]
            s_so = [sc.sem(f"{px}so{j}") for j in range(2)]

            free_xq = [None, None]
            free_x0b = [None, None]
            free_out = [None, None]
            free_psc = [None, None]
            free_ptr = [None, None]
            free_prT = [None, None]
            free_pav = None
            prev_prob = None
            prev_last_tr = None
            for qi in range(TT):
                b = qi % 2
                r0 = qi * 128
                nk = qi + 1
                L = nk * 128
                nch = (L + 511) // 512
                tok_q = sc.op("sync", lambda e, b=b, qi=qi: e.dma_start(
                    out=sxq[b].rearrange("p (c t) -> p c t", c=8),
                    in_=xnq_d[qi].rearrange("c p t -> p c t")),
                    waits=[free_xq[b]], inc=s_q[b], dma=True)
                tok_x0 = sc.op("sync", lambda e, b=b, r0=r0: e.dma_start(out=sx0[b][:], in_=x0xn_d[r0:r0 + 128, :]),
                               waits=[free_x0b[b]], inc=s_x0c[b], dma=True)
                ev_toks = []
                last_score_mm = None
                for ch in range(nch):
                    w = min(512, L - ch * 512)
                    pb = ch % 2
                    mm = None
                    for c in range(8):
                        mm = sc.op("tensor", lambda e, b=b, ch=ch, c=c, pb=pb, w=w: e.matmul(
                            psc[pb][:, 0:w], sxq[b][:, c * 128:(c + 1) * 128],
                            K_fm[:, c * T + ch * 512: c * T + ch * 512 + w],
                            start=(c == 0), stop=(c == 7)),
                            waits=[tok_q, free_psc[pb]] if c == 0 else [],
                            inc=s_t if c == 7 else None)
                    last_score_mm = mm
                    if ch == nch - 1:
                        off = r0 - ch * 512
                        if off > 0:
                            sc.op("vector", lambda e, ch=ch, pb=pb, off=off: e.tensor_copy(
                                ssc[:, ch * 512: ch * 512 + off], psc[pb][:, 0:off]), waits=[mm])
                        tok_ev = sc.op("vector", lambda e, ch=ch, pb=pb, off=off: e.tensor_add(
                            ssc[:, ch * 512 + off: ch * 512 + off + 128], psc[pb][:, off:off + 128], scm[:]),
                            waits=[mm, t_c])
                    else:
                        tok_ev = sc.op("vector", lambda e, ch=ch, pb=pb, w=w: e.tensor_copy(
                            ssc[:, ch * 512: ch * 512 + w], psc[pb][:, 0:w]), waits=[mm])
                    free_psc[pb] = tok_ev
                    ev_toks.append(tok_ev)
                tok_ngs = sc.op("vector", lambda e, L=L: e.reduce_max(sneg[:], ssc[:, 0:L], axis=AX.X, negate=True),
                                waits=[ev_toks[-1]])
                tok_prob = sc.op("scalar", lambda e, L=L: e.activation(out=sprob[:, 0:L], in_=ssc[:, 0:L],
                                                                      func=AF.Exp, bias=sneg[:, 0:1],
                                                                      accum_out=ssume[:]),
                                 waits=[tok_ngs, prev_prob, prev_last_tr])
                prev_prob = tok_prob
                tok_rcpo = sc.op("vector", lambda e: e.reciprocal(srcpo[:], ssume[:]), waits=[tok_prob])
                av_last = None
                tok_tr = None
                for k in range(nk):
                    tb = k % 2
                    tok_tr = sc.op("tensor", lambda e, k=k, tb=tb: e.transpose(ptr[tb][:],
                                                                               sprob[:, k * 128:(k + 1) * 128],
                                                                               sidb[:]),
                                   waits=([tok_prob] if k == 0 else []) + [free_ptr[tb]], inc=s_t)
                    tok_cp = sc.op("vector", lambda e, tb=tb: e.tensor_copy(sprT[tb][:], ptr[tb][:]),
                                   waits=[tok_tr, free_prT[tb]])
                    free_ptr[tb] = tok_cp
                    mm2 = None
                    for half in range(2):
                        mm2 = sc.op("tensor", lambda e, k=k, tb=tb, half=half, nk=nk: e.matmul(
                            pav[:, half * 512:(half + 1) * 512], sprT[tb][:],
                            V_tm[:, k * 1024 + half * 512: k * 1024 + half * 512 + 512],
                            start=(k == 0), stop=(k == nk - 1)),
                            waits=[tok_cp, free_pav] if (k == 0 and half == 0) else [tok_cp],
                            inc=s_t if (k == nk - 1 and half == 1) else None)
                    free_prT[tb] = mm2 if mm2 is not None else free_prT[tb]
                    if k == nk - 1:
                        av_last = mm2
                prev_last_tr = tok_tr
                out_toks = []
                for half in range(2):
                    ob = half
                    tok_o = sc.op("vector", lambda e, b=b, half=half, ob=ob: e.scalar_tensor_tensor(
                        out=sout[ob][:], in0=pav[:, half * 512:(half + 1) * 512], scalar=srcpo[:, 0:1],
                        in1=sx0[b][:, half * 512:(half + 1) * 512], op0=ALU.mult, op1=ALU.add),
                        waits=[av_last, tok_rcpo, tok_x0, free_out[ob]])
                    tok_os = sc.op("sync", lambda e, r0=r0, half=half, ob=ob: e.dma_start(
                        out=x1p_d[r0:r0 + 128, half * 512:(half + 1) * 512], in_=sout[ob][:]),
                        waits=[tok_o], inc=s_so[ob], dma=True)
                    free_out[ob] = tok_os
                    out_toks.append(tok_o)
                free_pav = out_toks
                free_xq[b] = last_score_mm
                free_x0b[b] = out_toks[-1]
            sc.barrier()
            sc.emit_phase("C_attn")


def build_k1(debug=False):
    nc = bass.Bass("TRN2", target_bir_lowering=False)
    x_d = nc.dram_tensor("x", [T, D], F32, kind="ExternalInput").ap()
    wt_d = nc.dram_tensor("wt", [D, 2 * D], F32, kind="ExternalInput").ap()
    gw_d = nc.dram_tensor("gw", [D, E0], F32, kind="ExternalInput").ap()
    gb_d = nc.dram_tensor("gb", [1, E0], F32, kind="ExternalInput").ap()
    idb_d = nc.dram_tensor("idb", [128, 128], BF16, kind="ExternalInput").ap()
    idf_d = nc.dram_tensor("idf", [128, 128], F32, kind="ExternalInput").ap()
    cm_d = nc.dram_tensor("cm", [128, 128], F32, kind="ExternalInput").ap()
    x1p_d = nc.dram_tensor("x1p", [T, D], F32, kind="ExternalOutput").ap()
    xnq_d = nc.dram_tensor("xnq", [TT, 8, 128, 128], BF16, kind="ExternalOutput").ap()
    wmf_d = nc.dram_tensor("wmf", [E0, T], F32, kind="ExternalOutput").ap()
    wmt_d = nc.dram_tensor("wmt", [T, E0], F32, kind="ExternalOutput").ap()
    x0xn_d = nc.dram_tensor("x0xn", [T, D], F32,
                            kind="ExternalOutput" if debug else "Internal").ap()
    sc = Sched(nc)
    emit_k1(nc, sc, "a_", x_d, wt_d, gw_d, gb_d, idb_d, idf_d, cm_d,
            x1p_d, xnq_d, wmf_d, wmt_d, x0xn_d)
    return nc



def emit_k2(nc, sc, px, xnq_d, wmt_d, wgf_d, x1p_d, ew1_d, ew3_d, ew2_d,
            sw1_d, sw3_d, sw2_d, ssel_d, stsel_d, rsel_d, ones_d, idb_d, idf_d,
            x1_d, wfin_d, ew1b_d, ew3b_d, ew2b_d, sw1b_d, sw3b_d, sw2b_d,
            experts=None, dump_acc=False, use_scratch=True, acc_dbg=None,
            wgf_wait=None):
    with ExitStack() as ctx:
        en = ctx.enter_context
        sidb = en(nc.sbuf_tensor(px + "sidb", [128, 128], BF16)).ap()
        sidf = en(nc.sbuf_tensor(px + "sidf", [128, 128], F32)).ap()

        s_t = sc.sem(px + "t")
        s_c = sc.sem(px + "c")
        s_cast = sc.sem(px + "cast")
        s_st = sc.sem(px + "st")

        sc.op("sync", lambda e: e.dma_start(out=sidb[:], in_=idb_d[:]), inc=s_c, dma=True)
        sc.op("sync", lambda e: e.dma_start(out=sidf[:], in_=idf_d[:]), inc=s_c, dma=True)

        if use_scratch:
            for e_i in range(E0):
                for src, dst in ((ew1_d, ew1b_d), (ew3_d, ew3b_d), (ew2_d, ew2b_d)):
                    sc.op("gpsimd", lambda e, s=src, d_=dst, e_i=e_i: e.dma_start(out=d_[e_i], in_=s[e_i]),
                          inc=s_cast, dma=True)
            for s_i in range(2):
                for src, dst in ((sw1_d, sw1b_d), (sw3_d, sw3b_d), (sw2_d, sw2b_d)):
                    sc.op("gpsimd", lambda e, s=src, d_=dst, s_i=s_i: e.dma_start(out=d_[s_i], in_=s[s_i]),
                          inc=s_cast, dma=True)
        tok_cast_all = (s_cast, s_cast.c) if use_scratch else None

        with ExitStack() as cd:
            en2 = cd.enter_context
            swg = en2(nc.sbuf_tensor(px + "swg", [128, T], F32)).ap()
            sgnscr = en2(nc.sbuf_tensor(px + "sgnscr", [128, T], BF16)).ap()
            sS = en2(nc.sbuf_tensor(px + "sS", [128, E0], F32)).ap()
            sST = en2(nc.sbuf_tensor(px + "sST", [E0, 128], F32)).ap()
            sR = en2(nc.sbuf_tensor(px + "sR", [128, E0], F32)).ap()
            sone = en2(nc.sbuf_tensor(px + "sone", [1, 128], F32)).ap()
            sb29 = en2(nc.sbuf_tensor(px + "sb29", [128, 1], F32)).ap()
            ssgn = en2(nc.sbuf_tensor(px + "ssgn", [128, 1], F32)).ap()
            scnt = en2(nc.sbuf_tensor(px + "scnt", [128, 1], F32)).ap()
            scntE = en2(nc.sbuf_tensor(px + "scntE", [E0, 1], F32)).ap()
            stgt16 = en2(nc.sbuf_tensor(px + "stgt16", [E0, 1], F32)).ap()
            slo = en2(nc.sbuf_tensor(px + "slo", [128, 1], F32)).ap()
            shi = en2(nc.sbuf_tensor(px + "shi", [128, 1], F32)).ap()
            stmp = en2(nc.sbuf_tensor(px + "stmp", [128, 1], F32)).ap()
            smid = en2(nc.sbuf_tensor(px + "smid", [128, 1], F32)).ap()
            snmid = en2(nc.sbuf_tensor(px + "snmid", [128, 1], F32)).ap()
            spredE = en2(nc.sbuf_tensor(px + "spredE", [E0, 1], F32)).ap()
            spred = en2(nc.sbuf_tensor(px + "spred", [128, 1], F32)).ap()
            sd1 = en2(nc.sbuf_tensor(px + "sd1", [128, 1], F32)).ap()
            sd2 = en2(nc.sbuf_tensor(px + "sd2", [128, 1], F32)).ap()
            strow = en2(nc.sbuf_tensor(px + "strow", [1, E0], F32)).ap()
            sthr = en2(nc.sbuf_tensor(px + "sthr", [128, E0], F32)).ap()
            swmtt = [en2(nc.sbuf_tensor(f"{px}swmtt{j}", [128, E0], F32)).ap() for j in range(2)]
            sselm = en2(nc.sbuf_tensor(px + "sselm", [128, E0], F32)).ap()
            swfin = [en2(nc.sbuf_tensor(f"{px}swfin{j}", [128, E0], F32)).ap() for j in range(2)]
            swf = en2(nc.sbuf_tensor(px + "swf", [128, T], F32)).ap()
            pcnt = en2(nc.psum_tensor(px + "pcnt", [E0, 1], F32)).ap()
            ppred = en2(nc.psum_tensor(px + "ppred", [128, 1], F32)).ap()
            ptrow = en2(nc.psum_tensor(px + "ptrow", [1, E0], F32)).ap()
            ptbc = en2(nc.psum_tensor(px + "ptbc", [128, E0], F32)).ap()
            pwt = [en2(nc.psum_tensor(f"{px}pwt{j}", [E0, 128], F32)).ap() for j in range(2)]

            s_wm = [sc.sem(f"{px}wm{j}") for j in range(2)]
            s_wf = sc.sem(px + "wfst")

            sc.op("sync", lambda e: e.dma_start(out=swg[:], in_=wgf_d[:]), waits=[wgf_wait], inc=s_c, dma=True)
            sc.op("sync", lambda e: e.dma_start(out=sS[:], in_=ssel_d[:]), inc=s_c, dma=True)
            sc.op("sync", lambda e: e.dma_start(out=sST[:], in_=stsel_d[:]), inc=s_c, dma=True)
            sc.op("sync", lambda e: e.dma_start(out=sR[:], in_=rsel_d[:]), inc=s_c, dma=True)
            sc.op("sync", lambda e: e.dma_start(out=sone[:], in_=ones_d[:]), inc=s_c, dma=True)
            t_c = (s_c, s_c.c)

            sc.op("vector", lambda e: e.memset(sb29[:], 1e29))
            tok_mx0 = None
            for hfi in range(2):
                sc.op("vector", lambda e, hfi=hfi: e.tensor_scalar(
                    out=sgnscr.bitcast(F32)[:, 0:T // 2], in0=swg[:, hfi * (T // 2):(hfi + 1) * (T // 2)],
                    scalar1=-1e29, scalar2=None, op0=ALU.is_gt), waits=[t_c])
                sc.op("vector", lambda e, hfi=hfi: e.tensor_mul(
                    sgnscr.bitcast(F32)[:, 0:T // 2], sgnscr.bitcast(F32)[:, 0:T // 2],
                    swg[:, hfi * (T // 2):(hfi + 1) * (T // 2)]))
                tok_mx0 = sc.op("vector", lambda e, hfi=hfi: e.reduce_max(
                    smid[:] if hfi == 0 else snmid[:], sgnscr.bitcast(F32)[:, 0:T // 2], axis=AX.X,
                    apply_absolute_value=True))
            tok_mx0 = sc.op("vector", lambda e: e.tensor_max(shi[:], smid[:], snmid[:]))
            mmx = sc.op("tensor", lambda e: e.matmul(pcnt[:], sS[:], shi[:], start=True, stop=True),
                        waits=[t_c, tok_mx0], inc=s_t)
            tok_hx = sc.op("vector", lambda e: e.tensor_copy(scntE[:], pcnt[:]), waits=[mmx])
            tok_hadd = sc.op("vector", lambda e: e.tensor_scalar_add(scntE[:], scntE[:], 1.0))
            mmx2 = sc.op("tensor", lambda e: e.matmul(ppred[:], sST[:], scntE[:], start=True, stop=True),
                         waits=[tok_hadd], inc=s_t)
            tok_hb = sc.op("vector", lambda e: e.tensor_copy(shi[:], ppred[:]), waits=[mmx2])
            tok_ms = sc.op("vector", lambda e: e.tensor_scalar_mul(slo[:], shi[:], -1.0))
            tok_sgn0 = sc.op("scalar", lambda e: e.activation(out=sgnscr[:], in_=swg[:], func=AF.Sign,
                                                             bias=sb29[:, 0:1], accum_out=ssgn[:]),
                             waits=[t_c, tok_ms])
            tok_cnt0 = sc.op("vector", lambda e: e.tensor_scalar(out=scnt[:], in0=ssgn[:], scalar1=float(T),
                                                                scalar2=0.5, op0=ALU.add, op1=ALU.mult),
                             waits=[tok_sgn0])
            mmc = sc.op("tensor", lambda e: e.matmul(pcnt[:], sS[:], scnt[:], start=True, stop=True),
                        waits=[t_c, tok_cnt0, tok_hx], inc=s_t)
            tok_cE = sc.op("vector", lambda e: e.tensor_copy(scntE[:], pcnt[:]), waits=[mmc])
            tok_tgt = sc.op("vector", lambda e: e.tensor_scalar_min(stgt16[:], scntE[:], float(CAP)))
            free_pcnt = tok_cE
            tok_upd = tok_tgt
            free_ppred = None
            for it in range(N_BIS):
                sc.op("vector", lambda e: e.tensor_add(stmp[:], slo[:], shi[:]), waits=[tok_upd])
                sc.op("vector", lambda e: e.tensor_scalar_mul(smid[:], stmp[:], 0.5))
                tok_nm = sc.op("vector", lambda e: e.tensor_scalar_mul(snmid[:], stmp[:], -0.5))
                tok_sgn = sc.op("scalar", lambda e: e.activation(out=sgnscr[:], in_=swg[:], func=AF.Sign,
                                                                bias=snmid[:, 0:1], accum_out=ssgn[:]),
                                waits=[tok_nm])
                tok_cnt = sc.op("vector", lambda e: e.tensor_scalar(out=scnt[:], in0=ssgn[:], scalar1=float(T),
                                                                   scalar2=0.5, op0=ALU.add, op1=ALU.mult),
                               waits=[tok_sgn])
                mmc = sc.op("tensor", lambda e: e.matmul(pcnt[:], sS[:], scnt[:], start=True, stop=True),
                            waits=[tok_cnt, free_pcnt], inc=s_t)
                tok_cE = sc.op("vector", lambda e: e.tensor_copy(scntE[:], pcnt[:]), waits=[mmc])
                free_pcnt = tok_cE
                tok_pE = sc.op("vector", lambda e: e.tensor_tensor(spredE[:], scntE[:], stgt16[:], ALU.is_ge))
                mmp = sc.op("tensor", lambda e: e.matmul(ppred[:], sST[:], spredE[:], start=True, stop=True),
                            waits=[tok_pE, free_ppred if free_ppred is not None else tok_hb], inc=s_t)
                tok_pr = sc.op("vector", lambda e: e.tensor_copy(spred[:], ppred[:]), waits=[mmp])
                free_ppred = tok_pr
                sc.op("vector", lambda e: e.tensor_sub(sd1[:], smid[:], slo[:]))
                sc.op("vector", lambda e: e.tensor_mul(sd1[:], spred[:], sd1[:]))
                sc.op("vector", lambda e: e.tensor_add(slo[:], slo[:], sd1[:]))
                sc.op("vector", lambda e: e.tensor_sub(sd2[:], shi[:], smid[:]))
                sc.op("vector", lambda e: e.tensor_mul(sd2[:], spred[:], sd2[:]))
                tok_upd = sc.op("vector", lambda e: e.tensor_add(shi[:], smid[:], sd2[:]))
            mmr = sc.op("tensor", lambda e: e.matmul(ptrow[:], slo[:], sR[:], start=True, stop=True),
                        waits=[tok_upd], inc=s_t)
            tok_tr = sc.op("vector", lambda e: e.tensor_copy(strow[:], ptrow[:]), waits=[mmr])
            mmb = sc.op("tensor", lambda e: e.matmul(ptbc[:], sone[:], strow[:], start=True, stop=True),
                        waits=[tok_tr], inc=s_t)
            tok_thr = sc.op("vector", lambda e: e.tensor_copy(sthr[:], ptbc[:]), waits=[mmb])
            free_wmtt = [None, None]
            free_wfin = [None, None]
            free_pwt = [None, None]
            last_wf = None
            for i in range(TT):
                b = i % 2
                tok_w = sc.op("sync", lambda e, b=b, i=i: e.dma_start(out=swmtt[b][:],
                                                                     in_=wmt_d[i * 128:(i + 1) * 128, :]),
                              waits=[free_wmtt[b]], inc=s_wm[b], dma=True)
                sc.op("vector", lambda e, b=b: e.tensor_tensor(sselm[:], swmtt[b][:], sthr[:], ALU.is_ge),
                      waits=[tok_w, tok_thr])
                tok_wf = sc.op("vector", lambda e, b=b: e.tensor_mul(swfin[b][:], sselm[:], swmtt[b][:]),
                               waits=[free_wfin[b]])
                free_wmtt[b] = tok_wf
                tok_T = sc.op("tensor", lambda e, b=b: e.transpose(pwt[b][:], swfin[b][:], sidf[:]),
                              waits=[tok_wf, t_c, free_pwt[b]], inc=s_t)
                free_wfin[b] = tok_T
                tok_cp = sc.op("vector", lambda e, i=i, b=b: e.tensor_copy(swf[0:E0, i * 128:(i + 1) * 128],
                                                                           pwt[b][:]),
                               waits=[tok_T])
                free_pwt[b] = tok_cp
                last_wf = tok_cp
            sc.op("sync", lambda e: e.dma_start(out=wfin_d[:], in_=swf[0:E0, :]), waits=[last_wf],
                  inc=s_wf, dma=True)
            sc.barrier()
            sc.emit_phase("D_bisect")

        with ExitStack() as cf:
            en2 = cf.enter_context
            sacc = en2(nc.sbuf_tensor(px + "sacc", [128, 8 * T], BF16)).ap()
            cf2 = en2(ExitStack())
            en3 = cf2.enter_context
            sw1s = en3(nc.sbuf_tensor(px + "sw1s", [128, 8 * H0], BF16)).ap()
            sw3s = en3(nc.sbuf_tensor(px + "sw3s", [128, 8 * H0], BF16)).ap()
            sw2s = en3(nc.sbuf_tensor(px + "sw2s", [128, 16 * D], BF16)).ap()
            sxnb = [en3(nc.sbuf_tensor(f"{px}sxnb{j}", [128, 8 * 512], BF16)).ap() for j in range(2)]
            sbc = [en3(nc.sbuf_tensor(f"{px}sbc{j}", [128, 512], F32)).ap() for j in range(2)]
            sbc05 = en3(nc.sbuf_tensor(px + "sbc05", [128, 512], F32)).ap()
            sG = en3(nc.sbuf_tensor(px + "sG", [128, 16 * 512], BF16)).ap()
            sg1 = en3(nc.sbuf_tensor(px + "sg1", [128, 512], BF16)).ap()
            sg3 = en3(nc.sbuf_tensor(px + "sg3", [128, 512], BF16)).ap()
            ph = [en3(nc.psum_tensor(f"{px}ph{j}", [128, 512], F32)).ap() for j in range(2)]
            po = [en3(nc.psum_tensor(f"{px}po{j}", [128, 512], F32)).ap() for j in range(2)]

            s_nb = [sc.sem(f"{px}nb{j}") for j in range(2)]
            s_bc = [sc.sem(f"{px}bc{j}") for j in range(2)]
            s_w1 = sc.sem(px + "w1")
            s_w3 = sc.sem(px + "w3")
            s_w2 = sc.sem(px + "w2")

            tok_acc0 = sc.op("vector", lambda e: e.memset(sacc[:], 0.0))
            sc.op("vector", lambda e: e.memset(sbc05[:], 0.5))
            free_xnb = [None, None]
            free_bc = [None, None]
            free_ph = [None, None]
            free_po = [None, None]
            free_w = None
            acc_tok = tok_acc0
            phi = 0
            expert_list = list(range(E0 + 2)) if experts is None else experts
            for ei in expert_list:
                routed = ei < E0
                if use_scratch:
                    w1src = ew1b_d[ei] if routed else sw1b_d[ei - E0]
                    w3src = ew3b_d[ei] if routed else sw3b_d[ei - E0]
                    w2src = ew2b_d[ei] if routed else sw2b_d[ei - E0]
                else:
                    w1src = ew1_d[ei] if routed else sw1_d[ei - E0]
                    w3src = ew3_d[ei] if routed else sw3_d[ei - E0]
                    w2src = ew2_d[ei] if routed else sw2_d[ei - E0]
                t_w1 = sc.op("gpsimd", lambda e, s=w1src: e.dma_start(
                    out=sw1s.rearrange("p (c h) -> p c h", c=8), in_=s.rearrange("(c p) h -> p c h", p=128)),
                    waits=[tok_cast_all, free_w], inc=s_w1, dma=True)
                t_w3 = sc.op("gpsimd", lambda e, s=w3src: e.dma_start(
                    out=sw3s.rearrange("p (c h) -> p c h", c=8), in_=s.rearrange("(c p) h -> p c h", p=128)),
                    inc=s_w3, dma=True)
                t_w2 = sc.op("gpsimd", lambda e, s=w2src: e.dma_start(
                    out=sw2s.rearrange("p (h o) -> p h o", h=16), in_=s.rearrange("(h p) o -> p h o", p=128)),
                    inc=s_w2, dma=True)
                last_mm_e = None
                for tc in range(8):
                    bx = tc % 2
                    tok_xn = None
                    for q4 in range(4):
                        tok_xn = sc.op("sync", lambda e, bx=bx, tc=tc, q4=q4: e.dma_start(
                            out=sxnb[bx].rearrange("p (c q t) -> p c q t", c=8, q=4)[:, :, q4, :],
                            in_=xnq_d[tc * 4 + q4].rearrange("c p t -> p c t")),
                            waits=[free_xnb[bx]] if q4 == 0 else [], inc=s_nb[bx], dma=True)
                    if routed:
                        tok_bc = sc.op("sync", lambda e, bx=bx, ei=ei, tc=tc: e.dma_start(
                            out=sbc[bx][:],
                            in_=wfin_d[ei:ei + 1, tc * 512:(tc + 1) * 512].broadcast_to([128, 512])),
                            waits=[free_bc[bx]], inc=s_bc[bx], dma=True)
                        bc_ap = sbc[bx]
                    else:
                        tok_bc = None
                        bc_ap = sbc05
                    g_toks = []
                    prev_G = None
                    for h in range(16):
                        pb = phi % 2
                        phi += 1
                        mm1 = None
                        for c in range(8):
                            mm1 = sc.op("tensor", lambda e, h=h, c=c, pb=pb, bx=bx: e.matmul(
                                ph[pb][:], sw1s[:, c * H0 + h * 128: c * H0 + h * 128 + 128],
                                sxnb[bx][:, c * 512:(c + 1) * 512], start=(c == 0), stop=(c == 7)),
                                waits=[t_w1, tok_xn, free_ph[pb]] if c == 0 else [],
                                inc=s_t if c == 7 else None)
                        tok_g1 = sc.op("scalar", lambda e, pb=pb: e.activation(out=sg1[:], in_=ph[pb][:],
                                                                              func=AF.Silu),
                                       waits=[mm1, prev_G])
                        free_ph[pb] = tok_g1
                        pb3 = phi % 2
                        phi += 1
                        mm3 = None
                        for c in range(8):
                            mm3 = sc.op("tensor", lambda e, h=h, c=c, pb3=pb3, bx=bx: e.matmul(
                                ph[pb3][:], sw3s[:, c * H0 + h * 128: c * H0 + h * 128 + 128],
                                sxnb[bx][:, c * 512:(c + 1) * 512], start=(c == 0), stop=(c == 7)),
                                waits=[t_w3, free_ph[pb3]] if c == 0 else [],
                                inc=s_t if c == 7 else None)
                        tok_g3 = sc.op("vector", lambda e, pb3=pb3, bc_ap=bc_ap: e.tensor_tensor(
                            sg3[:], ph[pb3][:], bc_ap[:], ALU.mult),
                            waits=[mm3, tok_bc])
                        free_ph[pb3] = tok_g3
                        tok_G = sc.op("vector", lambda e, h=h: e.tensor_mul(sG[:, h * 512:(h + 1) * 512],
                                                                            sg1[:], sg3[:]),
                                      waits=[tok_g1])
                        prev_G = tok_G
                        g_toks.append(tok_G)
                    for oc in range(8):
                        pb = phi % 2
                        phi += 1
                        mm2 = None
                        for h in range(16):
                            mm2 = sc.op("tensor", lambda e, oc=oc, h=h, pb=pb: e.matmul(
                                po[pb][:], sw2s[:, h * D + oc * 128: h * D + oc * 128 + 128],
                                sG[:, h * 512:(h + 1) * 512], start=(h == 0), stop=(h == 15)),
                                waits=[t_w2, g_toks[-1], free_po[pb]] if h == 0 else [],
                                inc=s_t if h == 15 else None)
                        acc_tok = sc.op("vector", lambda e, oc=oc, tc=tc, pb=pb: e.tensor_tensor(
                            sacc[:, oc * T + tc * 512: oc * T + (tc + 1) * 512],
                            sacc[:, oc * T + tc * 512: oc * T + (tc + 1) * 512], po[pb][:], ALU.add),
                            waits=[mm2, acc_tok])
                        free_po[pb] = acc_tok
                        last_mm_e = mm2
                    free_xnb[bx] = last_mm_e
                    free_bc[bx] = acc_tok
                free_w = last_mm_e
            if dump_acc:
                sc.op("sync", lambda e: e.dma_start(out=acc_dbg[:], in_=sacc[:]), waits=[acc_tok],
                      inc=s_st, dma=True)
            sc.barrier()
            sc.emit_phase("F_experts")
            cf2.close()

            with ExitStack() as cg:
                en4 = cg.enter_context
                sx1p = [en4(nc.sbuf_tensor(f"{px}sx1p{j}", [128, D], F32)).ap() for j in range(2)]
                sx1o = [en4(nc.sbuf_tensor(f"{px}sx1o{j}", [128, D], F32)).ap() for j in range(2)]
                pacc = [en4(nc.psum_tensor(f"{px}pacc{j}", [128, 128], BF16)).ap() for j in range(2)]
                s_xp = [sc.sem(f"{px}xp{j}") for j in range(2)]
                s_xo = [sc.sem(f"{px}xo{j}") for j in range(2)]
                free_p = [None, None]
                free_o = [None, None]
                free_i = [None, None]
                for i in range(TT):
                    b = i % 2
                    tok_x1p = sc.op("sync", lambda e, b=b, i=i: e.dma_start(out=sx1p[b][:],
                                                                           in_=x1p_d[i * 128:(i + 1) * 128, :]),
                                    waits=[free_i[b]], inc=s_xp[b], dma=True)
                    add_toks = []
                    for c in range(8):
                        pb = c % 2
                        tok_T = sc.op("tensor", lambda e, i=i, c=c, pb=pb: e.transpose(
                            pacc[pb][:], sacc[:, c * T + i * 128: c * T + i * 128 + 128], sidb[:]),
                            waits=[(s_c, 32), free_p[pb]] + ([acc_tok] if i == 0 and c == 0 else []), inc=s_t)
                        tok_add = sc.op("vector", lambda e, b=b, c=c, pb=pb: e.tensor_tensor(
                            sx1o[b][:, c * 128:(c + 1) * 128], sx1p[b][:, c * 128:(c + 1) * 128],
                            pacc[pb][:], ALU.add),
                            waits=[tok_T, tok_x1p] + ([free_o[b]] if c == 0 else []))
                        free_p[pb] = tok_add
                        add_toks.append(tok_add)
                    tok_st2 = sc.op("sync", lambda e, b=b, i=i: e.dma_start(out=x1_d[i * 128:(i + 1) * 128, :],
                                                                           in_=sx1o[b][:]),
                                    waits=[add_toks[-1]], inc=s_xo[b], dma=True)
                    free_o[b] = tok_st2
                    free_i[b] = add_toks[-1]
                sc.barrier()
                sc.emit_phase("G_final")


def build_k2(experts=None, dump_acc=False, use_scratch=True):
    nc = bass.Bass("TRN2", target_bir_lowering=False)
    xnq_d = nc.dram_tensor("xnq", [TT, 8, 128, 128], BF16, kind="ExternalInput").ap()
    wmt_d = nc.dram_tensor("wmtd", [T, E0], F32, kind="ExternalInput").ap()
    wgf_d = nc.dram_tensor("wgf", [128, T], F32, kind="ExternalInput").ap()
    x1p_d = nc.dram_tensor("x1p", [T, D], F32, kind="ExternalInput").ap()
    ew1_d = nc.dram_tensor("ew1", [E0, D, H0], F32, kind="ExternalInput").ap()
    ew3_d = nc.dram_tensor("ew3", [E0, D, H0], F32, kind="ExternalInput").ap()
    ew2_d = nc.dram_tensor("ew2", [E0, H0, D], F32, kind="ExternalInput").ap()
    sw1_d = nc.dram_tensor("sw1", [2, D, H0], F32, kind="ExternalInput").ap()
    sw3_d = nc.dram_tensor("sw3", [2, D, H0], F32, kind="ExternalInput").ap()
    sw2_d = nc.dram_tensor("sw2", [2, H0, D], F32, kind="ExternalInput").ap()
    ssel_d = nc.dram_tensor("Ssel", [128, E0], F32, kind="ExternalInput").ap()
    stsel_d = nc.dram_tensor("STsel", [E0, 128], F32, kind="ExternalInput").ap()
    rsel_d = nc.dram_tensor("Rsel", [128, E0], F32, kind="ExternalInput").ap()
    ones_d = nc.dram_tensor("ones_f", [1, 128], F32, kind="ExternalInput").ap()
    idb_d = nc.dram_tensor("idb", [128, 128], BF16, kind="ExternalInput").ap()
    idf_d = nc.dram_tensor("idf", [128, 128], F32, kind="ExternalInput").ap()
    x1_d = nc.dram_tensor("x1", [T, D], F32, kind="ExternalOutput").ap()
    wfin_d = nc.dram_tensor("wfin", [E0, T], F32, kind="ExternalOutput").ap()
    acc_dbg = nc.dram_tensor("accdbg", [128, 8 * T], BF16, kind="ExternalOutput").ap() if dump_acc else None
    ew1b_d = nc.dram_tensor("ew1b", [E0, D, H0], BF16, kind="Internal").ap()
    ew3b_d = nc.dram_tensor("ew3b", [E0, D, H0], BF16, kind="Internal").ap()
    ew2b_d = nc.dram_tensor("ew2b", [E0, H0, D], BF16, kind="Internal").ap()
    sw1b_d = nc.dram_tensor("sw1b", [2, D, H0], BF16, kind="Internal").ap()
    sw3b_d = nc.dram_tensor("sw3b", [2, D, H0], BF16, kind="Internal").ap()
    sw2b_d = nc.dram_tensor("sw2b", [2, H0, D], BF16, kind="Internal").ap()
    sc = Sched(nc)
    emit_k2(nc, sc, "b_", xnq_d, wmt_d, wgf_d, x1p_d, ew1_d, ew3_d, ew2_d,
            sw1_d, sw3_d, sw2_d, ssel_d, stsel_d, rsel_d, ones_d, idb_d, idf_d,
            x1_d, wfin_d, ew1b_d, ew3b_d, ew2b_d, sw1b_d, sw3b_d, sw2b_d,
            experts=experts, dump_acc=dump_acc, use_scratch=use_scratch,
            acc_dbg=acc_dbg)
    return nc


def k2_inputs(inputs, res1):
    idb, idf, cm = _host_consts()
    S = np.zeros((128, E0), np.float32)
    for r in range(NCORE):
        for e_i in range(E0):
            S[r * E0 + e_i, e_i] = 1.0
    ST = np.ascontiguousarray(S.T)
    ones_f = np.ones((1, 128), np.float32)
    wgf = np.concatenate([np.asarray(res1.results[b]["wmf"]) for b in range(NCORE)], axis=0)
    maps = []
    for b in range(NCORE):
        R = np.zeros((128, E0), np.float32)
        for e_i in range(E0):
            R[b * E0 + e_i, e_i] = 1.0
        maps.append({
            "xnq": np.asarray(res1.results[b]["xnq"]),
            "wmtd": np.asarray(res1.results[b]["wmt"]),
            "wgf": wgf,
            "x1p": np.asarray(res1.results[b]["x1p"]),
            "ew1": inputs["L0_ffn_ew1"], "ew3": inputs["L0_ffn_ew3"], "ew2": inputs["L0_ffn_ew2"],
            "sw1": inputs["L0_ffn_sw1"], "sw3": inputs["L0_ffn_sw3"], "sw2": inputs["L0_ffn_sw2"],
            "Ssel": S, "STsel": ST, "Rsel": R, "ones_f": ones_f, "idb": idb, "idf": idf,
        })
    return maps



JTOT = 1367
JPAD = 1408
NH = 16
DH = 64
SCALE = float(D) ** -0.5


def _lq(qi):
    win0 = (qi * 128) // 3 + 2
    return min(JTOT, win0 + 64), win0


def emit_k3(nc, sc, px, x1_d, wr_d, wqT_d, wcp_d, bcv_d, nlk_d, nlv_d,
            woT_d, bo_d, cml_d, idb_d, idf_d, x2p_d, xnq1_d, rsc_d, qq_d, attn_d):
    with ExitStack() as ctx:
        en = ctx.enter_context
        sidb = en(nc.sbuf_tensor(px + "sidb", [128, 128], BF16)).ap()
        sidf = en(nc.sbuf_tensor(px + "sidf", [128, 128], F32)).ap()
        kc_fm = en(nc.sbuf_tensor(px + "kc_fm", [128, 8 * JPAD], BF16)).ap()
        vc_tm = en(nc.sbuf_tensor(px + "vc_tm", [128, 11 * 1024], BF16)).ap()
        swcp = en(nc.sbuf_tensor(px + "swcp", [128, 3 * 8 * DH], BF16)).ap()
        sbcv = en(nc.sbuf_tensor(px + "sbcv", [128, 8], F32)).ap()
        snlk = en(nc.sbuf_tensor(px + "snlk", [128, 8], BF16)).ap()
        snlv = en(nc.sbuf_tensor(px + "snlv", [128, 8], BF16)).ap()

        s_t = sc.sem(px + "t")
        s_c = sc.sem(px + "c")

        sc.op("sync", lambda e: e.dma_start(out=sidb[:], in_=idb_d[:]), inc=s_c, dma=True)
        sc.op("sync", lambda e: e.dma_start(out=sidf[:], in_=idf_d[:]), inc=s_c, dma=True)
        sc.op("sync", lambda e: e.dma_start(out=sbcv[:], in_=bcv_d[:]), inc=s_c, dma=True)
        sc.op("gpsimd", lambda e: e.dma_start(out=swcp[:], in_=wcp_d[:]), inc=s_c, dma=True)
        sc.op("gpsimd", lambda e: e.dma_start(out=snlk[:], in_=nlk_d[:]), inc=s_c, dma=True)
        sc.op("gpsimd", lambda e: e.dma_start(out=snlv[:], in_=nlv_d[:]), inc=s_c, dma=True)
        t_c = (s_c, s_c.c)

        with ExitStack() as ch:
            en2 = ch.enter_context
            sx = [en2(nc.sbuf_tensor(f"{px}sx{j}", [128, D], F32)).ap() for j in range(2)]
            sxh = [en2(nc.sbuf_tensor(f"{px}sxh{j}", [128, D], BF16)).ap() for j in range(2)]
            stg = [en2(nc.sbuf_tensor(f"{px}stg{j}", [128, D], BF16)).ap() for j in range(2)]
            srt = [en2(nc.sbuf_tensor(f"{px}srt{j}", [128, E1], F32)).ap() for j in range(2)]
            swr = en2(nc.sbuf_tensor(px + "swr", [128, 8 * E1], BF16)).ap()
            srf = en2(nc.sbuf_tensor(px + "srf", [128, T], F32)).ap()
            sxn = en2(nc.sbuf_tensor(px + "sxn", [128, D], F32)).ap()
            sqscr = en2(nc.sbuf_tensor(px + "sqscr", [128, D], F32)).ap()
            seps = en2(nc.sbuf_tensor(px + "seps", [128, 1], F32)).ap()
            ssum = en2(nc.sbuf_tensor(px + "ssum", [128, 1], F32)).ap()
            ssumsq = en2(nc.sbuf_tensor(px + "ssumsq", [128, 1], F32)).ap()
            smean = en2(nc.sbuf_tensor(px + "smean", [128, 1], F32)).ap()
            sm2t = en2(nc.sbuf_tensor(px + "sm2t", [128, 1], F32)).ap()
            svar = en2(nc.sbuf_tensor(px + "svar", [128, 1], F32)).ap()
            srstd = en2(nc.sbuf_tensor(px + "srstd", [128, 1], F32)).ap()
            slog = en2(nc.sbuf_tensor(px + "slog", [128, E1], F32)).ap()
            smr = en2(nc.sbuf_tensor(px + "smr", [128, 1], F32)).ap()
            sexp = en2(nc.sbuf_tensor(px + "sexp", [128, E1], F32)).ap()
            sesum = en2(nc.sbuf_tensor(px + "sesum", [128, 1], F32)).ap()
            srcp = en2(nc.sbuf_tensor(px + "srcp", [128, 1], F32)).ap()
            seqr = en2(nc.sbuf_tensor(px + "seqr", [128, E1], F32)).ap()
            ptp = [en2(nc.psum_tensor(f"{px}ptp{j}", [128, 128], BF16)).ap() for j in range(2)]
            plog = en2(nc.psum_tensor(px + "plog", [128, E1], F32)).ap()
            pwm = en2(nc.psum_tensor(px + "pwm", [E1, 128], F32)).ap()

            s_x = [sc.sem(f"{px}x{j}") for j in range(2)]
            s_sq = [sc.sem(f"{px}sq{j}") for j in range(2)]
            s_rf = sc.sem(px + "rf")
            s_wrl = sc.sem(px + "wrl")

            t_wr = sc.op("gpsimd", lambda e: e.dma_start(out=swr.rearrange("p (c e) -> p c e", c=8),
                                                         in_=wr_d.rearrange("(c p) e -> p c e", p=128)),
                         inc=s_wrl, dma=True)
            t_eps = sc.op("vector", lambda e: e.memset(seps[:], 1e-5))
            free_sx = [None, None]
            free_xh = [None, None]
            free_stg = [None, None]
            free_rt = [None, None]
            free_ptp = [None, None]
            free_plog = None
            free_pwm = None
            last_rf = None
            prev_var_read = None
            prev_xn = None
            for i in range(TT):
                b = i % 2
                r0 = i * 128
                tok_x = sc.op("sync", lambda e, b=b, r0=r0: e.dma_start(out=sx[b][:], in_=x1_d[r0:r0 + 128, :]),
                              waits=[free_sx[b]], inc=s_x[b], dma=True)
                sc.op("vector", lambda e, b=b: e.reduce_sum(ssum[:], sx[b][:], axis=AX.X), waits=[tok_x])
                tok_sq = sc.op("scalar", lambda e, b=b: e.activation(out=sqscr[:], in_=sx[b][:], func=AF.Square,
                                                                    accum_out=ssumsq[:]),
                               waits=[tok_x, prev_var_read])
                tok_mean = sc.op("vector", lambda e: e.tensor_scalar_mul(smean[:], ssum[:], 1.0 / D))
                sc.op("vector", lambda e: e.tensor_mul(sm2t[:], smean[:], smean[:]))
                prev_var_read = sc.op("vector", lambda e: e.tensor_scalar_mul(svar[:], ssumsq[:], 1.0 / D),
                                      waits=[tok_sq])
                tok_var = sc.op("vector", lambda e: e.tensor_sub(svar[:], svar[:], sm2t[:]))
                sc.op("scalar", lambda e: e.activation(out=srstd[:], in_=svar[:], func=AF.Ln, bias=seps[:, 0:1]),
                      waits=[tok_var, t_eps, prev_xn])
                tok_rstd = sc.op("scalar", lambda e: e.activation(out=srstd[:], in_=srstd[:], func=AF.Exp, scale=-0.5))
                tok_xn = sc.op("vector", lambda e, b=b: e.tensor_scalar(out=sxn[:], in0=sx[b][:],
                                                                       scalar1=smean[:, 0:1],
                                                                       scalar2=srstd[:, 0:1], op0=ALU.subtract,
                                                                       op1=ALU.mult),
                               waits=[tok_mean, tok_rstd])
                prev_xn = tok_xn
                free_sx[b] = tok_xn
                tok_xh = sc.op("vector", lambda e, b=b: e.tensor_copy(sxh[b][:], sxn[:]), waits=[free_xh[b]])
                cp_toks = []
                wstg = free_stg[b]
                tok_tr = None
                for c in range(8):
                    tok_tr = sc.op("tensor", lambda e, b=b, c=c: e.transpose(ptp[c % 2][:],
                                                                             sxh[b][:, c * 128:(c + 1) * 128],
                                                                             sidb[:]),
                                   waits=[tok_xh, t_c, free_ptp[c % 2]], inc=s_t)
                    tok_cp = sc.op("vector", lambda e, b=b, c=c: e.tensor_copy(stg[b][:, c * 128:(c + 1) * 128],
                                                                               ptp[c % 2][:]),
                                   waits=[tok_tr, wstg])
                    free_ptp[c % 2] = tok_cp
                    cp_toks.append(tok_cp)
                free_xh[b] = tok_tr
                tok_stg = cp_toks[-1]
                tok_xnq = sc.op("sync", lambda e, b=b, i=i: e.dma_start(
                    out=xnq1_d[i].rearrange("c p t -> p c t"),
                    in_=stg[b].rearrange("p (c t) -> p c t", c=8)),
                    waits=[tok_stg], inc=s_sq[b], dma=True)
                mm_tok = None
                for c in range(8):
                    mm_tok = sc.op("tensor", lambda e, b=b, c=c: e.matmul(plog[:], stg[b][:, c * 128:(c + 1) * 128],
                                                                          swr[:, c * E1:(c + 1) * E1],
                                                                          start=(c == 0), stop=(c == 7)),
                                   waits=[tok_stg, t_wr, free_plog] if c == 0 else [],
                                   inc=s_t if c == 7 else None)
                free_stg[b] = [tok_xnq, mm_tok]
                tok_logcp = sc.op("vector", lambda e: e.tensor_copy(slog[:], plog[:]), waits=[mm_tok])
                free_plog = tok_logcp
                tok_mr = sc.op("vector", lambda e: e.reduce_max(smr[:], slog[:], axis=AX.X, negate=True))
                tok_exp = sc.op("scalar", lambda e: e.activation(out=sexp[:], in_=slog[:], func=AF.Exp,
                                                                bias=smr[:, 0:1], accum_out=sesum[:]),
                                waits=[tok_mr, tok_logcp])
                tok_rcp = sc.op("vector", lambda e: e.reciprocal(srcp[:], sesum[:]), waits=[tok_exp])
                sc.op("vector", lambda e: e.tensor_scalar(out=seqr[:], in0=sexp[:], scalar1=1.0, scalar2=None,
                                                          op0=ALU.is_ge))
                tok_rt = sc.op("vector", lambda e, b=b: e.tensor_scalar(out=srt[b][:], in0=seqr[:],
                                                                       scalar1=srcp[:, 0:1], scalar2=None,
                                                                       op0=ALU.mult),
                               waits=[tok_rcp, free_rt[b]])
                tok_rT = sc.op("tensor", lambda e, b=b: e.transpose(pwm[:], srt[b][:], sidf[:]),
                               waits=[tok_rt, t_c, free_pwm], inc=s_t)
                free_rt[b] = tok_rT
                tok_rcpc = sc.op("vector", lambda e, i=i: e.tensor_copy(srf[0:E1, i * 128:(i + 1) * 128], pwm[:]),
                                 waits=[tok_rT])
                free_pwm = tok_rcpc
                last_rf = tok_rcpc
            sc.op("sync", lambda e: e.dma_start(out=rsc_d[:], in_=srf[0:E1, :]), waits=[last_rf],
                  inc=s_rf, dma=True)
            sc.barrier()
            sc.emit_phase("H_ln1")

        for pno, part in enumerate(("q", "k", "v")):
            with ExitStack() as ci:
                en2 = ci.enter_context
                swq = en2(nc.sbuf_tensor(f"{px}swq_{part}", [128, 8 * D], BF16)).ap()
                sxnb = [en2(nc.sbuf_tensor(f"{px}sxnb_{part}{j}", [128, 8 * 512], BF16)).ap() for j in range(2)]
                sev = [en2(nc.sbuf_tensor(f"{px}sev_{part}{j}", [128, 512], BF16)).ap() for j in range(2)]
                pkv = [en2(nc.psum_tensor(f"{px}pkv_{part}{j}", [128, 512], F32)).ap() for j in range(2)]
                if part != "q":
                    kvf = en2(nc.sbuf_tensor(f"{px}kvf_{part}", [128, 8 * 4104], BF16)).ap()

                s_nb = [sc.sem(f"{px}nb_{part}{j}") for j in range(2)]
                s_ev = [sc.sem(f"{px}ev_{part}{j}") for j in range(2)]
                s_wq = sc.sem(f"{px}wq_{part}")
                t_wq = sc.op("gpsimd", lambda e, pno=pno: e.dma_start(
                    out=swq.rearrange("p (c k) -> p c k", c=8),
                    in_=wqT_d[:, pno * D:(pno + 1) * D].rearrange("(c p) k -> p c k", p=128)),
                    inc=s_wq, dma=True)
                if part != "q":
                    tok_pad = sc.op("vector", lambda e: e.memset(kvf[:], 0.0))
                free_xnb = [None, None]
                free_ev = [None, None]
                free_pkv = [None, None]
                pi = 0
                last_proj_v = None
                for tc in range(8):
                    b = tc % 2
                    tok_xn = None
                    for q4 in range(4):
                        tok_xn = sc.op("sync", lambda e, b=b, tc=tc, q4=q4: e.dma_start(
                            out=sxnb[b].rearrange("p (c q t) -> p c q t", c=8, q=4)[:, :, q4, :],
                            in_=xnq1_d[tc * 4 + q4].rearrange("c p t -> p c t")),
                            waits=[free_xnb[b]] if q4 == 0 else [], inc=s_nb[b], dma=True)
                    last_mm = None
                    for kk in range(8):
                        pb = pi % 2
                        pi += 1
                        mm = None
                        for c in range(8):
                            mm = sc.op("tensor", lambda e, b=b, kk=kk, c=c, pb=pb: e.matmul(
                                pkv[pb][:], swq[:, c * D + kk * 128: c * D + kk * 128 + 128],
                                sxnb[b][:, c * 512:(c + 1) * 512], start=(c == 0), stop=(c == 7)),
                                waits=[tok_xn, t_wq, free_pkv[pb]] if c == 0 else [],
                                inc=s_t if c == 7 else None)
                        last_mm = mm
                        if part == "q":
                            eb = pb
                            tok_e = sc.op("vector", lambda e, pb=pb, eb=eb: e.tensor_copy(sev[eb][:], pkv[pb][:]),
                                          waits=[mm, free_ev[eb]])
                            free_pkv[pb] = tok_e
                            tok_st = sc.op("sync", lambda e, tc=tc, kk=kk, eb=eb: e.dma_start(
                                out=qq_d[tc * 4:(tc + 1) * 4, kk].rearrange("q p t -> p q t"),
                                in_=sev[eb].rearrange("p (q t) -> p q t", q=4)),
                                waits=[tok_e], inc=s_ev[eb], dma=True)
                            free_ev[eb] = tok_st
                        else:
                            tok_e = sc.op("vector", lambda e, kk=kk, tc=tc, pb=pb: e.tensor_copy(
                                kvf[:, kk * 4104 + 2 + tc * 512: kk * 4104 + 2 + (tc + 1) * 512], pkv[pb][:]),
                                waits=[mm] + ([tok_pad] if tc == 0 and kk == 0 else []))
                            free_pkv[pb] = tok_e
                            last_proj_v = tok_e
                    free_xnb[b] = last_mm
                if part != "q":
                    with ExitStack() as cj:
                        en3 = cj.enter_context
                        svt = en3(nc.sbuf_tensor(f"{px}svt_{part}", [128, 128], BF16)).ap()
                        pcm = [en3(nc.psum_tensor(f"{px}pcm_{part}{j}", [64, 128], F32)).ap() for j in range(2)]
                        ptv = [en3(nc.psum_tensor(f"{px}ptv_{part}{j}", [128, 64], BF16)).ap() for j in range(2)]
                        free_pcm = [None, None]
                        free_ptv = [None, None]
                        free_svt = None
                        if part == "k":
                            tok_nl = sc.op("vector", lambda e: e.tensor_copy(
                                kc_fm.rearrange("p (cc j) -> p cc j", cc=8)[:, :, 0:1].squeeze(2), snlk[:]),
                                waits=[t_c])
                        else:
                            tok_nl = None
                        cidx = 0
                        for h in range(NH):
                            base = (h % 2) * 64
                            hp = h // 2
                            for jc in range(11):
                                jlo = jc * 128 - 1
                                wfull = min(128, 1366 - jlo)
                                o0 = 1 if jc == 0 else 0
                                jj0 = max(jlo, 0)
                                wv = wfull - o0
                                pb = cidx % 2
                                mmc_ = None
                                for tap in range(3):
                                    col0 = 3 * jj0 + tap
                                    mmc_ = sc.op("tensor", lambda e, h=h, hp=hp, base=base, tap=tap, jj0=jj0,
                                                 wv=wv, o0=o0, pb=pb, col0=col0: e.matmul(
                                        pcm[pb][:, o0:o0 + wv],
                                        swcp[base:base + 64, tap * 512 + hp * 64: tap * 512 + hp * 64 + 64],
                                        kvf[base:base + 64, hp * 4104 + col0: hp * 4104 + col0 + 3 * wv: 3],
                                        start=(tap == 0), stop=(tap == 2)),
                                        waits=[last_proj_v, t_c, free_pcm[pb]] if tap == 0 else [],
                                        inc=s_t if tap == 2 else None)
                                if part == "k":
                                    tok_ev2 = sc.op("vector", lambda e, h=h, hp=hp, base=base, jc=jc, o0=o0,
                                                    wv=wv, pb=pb: e.tensor_scalar(
                                        out=kc_fm[base:base + 64,
                                                  hp * JPAD + jc * 128 + o0: hp * JPAD + jc * 128 + o0 + wv],
                                        in0=pcm[pb][:, o0:o0 + wv],
                                        scalar1=sbcv[base:base + 64, hp:hp + 1], scalar2=None, op0=ALU.add),
                                        waits=[mmc_])
                                    free_pcm[pb] = tok_ev2
                                else:
                                    tok_b = sc.op("vector", lambda e, base=base, hp=hp, o0=o0, wv=wv, pb=pb: e.tensor_scalar(
                                        out=svt[base:base + 64, o0:o0 + wv], in0=pcm[pb][:, o0:o0 + wv],
                                        scalar1=sbcv[base:base + 64, hp:hp + 1], scalar2=None, op0=ALU.add),
                                        waits=[mmc_, free_svt])
                                    free_pcm[pb] = tok_b
                                    if jc == 0:
                                        tok_b = sc.op("vector", lambda e, base=base, hp=hp: e.tensor_copy(
                                            svt[base:base + 64, 0:1], snlv[base:base + 64, hp:hp + 1]),
                                            waits=[tok_b, t_c])
                                    tb = cidx % 2
                                    wtot = o0 + wv if jc == 0 else wv
                                    tok_T = sc.op("tensor", lambda e, base=base, tb=tb, wtot=wtot: e.transpose(
                                        ptv[tb][0:wtot, :], svt[base:base + 64, 0:wtot],
                                        sidb[base:base + 64, base:base + 64]),
                                        waits=[tok_b, free_ptv[tb]], inc=s_t)
                                    free_svt = tok_T
                                    tok_vc = sc.op("vector", lambda e, h=h, jc=jc, tb=tb, wtot=wtot: e.tensor_copy(
                                        vc_tm[0:wtot, jc * 1024 + h * 64: jc * 1024 + h * 64 + 64]
                                        if False else
                                        vc_tm.rearrange("p (jcb d) -> p jcb d", jcb=11)[0:wtot, jc, h * 64:h * 64 + 64],
                                        ptv[tb][0:wtot, :]),
                                        waits=[tok_T])
                                    free_ptv[tb] = tok_vc
                                cidx += 1
                sc.barrier()
                sc.emit_phase(f"I_{part}")

        with ExitStack() as cj2:
            en2 = cj2.enter_context
            sq = [en2(nc.sbuf_tensor(f"{px}sq{j}", [128, D], BF16)).ap() for j in range(2)]
            ssc1 = [en2(nc.sbuf_tensor(f"{px}ssc1{j}", [128, JPAD], F32)).ap() for j in range(2)]
            sexp1 = [en2(nc.sbuf_tensor(f"{px}sexp1{j}", [128, JPAD], F32)).ap() for j in range(2)]
            sprob1 = [en2(nc.sbuf_tensor(f"{px}sprob1{j}", [128, JPAD], BF16)).ap() for j in range(2)]
            sprT = [en2(nc.sbuf_tensor(f"{px}sprT{j}", [128, 128], BF16)).ap() for j in range(2)]
            scml = [en2(nc.sbuf_tensor(f"{px}scml{j}", [128, 64], F32)).ap() for j in range(2)]
            sneg = [en2(nc.sbuf_tensor(f"{px}sneg{j}", [128, 1], F32)).ap() for j in range(2)]
            snegs = [en2(nc.sbuf_tensor(f"{px}snegs{j}", [128, 1], F32)).ap() for j in range(2)]
            ssume = [en2(nc.sbuf_tensor(f"{px}ssume{j}", [128, 1], F32)).ap() for j in range(2)]
            srcpo = [en2(nc.sbuf_tensor(f"{px}srcpo{j}", [128, 1], F32)).ap() for j in range(2)]
            psc = [en2(nc.psum_tensor(f"{px}psc{j}", [128, 512], F32)).ap() for j in range(3)]
            ptr = [en2(nc.psum_tensor(f"{px}ptr{j}", [128, 128], BF16)).ap() for j in range(2)]
            pout = [en2(nc.psum_tensor(f"{px}pout{j}", [64, 128], F32)).ap() for j in range(2)]

            satt = [en2(nc.sbuf_tensor(f"{px}satt{j}", [128, 8 * 128], BF16)).ap() for j in range(2)]
            s_q = [sc.sem(f"{px}qj{j}") for j in range(2)]
            s_cm = [sc.sem(f"{px}cm{j}") for j in range(2)]
            s_at = [sc.sem(f"{px}at{j}") for j in range(2)]

            free_q = [None, None]
            free_cm = [None, None]
            free_att = [None, None]
            free_psc = [None, None, None]
            free_ptr = [None, None]
            free_prT = [None, None]
            free_pout = [None, None]
            prev_norm = [None, None]
            tr_done = [None, None]
            exp_done = [None, None]
            psci = 0
            oidx = 0
            for qi in range(TT):
                bq = qi % 2
                Lq, win0 = _lq(qi)
                nblk = (Lq + 127) // 128
                nch = (Lq + 511) // 512
                tok_q = sc.op("sync", lambda e, bq=bq, qi=qi: e.dma_start(
                    out=sq[bq].rearrange("p (c t) -> p c t", c=8),
                    in_=qq_d[qi].rearrange("c p t -> p c t")),
                    waits=[free_q[bq]], inc=s_q[bq], dma=True)
                tok_cmq = sc.op("sync", lambda e, bq=bq, qi=qi: e.dma_start(out=scml[bq][:], in_=cml_d[qi]),
                                waits=[free_cm[bq]], inc=s_cm[bq], dma=True)
                last_cm_read = None
                last_oc = None

                def consume(h, tok_pr, qi=qi, bq=bq, Lq=Lq, nblk=nblk):
                    nonlocal oidx, last_oc
                    hb = h % 2
                    base = (h % 2) * 64
                    hp = h // 2
                    ob = oidx % 2
                    out_last = None
                    tok_tr = None
                    for jc in range(nblk):
                        wj = min(128, Lq - jc * 128)
                        tb = jc % 2
                        tok_tr = sc.op("tensor", lambda e, hb=hb, jc=jc, tb=tb, wj=wj: e.transpose(
                            ptr[tb][0:wj, :], sprob1[hb][:, jc * 128: jc * 128 + wj], sidb[:]),
                            waits=([tok_pr] if jc == 0 else []) + [free_ptr[tb]], inc=s_t)
                        tok_cp = sc.op("vector", lambda e, tb=tb, wj=wj: e.tensor_copy(sprT[tb][0:wj, :],
                                                                                      ptr[tb][0:wj, :]),
                                       waits=[tok_tr, free_prT[tb]])
                        free_ptr[tb] = tok_cp
                        mm2 = sc.op("tensor", lambda e, jc=jc, tb=tb, h=h, wj=wj, ob=ob, nblk=nblk: e.matmul(
                            pout[ob][:],
                            vc_tm[0:wj, jc * 1024 + h * 64: jc * 1024 + h * 64 + 64],
                            sprT[tb][0:wj, :], start=(jc == 0), stop=(jc == nblk - 1)),
                            waits=[tok_cp] + ([free_pout[ob]] if jc == 0 else []),
                            inc=s_t if jc == nblk - 1 else None)
                        free_prT[tb] = mm2
                        out_last = mm2
                    tr_done[hb] = tok_tr
                    tok_oc = sc.op("vector", lambda e, base=base, hp=hp, bq=bq, ob=ob: e.tensor_copy(
                        satt[bq][base:base + 64, hp * 128:(hp + 1) * 128], pout[ob][:]),
                        waits=[out_last] + ([free_att[bq]] if h == 0 else []))
                    free_pout[ob] = tok_oc
                    last_oc = tok_oc
                    oidx += 1

                pending = None
                for h in range(NH):
                    hb = h % 2
                    base = (h % 2) * 64
                    hp = h // 2
                    ev_toks = []
                    for chk in range(nch):
                        wch = min(512, Lq - chk * 512)
                        pb = psci % 3
                        psci += 1
                        mm = sc.op("tensor", lambda e, base=base, hp=hp, bq=bq, chk=chk, wch=wch, pb=pb: e.matmul(
                            psc[pb][:, 0:wch],
                            sq[bq].rearrange("p (c t) -> p c t", c=8)[base:base + 64, hp, :],
                            kc_fm[base:base + 64, hp * JPAD + chk * 512: hp * JPAD + chk * 512 + wch],
                            start=True, stop=True),
                            waits=[tok_q, free_psc[pb]], inc=s_t)
                        lo = chk * 512
                        hi = lo + wch
                        if lo < win0:
                            w0 = min(hi, win0) - lo
                            sc.op("vector", lambda e, hb=hb, lo=lo, w0=w0, pb=pb: e.tensor_copy(
                                ssc1[hb][:, lo:lo + w0], psc[pb][:, 0:w0]), waits=[mm])
                        if hi > win0:
                            m0 = max(lo, win0)
                            wmm = hi - m0
                            tok_ev = sc.op("vector", lambda e, hb=hb, m0=m0, wmm=wmm, lo=lo, pb=pb, bq=bq,
                                           win0=win0: e.tensor_add(
                                ssc1[hb][:, m0:m0 + wmm], psc[pb][:, m0 - lo:m0 - lo + wmm],
                                scml[bq][:, m0 - win0:m0 - win0 + wmm]),
                                waits=[mm, tok_cmq])
                            last_cm_read = tok_ev
                        else:
                            tok_ev = sc.op("vector", lambda e: e.engine_nop(), waits=[mm])
                        free_psc[pb] = tok_ev
                        ev_toks.append(tok_ev)
                    tok_ngs = sc.op("vector", lambda e, hb=hb, Lq=Lq: e.reduce_max(sneg[hb][:], ssc1[hb][:, 0:Lq],
                                                                                  axis=AX.X, negate=True),
                                    waits=[ev_toks[-1]])
                    tok_ngss = sc.op("vector", lambda e, hb=hb: e.tensor_scalar_mul(snegs[hb][:], sneg[hb][:],
                                                                                   SCALE))
                    tok_exp = sc.op("scalar", lambda e, hb=hb, Lq=Lq: e.activation(
                        out=sexp1[hb][:, 0:Lq], in_=ssc1[hb][:, 0:Lq], func=AF.Exp, bias=snegs[hb][:, 0:1],
                        scale=SCALE, accum_out=ssume[hb][:]),
                        waits=[tok_ngss, prev_norm[hb]])
                    exp_done[hb] = tok_exp
                    tok_rcp = sc.op("vector", lambda e, hb=hb: e.reciprocal(srcpo[hb][:], ssume[hb][:]),
                                    waits=[tok_exp])
                    tok_pr = sc.op("vector", lambda e, hb=hb, Lq=Lq: e.tensor_scalar(
                        out=sprob1[hb][:, 0:Lq], in0=sexp1[hb][:, 0:Lq], scalar1=srcpo[hb][:, 0:1],
                        scalar2=None, op0=ALU.mult),
                        waits=[tok_rcp, tr_done[hb]])
                    prev_norm[hb] = tok_pr
                    if pending is not None:
                        consume(*pending)
                    pending = (h, tok_pr)
                consume(*pending)
                free_q[bq] = prev_norm[pending[0] % 2]
                free_cm[bq] = last_cm_read
                free_att[bq] = sc.op("sync", lambda e, bq=bq, qi=qi: e.dma_start(
                    out=attn_d[qi].rearrange("c p t -> p c t"),
                    in_=satt[bq].rearrange("p (c t) -> p c t", c=8)),
                    waits=[last_oc], inc=s_at[bq], dma=True)
            sc.barrier()
            sc.emit_phase("J_attn")

        with ExitStack() as ck:
            en2 = ck.enter_context
            swo = en2(nc.sbuf_tensor(px + "swo", [128, 8 * D], BF16)).ap()
            sbo = en2(nc.sbuf_tensor(px + "sbo", [128, D], F32)).ap()
            sx1r = [en2(nc.sbuf_tensor(f"{px}sx1r{j}", [128, D], F32)).ap() for j in range(2)]
            sat2 = [en2(nc.sbuf_tensor(f"{px}sat2{j}", [128, 8 * 128], BF16)).ap() for j in range(2)]
            so2 = [en2(nc.sbuf_tensor(f"{px}so2{j}", [128, 512], F32)).ap() for j in range(2)]
            pwo = [en2(nc.psum_tensor(f"{px}pwo{j}", [128, 512], F32)).ap() for j in range(2)]

            s_wo = sc.sem(px + "wo")
            s_bo = sc.sem(px + "bol")
            s_x1r = [sc.sem(f"{px}x1r{j}") for j in range(2)]
            s_o2 = [sc.sem(f"{px}o2{j}") for j in range(2)]
            s_at2 = [sc.sem(f"{px}at2{j}") for j in range(2)]

            t_wo = sc.op("gpsimd", lambda e: e.dma_start(out=swo.rearrange("p (c k) -> p c k", c=8),
                                                         in_=woT_d.rearrange("(c p) k -> p c k", p=128)),
                         inc=s_wo, dma=True)
            t_bo = sc.op("sync", lambda e: e.dma_start(out=sbo[:], in_=bo_d.broadcast_to([128, D])),
                         inc=s_bo, dma=True)
            free_x1r = [None, None]
            free_o2 = [None, None]
            free_pwo = [None, None]
            free_at2 = [None, None]
            for qi in range(TT):
                b = qi % 2
                r0 = qi * 128
                tok_x1 = sc.op("sync", lambda e, b=b, r0=r0: e.dma_start(out=sx1r[b][:], in_=x1_d[r0:r0 + 128, :]),
                               waits=[free_x1r[b]], inc=s_x1r[b], dma=True)
                tok_at = sc.op("sync", lambda e, b=b, qi=qi: e.dma_start(
                    out=sat2[b].rearrange("p (c t) -> p c t", c=8),
                    in_=attn_d[qi].rearrange("c p t -> p c t")),
                    waits=[free_at2[b]], inc=s_at2[b], dma=True)
                st_toks = []
                for half in range(2):
                    pb = half
                    mm = None
                    for c in range(8):
                        mm = sc.op("tensor", lambda e, b=b, half=half, c=c, pb=pb: e.matmul(
                            pwo[pb][:], sat2[b][:, c * 128:(c + 1) * 128],
                            swo[:, c * D + half * 512: c * D + half * 512 + 512],
                            start=(c == 0), stop=(c == 7)),
                            waits=[t_wo, tok_at, free_pwo[pb]] if c == 0 else [],
                            inc=s_t if c == 7 else None)
                    ob = half
                    tok_o = sc.op("vector", lambda e, b=b, half=half, ob=ob, pb=pb: e.tensor_add(
                        so2[ob][:], pwo[pb][:], sbo[:, half * 512:(half + 1) * 512]),
                        waits=[mm, t_bo, free_o2[ob]])
                    tok_o2 = sc.op("vector", lambda e, b=b, half=half, ob=ob: e.tensor_add(
                        so2[ob][:], so2[ob][:], sx1r[b][:, half * 512:(half + 1) * 512]),
                        waits=[tok_x1])
                    tok_st = sc.op("sync", lambda e, r0=r0, half=half, ob=ob: e.dma_start(
                        out=x2p_d[r0:r0 + 128, half * 512:(half + 1) * 512], in_=so2[ob][:]),
                        waits=[tok_o2], inc=s_o2[ob], dma=True)
                    free_o2[ob] = tok_st
                    free_pwo[pb] = tok_o
                    st_toks.append(tok_o2)
                free_x1r[b] = st_toks[-1]
                free_at2[b] = mm
            sc.barrier()
            sc.emit_phase("K_wout")


def build_k3():
    nc = bass.Bass("TRN2", target_bir_lowering=False)
    x1_d = nc.dram_tensor("x1", [T, D], F32, kind="ExternalInput").ap()
    wr_d = nc.dram_tensor("wr", [D, E1], F32, kind="ExternalInput").ap()
    wqT_d = nc.dram_tensor("wqT", [D, 3 * D], F32, kind="ExternalInput").ap()
    wcp_d = nc.dram_tensor("wcp", [128, 3 * 8 * DH], F32, kind="ExternalInput").ap()
    bcv_d = nc.dram_tensor("bcv", [128, 8], F32, kind="ExternalInput").ap()
    nlk_d = nc.dram_tensor("nlk", [128, 8], F32, kind="ExternalInput").ap()
    nlv_d = nc.dram_tensor("nlv", [128, 8], F32, kind="ExternalInput").ap()
    woT_d = nc.dram_tensor("woT", [D, D], F32, kind="ExternalInput").ap()
    bo_d = nc.dram_tensor("bo", [1, D], F32, kind="ExternalInput").ap()
    cml_d = nc.dram_tensor("cml", [TT, 128, 64], F32, kind="ExternalInput").ap()
    idb_d = nc.dram_tensor("idb", [128, 128], BF16, kind="ExternalInput").ap()
    idf_d = nc.dram_tensor("idf", [128, 128], F32, kind="ExternalInput").ap()
    x2p_d = nc.dram_tensor("x2p", [T, D], F32, kind="ExternalOutput").ap()
    xnq1_d = nc.dram_tensor("xnq1", [TT, 8, 128, 128], BF16, kind="ExternalOutput").ap()
    rsc_d = nc.dram_tensor("rsc", [E1, T], F32, kind="ExternalOutput").ap()
    qq_d = nc.dram_tensor("qq", [TT, 8, 128, 128], BF16, kind="Internal").ap()
    attn_d = nc.dram_tensor("attn", [TT, 8, 128, 128], BF16, kind="Internal").ap()
    sc = Sched(nc)
    emit_k3(nc, sc, "c_", x1_d, wr_d, wqT_d, wcp_d, bcv_d, nlk_d, nlv_d,
            woT_d, bo_d, cml_d, idb_d, idf_d, x2p_d, xnq1_d, rsc_d, qq_d, attn_d)
    return nc


def k3_inputs(inputs, res2):
    idb, idf, cm = _host_consts()
    wqT = np.ascontiguousarray(inputs["L1_attn_w_qkv"].T)
    wc = inputs["L1_attn_w_conv"]
    wcp = np.zeros((128, 3 * 8 * DH), np.float32)
    for h in range(NH):
        base = (h % 2) * 64
        hp = h // 2
        blk = wc[h * DH:(h + 1) * DH]
        for tap in range(3):
            wcp[base:base + 64, tap * 512 + hp * 64: tap * 512 + (hp + 1) * 64] = blk[:, :, tap].T
    bcv = np.ascontiguousarray(inputs["L1_attn_b_conv"].reshape(8, 128).T)
    nlk = np.ascontiguousarray(inputs["L1_attn_null_k"].reshape(8, 128).T)
    nlv = np.ascontiguousarray(inputs["L1_attn_null_v"].reshape(8, 128).T)
    woT = np.ascontiguousarray(inputs["L1_attn_w_out"].T)
    bo = np.ascontiguousarray(inputs["L1_attn_b_out"].reshape(1, D))
    wr = np.ascontiguousarray(inputs["L1_ffn_Wr"])
    cml = np.zeros((TT, 128, 64), np.float32)
    for qi in range(TT):
        Lq, win0 = _lq(qi)
        for r in range(128):
            i = qi * 128 + r
            fm = i // 3 + 2
            for c in range(64):
                j = win0 + c
                if j >= fm:
                    cml[qi, r, c] = NEG
    maps = []
    for b in range(NCORE):
        maps.append({
            "x1": np.asarray(res2.results[b]["x1"]),
            "wr": wr, "wqT": wqT, "wcp": wcp, "bcv": bcv, "nlk": nlk, "nlv": nlv,
            "woT": woT, "bo": bo, "cml": cml, "idb": idb, "idf": idf,
        })
    return maps



def emit_k4(nc, sc, px, xnq1_d, rsc_d, x2p_d, w1_d, b1_d, w2_d, b2_d,
            idb_d, out_d, w1b_d, w2b_d):
    with ExitStack() as ctx:
        en = ctx.enter_context
        sidb = en(nc.sbuf_tensor(px + "sidb", [128, 128], BF16)).ap()
        sacc = en(nc.sbuf_tensor(px + "sacc", [128, 8 * T], BF16)).ap()

        s_t = sc.sem(px + "t")
        s_c = sc.sem(px + "c")
        s_cast = sc.sem(px + "cast")

        sc.op("sync", lambda e: e.dma_start(out=sidb[:], in_=idb_d[:]), inc=s_c, dma=True)
        for e_i in range(E1):
            sc.op("gpsimd", lambda e, e_i=e_i: e.dma_start(out=w1b_d[e_i], in_=w1_d[e_i]), inc=s_cast, dma=True)
            sc.op("gpsimd", lambda e, e_i=e_i: e.dma_start(out=w2b_d[e_i], in_=w2_d[e_i]), inc=s_cast, dma=True)
        tok_cast = (s_cast, s_cast.c)

        with ExitStack() as cf:
            en2 = cf.enter_context
            sw2s = en2(nc.sbuf_tensor(px + "sw2s", [128, 32 * D], BF16)).ap()
            sw1f = [en2(nc.sbuf_tensor(f"{px}sw1f{j}", [128, 8 * 128], BF16)).ap() for j in range(3)]
            sxnb = [en2(nc.sbuf_tensor(f"{px}sxnb{j}", [128, 8 * 512], BF16)).ap() for j in range(2)]
            sbc = [en2(nc.sbuf_tensor(f"{px}sbc{j}", [128, 512], F32)).ap() for j in range(2)]
            sG = en2(nc.sbuf_tensor(px + "sG", [128, 32 * 512], BF16)).ap()
            sb1f = en2(nc.sbuf_tensor(px + "sb1f", [128, 32], F32)).ap()
            stmp5 = en2(nc.sbuf_tensor(px + "stmp5", [128, 512], F32)).ap()
            cb2 = en2(ExitStack())
            sb2r = cb2.enter_context(nc.sbuf_tensor(px + "sb2r", [E1, D], F32)).ap()
            srt = cb2.enter_context(nc.sbuf_tensor(px + "srt", [E1, 512], F32)).ap()
            ph = [en2(nc.psum_tensor(f"{px}ph{j}", [128, 512], F32)).ap() for j in range(2)]
            po = [en2(nc.psum_tensor(f"{px}po{j}", [128, 512], F32)).ap() for j in range(2)]

            s_nb = [sc.sem(f"{px}nb{j}") for j in range(2)]
            s_bc = [sc.sem(f"{px}bc{j}") for j in range(2)]
            s_w1f = [sc.sem(f"{px}w1f{j}") for j in range(3)]
            s_w2l = sc.sem(px + "w2l")
            s_b1l = sc.sem(px + "b1l")

            sc.op("sync", lambda e: e.dma_start(out=sb2r[:], in_=b2_d[:]), inc=s_c, dma=True)
            t_c = (s_c, s_c.c)
            s_rtl = sc.sem(px + "rtl")

            acc_tok = None
            free_pb2 = [None, None]
            free_rt = None
            for tc in range(8):
                tok_rt = sc.op("sync", lambda e, tc=tc: e.dma_start(out=srt[:], in_=rsc_d[:, tc * 512:(tc + 1) * 512]),
                               waits=[free_rt], inc=s_rtl, dma=True)
                last_mm_rt = None
                for oc in range(8):
                    pbi = oc % 2
                    mm = sc.op("tensor", lambda e, oc=oc, tc=tc, pbi=pbi: e.matmul(
                        po[pbi][:], sb2r[:, oc * 128:(oc + 1) * 128], srt[:],
                        start=True, stop=True),
                        waits=[t_c, tok_rt, free_pb2[pbi]], inc=s_t)
                    acc_tok = sc.op("vector", lambda e, oc=oc, tc=tc, pbi=pbi: e.tensor_copy(
                        sacc[:, oc * T + tc * 512: oc * T + (tc + 1) * 512], po[pbi][:]), waits=[mm])
                    free_pb2[pbi] = acc_tok
                    last_mm_rt = mm
                free_rt = last_mm_rt

            sc.barrier()
            sc.emit_phase("L_b2term")
            cb2.close()
            free_xnb = [None, None]
            free_bc = [None, None]
            free_ph = [None, None]
            free_po = [None, None]
            free_w1f = [None, None, None]
            free_w2 = None
            free_b1 = None
            phi = 0
            w1i = 0
            for ei in range(E1):
                t_w2 = sc.op("gpsimd", lambda e, ei=ei: e.dma_start(
                    out=sw2s.rearrange("p (f o) -> p f o", f=32),
                    in_=w2b_d[ei].rearrange("(f p) o -> p f o", p=128)),
                    waits=[tok_cast, free_w2], inc=s_w2l, dma=True)
                t_b1 = sc.op("sync", lambda e, ei=ei: e.dma_start(out=sb1f[:], in_=b1_d[ei]),
                             waits=[free_b1], inc=s_b1l, dma=True)
                last_mm_e = None
                for tc in range(8):
                    bx = tc % 2
                    tok_xn = None
                    for q4 in range(4):
                        tok_xn = sc.op("sync", lambda e, bx=bx, tc=tc, q4=q4: e.dma_start(
                            out=sxnb[bx].rearrange("p (c q t) -> p c q t", c=8, q=4)[:, :, q4, :],
                            in_=xnq1_d[tc * 4 + q4].rearrange("c p t -> p c t")),
                            waits=[free_xnb[bx]] if q4 == 0 else [], inc=s_nb[bx], dma=True)
                    tok_bc = sc.op("sync", lambda e, bx=bx, ei=ei, tc=tc: e.dma_start(
                        out=sbc[bx][:], in_=rsc_d[ei:ei + 1, tc * 512:(tc + 1) * 512].broadcast_to([128, 512])),
                        waits=[free_bc[bx]], inc=s_bc[bx], dma=True)
                    g_toks = []
                    prev_G = None
                    for f in range(32):
                        wslot = w1i % 3
                        w1i += 1
                        t_w1f = sc.op("gpsimd", lambda e, ei=ei, f=f, wslot=wslot: e.dma_start(
                            out=sw1f[wslot].rearrange("p (c t) -> p c t", c=8),
                            in_=w1b_d[ei].rearrange("(c p) f -> p c f", p=128)[:, :, f * 128:(f + 1) * 128]),
                            waits=[tok_cast, free_w1f[wslot]], inc=s_w1f[wslot], dma=True)
                        pb = phi % 2
                        phi += 1
                        mm1 = None
                        for c in range(8):
                            mm1 = sc.op("tensor", lambda e, f=f, c=c, pb=pb, bx=bx, wslot=wslot: e.matmul(
                                ph[pb][:], sw1f[wslot][:, c * 128:(c + 1) * 128],
                                sxnb[bx][:, c * 512:(c + 1) * 512], start=(c == 0), stop=(c == 7)),
                                waits=[t_w1f, tok_xn, free_ph[pb]] if c == 0 else [],
                                inc=s_t if c == 7 else None)
                        free_w1f[wslot] = mm1
                        tok_G = sc.op("scalar", lambda e, f=f, pb=pb: e.activation(
                            out=sG[:, f * 512:(f + 1) * 512], in_=ph[pb][:], func=AF.Gelu,
                            bias=sb1f[:, f:f + 1]),
                            waits=[mm1, t_b1, prev_G])
                        prev_G = tok_G
                        free_ph[pb] = tok_G
                        g_toks.append(tok_G)
                    for oc in range(8):
                        pb = phi % 2
                        phi += 1
                        mm2 = None
                        for f in range(32):
                            mm2 = sc.op("tensor", lambda e, oc=oc, f=f, pb=pb: e.matmul(
                                po[pb][:], sw2s[:, f * D + oc * 128: f * D + oc * 128 + 128],
                                sG[:, f * 512:(f + 1) * 512], start=(f == 0), stop=(f == 31)),
                                waits=[t_w2, g_toks[-1], free_po[pb]] if f == 0 else [],
                                inc=s_t if f == 31 else None)
                        tok_t5 = sc.op("vector", lambda e, pb=pb, bx=bx: e.tensor_tensor(
                            stmp5[:], po[pb][:], sbc[bx][:], ALU.mult), waits=[mm2, tok_bc])
                        acc_tok = sc.op("vector", lambda e, oc=oc, tc=tc: e.tensor_tensor(
                            sacc[:, oc * T + tc * 512: oc * T + (tc + 1) * 512],
                            sacc[:, oc * T + tc * 512: oc * T + (tc + 1) * 512], stmp5[:], ALU.add),
                            waits=[acc_tok])
                        free_po[pb] = tok_t5
                        last_mm_e = mm2
                    free_xnb[bx] = last_mm_e
                    free_bc[bx] = acc_tok
                free_w2 = last_mm_e
                free_b1 = prev_G
            sc.barrier()
            sc.emit_phase("L_ffn")

            with ExitStack() as cg:
                en4 = cg.enter_context
                sx2p = [en4(nc.sbuf_tensor(f"{px}sx2p{j}", [128, D], F32)).ap() for j in range(2)]
                sxo = [en4(nc.sbuf_tensor(f"{px}sxo{j}", [128, D], F32)).ap() for j in range(2)]
                pacc = [en4(nc.psum_tensor(f"{px}pacc{j}", [128, 128], BF16)).ap() for j in range(2)]
                s_xp = [sc.sem(f"{px}xp{j}") for j in range(2)]
                s_xo = [sc.sem(f"{px}xoo{j}") for j in range(2)]
                free_p = [None, None]
                free_o = [None, None]
                free_i = [None, None]
                for i in range(TT):
                    b = i % 2
                    tok_x2p = sc.op("sync", lambda e, b=b, i=i: e.dma_start(out=sx2p[b][:],
                                                                           in_=x2p_d[i * 128:(i + 1) * 128, :]),
                                    waits=[free_i[b]], inc=s_xp[b], dma=True)
                    add_toks = []
                    for c in range(8):
                        pb = c % 2
                        tok_T = sc.op("tensor", lambda e, i=i, c=c, pb=pb: e.transpose(
                            pacc[pb][:], sacc[:, c * T + i * 128: c * T + i * 128 + 128], sidb[:]),
                            waits=[(s_c, 16), free_p[pb]] + ([acc_tok] if i == 0 and c == 0 else []), inc=s_t)
                        tok_add = sc.op("vector", lambda e, b=b, c=c, pb=pb: e.tensor_tensor(
                            sxo[b][:, c * 128:(c + 1) * 128], sx2p[b][:, c * 128:(c + 1) * 128],
                            pacc[pb][:], ALU.add),
                            waits=[tok_T, tok_x2p] + ([free_o[b]] if c == 0 else []))
                        free_p[pb] = tok_add
                        add_toks.append(tok_add)
                    tok_st2 = sc.op("sync", lambda e, b=b, i=i: e.dma_start(out=out_d[i * 128:(i + 1) * 128, :],
                                                                           in_=sxo[b][:]),
                                    waits=[add_toks[-1]], inc=s_xo[b], dma=True)
                    free_o[b] = tok_st2
                    free_i[b] = add_toks[-1]
                sc.barrier()
                sc.emit_phase("M_final")


def build_k4():
    nc = bass.Bass("TRN2", target_bir_lowering=False)
    xnq1_d = nc.dram_tensor("xnq1", [TT, 8, 128, 128], BF16, kind="ExternalInput").ap()
    rsc_d = nc.dram_tensor("rsc", [E1, T], F32, kind="ExternalInput").ap()
    x2p_d = nc.dram_tensor("x2p", [T, D], F32, kind="ExternalInput").ap()
    w1_d = nc.dram_tensor("w1", [E1, D, F1], F32, kind="ExternalInput").ap()
    b1_d = nc.dram_tensor("b1r", [E1, 128, 32], F32, kind="ExternalInput").ap()
    w2_d = nc.dram_tensor("w2", [E1, F1, D], F32, kind="ExternalInput").ap()
    b2_d = nc.dram_tensor("b2", [E1, D], F32, kind="ExternalInput").ap()
    idb_d = nc.dram_tensor("idb", [128, 128], BF16, kind="ExternalInput").ap()
    out_d = nc.dram_tensor("outp", [T, D], F32, kind="ExternalOutput").ap()
    w1b_d = nc.dram_tensor("w1b", [E1, D, F1], BF16, kind="Internal").ap()
    w2b_d = nc.dram_tensor("w2b", [E1, F1, D], BF16, kind="Internal").ap()
    sc = Sched(nc)
    emit_k4(nc, sc, "d_", xnq1_d, rsc_d, x2p_d, w1_d, b1_d, w2_d, b2_d,
            idb_d, out_d, w1b_d, w2b_d)
    return nc


def k4_inputs(inputs, res3):
    idb, idf, cm = _host_consts()
    b1r = np.ascontiguousarray(inputs["L1_ffn_b1"].reshape(E1, 32, 128).transpose(0, 2, 1))
    maps = []
    for b in range(NCORE):
        maps.append({
            "xnq1": np.asarray(res3.results[b]["xnq1"]),
            "rsc": np.asarray(res3.results[b]["rsc"]),
            "x2p": np.asarray(res3.results[b]["x2p"]),
            "w1": inputs["L1_ffn_W1"], "b1r": b1r,
            "w2": inputs["L1_ffn_W2"], "b2": inputs["L1_ffn_b2"],
            "idb": idb,
        })
    return maps



def _host_consts():
    idb = np.eye(128, dtype=np.float32).astype(BF)
    idf = np.eye(128, dtype=np.float32)
    cm = np.triu(np.full((128, 128), NEG, np.float32), 1)
    return idb, idf, cm


def k1_inputs(inputs):
    idb, idf, cm = _host_consts()
    x = np.ascontiguousarray(inputs["L0_attn_x"].astype(np.float32))
    wt = np.ascontiguousarray(inputs["L0_attn_W"].T)
    gw = np.ascontiguousarray(inputs["L0_ffn_gate_w"])
    gb = np.ascontiguousarray(inputs["L0_ffn_gate_b"].reshape(1, E0))
    return [{"x": x[b], "wt": wt, "gw": gw, "gb": gb, "idb": idb, "idf": idf, "cm": cm}
            for b in range(NCORE)]


def build_fused():
    nc = bass.Bass("TRN2", target_bir_lowering=False, num_devices=NCORE)
    I, O, N = "ExternalInput", "ExternalOutput", "Internal"
    dt = nc.dram_tensor
    x_d = dt("x", [T, D], F32, kind=I).ap()
    wt_d = dt("wt", [D, 2 * D], F32, kind=I).ap()
    gw_d = dt("gw", [D, E0], F32, kind=I).ap()
    gb_d = dt("gb", [1, E0], F32, kind=I).ap()
    idb_d = dt("idb", [128, 128], BF16, kind=I).ap()
    idf_d = dt("idf", [128, 128], F32, kind=I).ap()
    cm_d = dt("cm", [128, 128], F32, kind=I).ap()
    ew1_d = dt("ew1", [E0, D, H0], F32, kind=I).ap()
    ew3_d = dt("ew3", [E0, D, H0], F32, kind=I).ap()
    ew2_d = dt("ew2", [E0, H0, D], F32, kind=I).ap()
    sw1_d = dt("sw1", [2, D, H0], F32, kind=I).ap()
    sw3_d = dt("sw3", [2, D, H0], F32, kind=I).ap()
    sw2_d = dt("sw2", [2, H0, D], F32, kind=I).ap()
    ssel_d = dt("Ssel", [128, E0], F32, kind=I).ap()
    stsel_d = dt("STsel", [E0, 128], F32, kind=I).ap()
    rsel_d = dt("Rsel", [128, E0], F32, kind=I).ap()
    ones_d = dt("ones_f", [1, 128], F32, kind=I).ap()
    wr_d = dt("wr", [D, E1], F32, kind=I).ap()
    wqT_d = dt("wqT", [D, 3 * D], F32, kind=I).ap()
    wcp_d = dt("wcp", [128, 3 * 8 * DH], F32, kind=I).ap()
    bcv_d = dt("bcv", [128, 8], F32, kind=I).ap()
    nlk_d = dt("nlk", [128, 8], F32, kind=I).ap()
    nlv_d = dt("nlv", [128, 8], F32, kind=I).ap()
    woT_d = dt("woT", [D, D], F32, kind=I).ap()
    bo_d = dt("bo", [1, D], F32, kind=I).ap()
    cml_d = dt("cml", [TT, 128, 64], F32, kind=I).ap()
    w1_d = dt("w1", [E1, D, F1], F32, kind=I).ap()
    b1_d = dt("b1r", [E1, 128, 32], F32, kind=I).ap()
    w2_d = dt("w2", [E1, F1, D], F32, kind=I).ap()
    b2_d = dt("b2", [E1, D], F32, kind=I).ap()
    out_d = dt("outp", [T, D], F32, kind=O).ap()
    x1p_d = dt("x1p", [T, D], F32, kind=N).ap()
    xnq_d = dt("xnq", [TT, 8, 128, 128], BF16, kind=N).ap()
    wmf_d = dt("wmf", [E0, T], F32, kind=N).ap()
    wmt_d = dt("wmt", [T, E0], F32, kind=N).ap()
    x0xn_d = dt("x0xn", [T, D], F32, kind=N).ap()
    wgf_d = dt("wgf", [128, T], F32, kind=N).ap()
    x1_d = dt("x1", [T, D], F32, kind=N).ap()
    wfin_d = dt("wfin", [E0, T], F32, kind=N).ap()
    ew1b_d = dt("ew1b", [E0, D, H0], BF16, kind=N).ap()
    ew3b_d = dt("ew3b", [E0, D, H0], BF16, kind=N).ap()
    ew2b_d = dt("ew2b", [E0, H0, D], BF16, kind=N).ap()
    sw1b_d = dt("sw1b", [2, D, H0], BF16, kind=N).ap()
    sw3b_d = dt("sw3b", [2, D, H0], BF16, kind=N).ap()
    sw2b_d = dt("sw2b", [2, H0, D], BF16, kind=N).ap()
    xnq1_d = dt("xnq1", [TT, 8, 128, 128], BF16, kind=N).ap()
    rsc_d = dt("rsc", [E1, T], F32, kind=N).ap()
    x2p_d = dt("x2p", [T, D], F32, kind=N).ap()
    qq_d = dt("qq", [TT, 8, 128, 128], BF16, kind=N).ap()
    attn_d = dt("attn", [TT, 8, 128, 128], BF16, kind=N).ap()
    w1b_d = dt("w1b", [E1, D, F1], BF16, kind=N).ap()
    w2b_d = dt("w2b", [E1, F1, D], BF16, kind=N).ap()

    sc = Sched(nc)
    emit_k1(nc, sc, "a_", x_d, wt_d, gw_d, gb_d, idb_d, idf_d, cm_d,
            x1p_d, xnq_d, wmf_d, wmt_d, x0xn_d)
    tok_ag = sc.op("gpsimd", lambda e: e.collective_compute(
        "AllGather", mybir.AluOpType.bypass,
        replica_groups=[list(range(NCORE))],
        ins=[wmf_d.opt()], outs=[wgf_d.opt()]))
    emit_k2(nc, sc, "b_", xnq_d, wmt_d, wgf_d, x1p_d, ew1_d, ew3_d, ew2_d,
            sw1_d, sw3_d, sw2_d, ssel_d, stsel_d, rsel_d, ones_d, idb_d, idf_d,
            x1_d, wfin_d, ew1b_d, ew3b_d, ew2b_d, sw1b_d, sw3b_d, sw2b_d,
            wgf_wait=tok_ag)
    emit_k3(nc, sc, "c_", x1_d, wr_d, wqT_d, wcp_d, bcv_d, nlk_d, nlv_d,
            woT_d, bo_d, cml_d, idb_d, idf_d, x2p_d, xnq1_d, rsc_d, qq_d, attn_d)
    emit_k4(nc, sc, "d_", xnq1_d, rsc_d, x2p_d, w1_d, b1_d, w2_d, b2_d,
            idb_d, out_d, w1b_d, w2b_d)
    return nc


def fused_inputs(inputs):
    idb, idf, cm = _host_consts()
    x = np.ascontiguousarray(inputs["L0_attn_x"].astype(np.float32))
    wt = np.ascontiguousarray(inputs["L0_attn_W"].T)
    gw = np.ascontiguousarray(inputs["L0_ffn_gate_w"])
    gb = np.ascontiguousarray(inputs["L0_ffn_gate_b"].reshape(1, E0))
    S = np.zeros((128, E0), np.float32)
    for r in range(NCORE):
        for e_i in range(E0):
            S[r * E0 + e_i, e_i] = 1.0
    ST = np.ascontiguousarray(S.T)
    ones_f = np.ones((1, 128), np.float32)
    wqT = np.ascontiguousarray(inputs["L1_attn_w_qkv"].T)
    wc = inputs["L1_attn_w_conv"]
    wcp = np.zeros((128, 3 * 8 * DH), np.float32)
    for h in range(NH):
        base = (h % 2) * 64
        hp = h // 2
        blk = wc[h * DH:(h + 1) * DH]
        for tap in range(3):
            wcp[base:base + 64, tap * 512 + hp * 64: tap * 512 + (hp + 1) * 64] = blk[:, :, tap].T
    bcv = np.ascontiguousarray(inputs["L1_attn_b_conv"].reshape(8, 128).T)
    nlk = np.ascontiguousarray(inputs["L1_attn_null_k"].reshape(8, 128).T)
    nlv = np.ascontiguousarray(inputs["L1_attn_null_v"].reshape(8, 128).T)
    woT = np.ascontiguousarray(inputs["L1_attn_w_out"].T)
    bo = np.ascontiguousarray(inputs["L1_attn_b_out"].reshape(1, D))
    wr = np.ascontiguousarray(inputs["L1_ffn_Wr"])
    cml = np.zeros((TT, 128, 64), np.float32)
    for qi in range(TT):
        Lq, win0 = _lq(qi)
        for r in range(128):
            i = qi * 128 + r
            fm = i // 3 + 2
            for c in range(64):
                if win0 + c >= fm:
                    cml[qi, r, c] = NEG
    b1r = np.ascontiguousarray(inputs["L1_ffn_b1"].reshape(E1, 32, 128).transpose(0, 2, 1))
    maps = []
    for b in range(NCORE):
        R = np.zeros((128, E0), np.float32)
        for e_i in range(E0):
            R[b * E0 + e_i, e_i] = 1.0
        maps.append({
            "x": x[b], "wt": wt, "gw": gw, "gb": gb, "idb": idb, "idf": idf, "cm": cm,
            "ew1": inputs["L0_ffn_ew1"], "ew3": inputs["L0_ffn_ew3"], "ew2": inputs["L0_ffn_ew2"],
            "sw1": inputs["L0_ffn_sw1"], "sw3": inputs["L0_ffn_sw3"], "sw2": inputs["L0_ffn_sw2"],
            "Ssel": S, "STsel": ST, "Rsel": R, "ones_f": ones_f,
            "wr": wr, "wqT": wqT, "wcp": wcp, "bcv": bcv, "nlk": nlk, "nlv": nlv,
            "woT": woT, "bo": bo, "cml": cml,
            "w1": inputs["L1_ffn_W1"], "b1r": b1r, "w2": inputs["L1_ffn_W2"],
            "b2": inputs["L1_ffn_b2"],
        })
    return maps


def kernel(**inputs):
    nc = build_fused()
    res = run_bass_kernel_spmd(nc, fused_inputs(inputs), core_ids=list(range(NCORE)))
    return np.stack([np.asarray(res.results[b]["outp"]) for b in range(NCORE)])


def kernel_multilaunch(**inputs):
    nc1 = build_k1()
    res1 = run_bass_kernel_spmd(nc1, k1_inputs(inputs), core_ids=list(range(NCORE)))
    nc2 = build_k2()
    res2 = run_bass_kernel_spmd(nc2, k2_inputs(inputs, res1), core_ids=list(range(NCORE)))
    nc3 = build_k3()
    res3 = run_bass_kernel_spmd(nc3, k3_inputs(inputs, res2), core_ids=list(range(NCORE)))
    nc4 = build_k4()
    res4 = run_bass_kernel_spmd(nc4, k4_inputs(inputs, res3), core_ids=list(range(NCORE)))
    out = np.stack([np.asarray(res4.results[b]["outp"]) for b in range(NCORE)])
    return out


# version 05 (final)
# speedup vs baseline: 1.1793x; 1.1670x over previous
import math
from contextlib import ExitStack

import numpy as np
import ml_dtypes

import concourse.bass as bass
import concourse.mybir as mybir
from concourse.bass_utils import run_bass_kernel_spmd

F32 = mybir.dt.float32
BF16 = mybir.dt.bfloat16
AF = mybir.ActivationFunctionType
ALU = mybir.AluOpType
AX = mybir.AxisListType
BF = ml_dtypes.bfloat16

T, D = 4096, 1024
TT = T // 128
NCORE = 8
E0 = 16
CAP = math.ceil(NCORE * T * 2 / E0 * 1.25)
E1 = 8
H0 = 2048
F1 = 4096
NEG = -1e30
N_BIS = 34
BIS_BOUND = 1e5

ENGS = ("sync", "tensor", "vector", "scalar", "gpsimd")
CHAIN = ("vector", "scalar", "gpsimd")


def _aslist(x):
    if x is None:
        return []
    if isinstance(x, list):
        return [t for t in x if t is not None]
    return [x]


class Sem:
    __slots__ = ("h", "c", "name")

    def __init__(self, h, name):
        self.h = h
        self.c = 0
        self.name = name


class Sched:
    def __init__(self, nc):
        self.nc = nc
        self.items = {e: [] for e in ENGS}
        self._phase_idx = 0
        self.all_sems = []
        self.chain = {e: self.sem(f"ch_{e}") for e in CHAIN}
        self._bar = self.sem("bar")

    def sem(self, name):
        if not hasattr(self, "_sem_registry"):
            self._sem_registry = {}
        if name in self._sem_registry:
            return self._sem_registry[name]
        s = Sem(self.nc.alloc_semaphore(name), name)
        self.all_sems.append(s)
        self._sem_registry[name] = s
        return s

    def op(self, eng, fn, waits=(), inc=None, dma=False):
        ws = []
        for w in waits:
            ws.extend(_aslist(w))
        if eng in CHAIN and not dma:
            assert inc is None
            cs = self.chain[eng]
            if cs.c > 0:
                ws.append((cs, cs.c))
            cs.c += 1
            self.items[eng].append((fn, tuple(ws), cs, 1))
            return (cs, cs.c)
        amt = 16 if dma else 1
        tok = None
        if inc is not None:
            inc.c += amt
            tok = (inc, inc.c)
        self.items[eng].append((fn, tuple(ws), inc, amt))
        return tok

    def barrier(self):
        bs = self._bar
        ex = tuple((s, s.c) for s in self.all_sems if s.c > 0 and s is not bs)
        for e in ENGS:
            self.items[e].append(("drain", ex, bs, 1))
            bs.c += 1
        tgt = bs.c
        for e in ENGS:
            self.items[e].append((None, ((bs, tgt),), None, 0))

    def emit_phase(self, name):
        items = self.items
        self.items = {e: [] for e in ENGS}
        self._phase_idx += 1
        with self.nc.Block(f"ph{self._phase_idx}_{name}") as block:
            for e in ENGS:
                lst = items[e]
                if not lst:
                    continue

                def body(eng, lst=lst):
                    water = {}
                    for fn, waits, inc, amt in lst:
                        for (s, v) in waits:
                            if water.get(id(s), -1) < v:
                                eng.wait_ge(s.h, v)
                                water[id(s)] = v
                        ins = None
                        if fn == "drain":
                            ins = eng.drain()
                        elif fn is not None:
                            ins = fn(eng)
                        if inc is not None:
                            if ins is None:
                                ins = eng.nop()
                            ins.then_inc(inc.h, amt)

                getattr(block, e)(body)



def emit_k1(nc, sc, px, x_d, wt_d, gw_d, gb_d, idb_d, idf_d, cm_d,
            x1p_d, xnq_d, wmf_d, wmt_d, x0xn_d):
    with ExitStack() as ctx:
        en = ctx.enter_context
        K_fm = en(nc.sbuf_tensor(px + "K_fm", [128, 8 * T], BF16)).ap()
        V_tm = en(nc.sbuf_tensor(px + "V_tm", [128, TT * 1024], BF16)).ap()
        sidb = en(nc.sbuf_tensor(px + "sidb", [128, 128], BF16)).ap()
        sidf = en(nc.sbuf_tensor(px + "sidf", [128, 128], F32)).ap()
        scm = en(nc.sbuf_tensor(px + "scm", [128, 128], F32)).ap()
        sgw = en(nc.sbuf_tensor(px + "sgw", [128, 8 * E0], BF16)).ap()
        sgb = en(nc.sbuf_tensor(px + "sgb", [128, E0], F32)).ap()

        s_t = sc.sem("t")
        s_c = sc.sem("c")
        s_gw = sc.sem("gwl")
        s_wt = sc.sem("wtl")

        sc.op("sync", lambda e: e.dma_start(out=sidb[:], in_=idb_d[:]), inc=s_c, dma=True)
        sc.op("sync", lambda e: e.dma_start(out=sidf[:], in_=idf_d[:]), inc=s_c, dma=True)
        sc.op("sync", lambda e: e.dma_start(out=scm[:], in_=cm_d[:]), inc=s_c, dma=True)
        sc.op("sync", lambda e: e.dma_start(out=sgb[:], in_=gb_d.broadcast_to([128, E0])), inc=s_c, dma=True)
        t_gw = sc.op("gpsimd", lambda e: e.dma_start(out=sgw.rearrange("p (c e) -> p c e", c=8),
                                                     in_=gw_d.rearrange("(c p) e -> p c e", p=128)),
                     inc=s_gw, dma=True)
        t_c = (s_c, s_c.c)

        with ExitStack() as ca:
            en2 = ca.enter_context
            sx = [en2(nc.sbuf_tensor(f"{px}sx{j}", [128, D], F32)).ap() for j in range(2)]
            sx0xn = [en2(nc.sbuf_tensor(f"{px}sx0xn{j}", [128, D], F32)).ap() for j in range(2)]
            sxh = [en2(nc.sbuf_tensor(f"{px}sxh{j}", [128, D], BF16)).ap() for j in range(2)]
            stg = [en2(nc.sbuf_tensor(f"{px}stg{j}", [128, D], BF16)).ap() for j in range(2)]
            swmt = [en2(nc.sbuf_tensor(f"{px}swmt{j}", [128, E0], F32)).ap() for j in range(2)]
            sxn = en2(nc.sbuf_tensor(px + "sxn", [128, D], F32)).ap()
            sqscr = en2(nc.sbuf_tensor(px + "sqscr", [128, D], F32)).ap()
            swmf = en2(nc.sbuf_tensor(px + "swmf", [128, T], F32)).ap()
            seps = en2(nc.sbuf_tensor(px + "seps", [128, 1], F32)).ap()
            ssum = en2(nc.sbuf_tensor(px + "ssum", [128, 1], F32)).ap()
            ssumsq = en2(nc.sbuf_tensor(px + "ssumsq", [128, 1], F32)).ap()
            smean = en2(nc.sbuf_tensor(px + "smean", [128, 1], F32)).ap()
            sm2t = en2(nc.sbuf_tensor(px + "sm2t", [128, 1], F32)).ap()
            svar = en2(nc.sbuf_tensor(px + "svar", [128, 1], F32)).ap()
            srstd = en2(nc.sbuf_tensor(px + "srstd", [128, 1], F32)).ap()
            slog = en2(nc.sbuf_tensor(px + "slog", [128, E0], F32)).ap()
            snegmx = en2(nc.sbuf_tensor(px + "snegmx", [128, 1], F32)).ap()
            sexp = en2(nc.sbuf_tensor(px + "sexp", [128, E0], F32)).ap()
            sesum = en2(nc.sbuf_tensor(px + "sesum", [128, 1], F32)).ap()
            srcp = en2(nc.sbuf_tensor(px + "srcp", [128, 1], F32)).ap()
            sscor = en2(nc.sbuf_tensor(px + "sscor", [128, E0], F32)).ap()
            sm1 = en2(nc.sbuf_tensor(px + "sm1", [128, 1], F32)).ap()
            seq1 = en2(nc.sbuf_tensor(px + "seq1", [128, E0], F32)).ap()
            smk = en2(nc.sbuf_tensor(px + "smk", [128, E0], F32)).ap()
            sm2 = en2(nc.sbuf_tensor(px + "sm2", [128, 1], F32)).ap()
            seq2 = en2(nc.sbuf_tensor(px + "seq2", [128, E0], F32)).ap()
            sasg = en2(nc.sbuf_tensor(px + "sasg", [128, E0], F32)).ap()
            st1 = en2(nc.sbuf_tensor(px + "st1", [128, E0], F32)).ap()
            st2 = en2(nc.sbuf_tensor(px + "st2", [128, E0], F32)).ap()
            ptp = [en2(nc.psum_tensor(f"{px}ptp{j}", [128, 128], BF16)).ap() for j in range(2)]
            plog = en2(nc.psum_tensor(px + "plog", [128, E0], F32)).ap()
            pwm = en2(nc.psum_tensor(px + "pwm", [E0, 128], F32)).ap()

            s_x = [sc.sem(f"x{j}") for j in range(2)]
            s_sx0 = [sc.sem(f"sx0_{j}") for j in range(2)]
            s_sq = [sc.sem(f"sq{j}") for j in range(2)]
            s_swm = [sc.sem(f"swm{j}") for j in range(2)]
            s_swf = sc.sem("swf")

            t_eps = sc.op("vector", lambda e: e.memset(seps[:], 1e-5))
            free_sx = [None, None]
            free_x0 = [None, None]
            free_xh = [None, None]
            free_stg = [None, None]
            free_wmt = [None, None]
            free_ptp = [None, None]
            free_plog = None
            free_pwm = None
            last_wmf_cp = None
            prev_var_read = None
            prev_xn = None
            for i in range(TT):
                b = i % 2
                r0 = i * 128
                tok_x = sc.op("sync", lambda e, b=b, r0=r0: e.dma_start(out=sx[b][:], in_=x_d[r0:r0 + 128, :]),
                              waits=[free_sx[b]], inc=s_x[b], dma=True)
                sc.op("vector", lambda e, b=b: e.reduce_sum(ssum[:], sx[b][:], axis=AX.X), waits=[tok_x])
                tok_sq = sc.op("scalar", lambda e, b=b: e.activation(out=sqscr[:], in_=sx[b][:], func=AF.Square,
                                                                    accum_out=ssumsq[:]),
                               waits=[tok_x, prev_var_read])
                tok_mean = sc.op("vector", lambda e: e.tensor_scalar_mul(smean[:], ssum[:], 1.0 / D))
                sc.op("vector", lambda e: e.tensor_mul(sm2t[:], smean[:], smean[:]))
                prev_var_read = sc.op("vector", lambda e: e.tensor_scalar_mul(svar[:], ssumsq[:], 1.0 / D),
                                      waits=[tok_sq])
                tok_var = sc.op("vector", lambda e: e.tensor_sub(svar[:], svar[:], sm2t[:]))
                sc.op("scalar", lambda e: e.activation(out=srstd[:], in_=svar[:], func=AF.Ln, bias=seps[:, 0:1]),
                      waits=[tok_var, t_eps, prev_xn])
                tok_rstd = sc.op("scalar", lambda e: e.activation(out=srstd[:], in_=srstd[:], func=AF.Exp, scale=-0.5))
                tok_xn = sc.op("vector", lambda e, b=b: e.tensor_scalar(out=sxn[:], in0=sx[b][:], scalar1=smean[:, 0:1],
                                                                       scalar2=srstd[:, 0:1], op0=ALU.subtract,
                                                                       op1=ALU.mult),
                               waits=[tok_mean, tok_rstd])
                prev_xn = tok_xn
                tok_x0xn = sc.op("vector", lambda e, b=b: e.tensor_add(sx0xn[b][:], sx[b][:], sxn[:]),
                                 waits=[free_x0[b]])
                free_sx[b] = tok_x0xn
                free_x0[b] = sc.op("sync", lambda e, b=b, r0=r0: e.dma_start(out=x0xn_d[r0:r0 + 128, :],
                                                                            in_=sx0xn[b][:]),
                                   waits=[tok_x0xn], inc=s_sx0[b], dma=True)
                tok_xh = sc.op("vector", lambda e, b=b: e.tensor_copy(sxh[b][:], sxn[:]), waits=[free_xh[b]])
                cp_toks = []
                wstg = free_stg[b]
                tok_tr = None
                for c in range(8):
                    tok_tr = sc.op("tensor", lambda e, b=b, c=c: e.transpose(ptp[c % 2][:],
                                                                             sxh[b][:, c * 128:(c + 1) * 128],
                                                                             sidb[:]),
                                   waits=[tok_xh, t_c, free_ptp[c % 2]], inc=s_t)
                    tok_cp = sc.op("vector", lambda e, b=b, c=c: e.tensor_copy(stg[b][:, c * 128:(c + 1) * 128],
                                                                               ptp[c % 2][:]),
                                   waits=[tok_tr, wstg])
                    free_ptp[c % 2] = tok_cp
                    cp_toks.append(tok_cp)
                free_xh[b] = tok_tr
                tok_stg = cp_toks[-1]
                tok_xnq = sc.op("sync", lambda e, b=b, i=i: e.dma_start(
                    out=xnq_d[i].rearrange("c p t -> p c t"),
                    in_=stg[b].rearrange("p (c t) -> p c t", c=8)),
                    waits=[tok_stg], inc=s_sq[b], dma=True)
                mm_tok = None
                for c in range(8):
                    mm_tok = sc.op("tensor", lambda e, b=b, c=c: e.matmul(plog[:], stg[b][:, c * 128:(c + 1) * 128],
                                                                          sgw[:, c * E0:(c + 1) * E0],
                                                                          start=(c == 0), stop=(c == 7)),
                                   waits=[tok_stg, t_gw, free_plog] if c == 0 else [],
                                   inc=s_t if c == 7 else None)
                free_stg[b] = [tok_xnq, mm_tok]
                tok_logcp = sc.op("vector", lambda e: e.tensor_copy(slog[:], plog[:]), waits=[mm_tok])
                free_plog = tok_logcp
                tok_ngx = sc.op("vector", lambda e: e.reduce_max(snegmx[:], slog[:], axis=AX.X, negate=True))
                tok_exp = sc.op("scalar", lambda e: e.activation(out=sexp[:], in_=slog[:], func=AF.Exp,
                                                                bias=snegmx[:, 0:1], accum_out=sesum[:]),
                                waits=[tok_ngx, tok_logcp])
                tok_rcp = sc.op("vector", lambda e: e.reciprocal(srcp[:], sesum[:]), waits=[tok_exp])
                sc.op("vector", lambda e: e.scalar_tensor_tensor(out=sscor[:], in0=sexp[:], scalar=srcp[:, 0:1],
                                                                 in1=sgb[:], op0=ALU.mult, op1=ALU.add),
                      waits=[tok_rcp, t_c])
                tok_m1 = sc.op("vector", lambda e: e.reduce_max(sm1[:], sscor[:], axis=AX.X))
                sc.op("vector", lambda e: e.tensor_scalar(out=seq1[:], in0=sscor[:], scalar1=sm1[:, 0:1],
                                                          scalar2=None, op0=ALU.is_equal),
                      waits=[tok_m1])
                sc.op("vector", lambda e: e.scalar_tensor_tensor(out=smk[:], in0=seq1[:], scalar=NEG,
                                                                 in1=sscor[:], op0=ALU.mult, op1=ALU.add))
                tok_m2 = sc.op("vector", lambda e: e.reduce_max(sm2[:], smk[:], axis=AX.X))
                sc.op("vector", lambda e: e.tensor_scalar(out=seq2[:], in0=smk[:], scalar1=sm2[:, 0:1],
                                                          scalar2=None, op0=ALU.is_equal),
                      waits=[tok_m2])
                sc.op("vector", lambda e: e.tensor_add(sasg[:], seq1[:], seq2[:]))
                sc.op("vector", lambda e: e.tensor_mul(st1[:], slog[:], sasg[:]))
                sc.op("vector", lambda e: e.tensor_scalar(out=st2[:], in0=sasg[:], scalar1=1.0, scalar2=-NEG,
                                                          op0=ALU.subtract, op1=ALU.mult))
                tok_wmt = sc.op("vector", lambda e, b=b: e.tensor_add(swmt[b][:], st1[:], st2[:]),
                                waits=[free_wmt[b]])
                tok_wmt_st = sc.op("sync", lambda e, b=b, r0=r0: e.dma_start(out=wmt_d[r0:r0 + 128, :],
                                                                            in_=swmt[b][:]),
                                   waits=[tok_wmt], inc=s_swm[b], dma=True)
                tok_wmT = sc.op("tensor", lambda e, b=b: e.transpose(pwm[:], swmt[b][:], sidf[:]),
                                waits=[tok_wmt, t_c, free_pwm], inc=s_t)
                free_wmt[b] = [tok_wmt_st, tok_wmT]
                tok_wmfcp = sc.op("vector", lambda e, i=i: e.tensor_copy(swmf[0:E0, i * 128:(i + 1) * 128], pwm[:]),
                                  waits=[tok_wmT])
                free_pwm = tok_wmfcp
                last_wmf_cp = tok_wmfcp
            sc.op("sync", lambda e: e.dma_start(out=wmf_d[:], in_=swmf[0:E0, :]), waits=[last_wmf_cp],
                  inc=s_swf, dma=True)
            sc.barrier()
            sc.emit_phase("A_ln0")

        with ExitStack() as cb:
            en2 = cb.enter_context
            sWT = en2(nc.sbuf_tensor(px + "sWT", [128, 8 * 2 * D], BF16)).ap()
            sxnb = [en2(nc.sbuf_tensor(f"{px}sxnb{j}", [128, 8 * 512], BF16)).ap() for j in range(2)]
            pkv = [en2(nc.psum_tensor(f"{px}pkv{j}", [128, 512], F32)).ap() for j in range(2)]

            s_nb = [sc.sem(f"nb{j}") for j in range(2)]
            t_wt = sc.op("gpsimd", lambda e: e.dma_start(out=sWT.rearrange("p (c k) -> p c k", c=8),
                                                         in_=wt_d.rearrange("(c p) k -> p c k", p=128)),
                         inc=s_wt, dma=True)
            free_xnb = [None, None]
            free_pkv = [None, None]
            pi = 0
            for tc in range(8):
                b = tc % 2
                tok_xn = None
                for q4 in range(4):
                    tok_xn = sc.op("sync", lambda e, b=b, tc=tc, q4=q4: e.dma_start(
                        out=sxnb[b].rearrange("p (c q t) -> p c q t", c=8, q=4)[:, :, q4, :],
                        in_=xnq_d[tc * 4 + q4].rearrange("c p t -> p c t")),
                        waits=[free_xnb[b]] if q4 == 0 else [], inc=s_nb[b], dma=True)
                last_mm = None
                for kk in range(8):
                    pb = pi % 2
                    pi += 1
                    mm = None
                    for c in range(8):
                        mm = sc.op("tensor", lambda e, b=b, kk=kk, c=c, pb=pb: e.matmul(
                            pkv[pb][:], sWT[:, c * 2048 + kk * 128: c * 2048 + kk * 128 + 128],
                            sxnb[b][:, c * 512:(c + 1) * 512], start=(c == 0), stop=(c == 7)),
                            waits=[tok_xn, t_wt, free_pkv[pb]] if c == 0 else [],
                            inc=s_t if c == 7 else None)
                    tok_ev = sc.op("vector", lambda e, kk=kk, tc=tc, pb=pb: e.tensor_copy(
                        K_fm[:, kk * T + tc * 512: kk * T + (tc + 1) * 512], pkv[pb][:]), waits=[mm])
                    free_pkv[pb] = tok_ev
                    last_mm = mm
                for q4 in range(4):
                    qt = tc * 4 + q4
                    for dvh in range(2):
                        pb = pi % 2
                        pi += 1
                        mm = None
                        for c in range(8):
                            mm = sc.op("tensor", lambda e, b=b, q4=q4, dvh=dvh, c=c, pb=pb: e.matmul(
                                pkv[pb][:], sxnb[b][:, c * 512 + q4 * 128: c * 512 + q4 * 128 + 128],
                                sWT[:, c * 2048 + D + dvh * 512: c * 2048 + D + dvh * 512 + 512],
                                start=(c == 0), stop=(c == 7)),
                                waits=[free_pkv[pb]] if c == 0 else [],
                                inc=s_t if c == 7 else None)
                        tok_ev = sc.op("vector", lambda e, qt=qt, dvh=dvh, pb=pb: e.tensor_copy(
                            V_tm[:, qt * 1024 + dvh * 512: qt * 1024 + (dvh + 1) * 512], pkv[pb][:]),
                            waits=[mm])
                        free_pkv[pb] = tok_ev
                        last_mm = mm
                free_xnb[b] = last_mm
            sc.barrier()
            sc.emit_phase("B_kv")

        with ExitStack() as cc:
            en2 = cc.enter_context
            sxq = [en2(nc.sbuf_tensor(f"{px}sxq{j}", [128, D], BF16)).ap() for j in range(2)]
            sx0 = [en2(nc.sbuf_tensor(f"{px}sx0{j}", [128, D], F32)).ap() for j in range(2)]
            sout = [en2(nc.sbuf_tensor(f"{px}sout{j}", [128, 512], F32)).ap() for j in range(2)]
            sprT = [en2(nc.sbuf_tensor(f"{px}sprT{j}", [128, 128], BF16)).ap() for j in range(2)]
            ssc = en2(nc.sbuf_tensor(px + "ssc", [128, T], F32)).ap()
            sprob = en2(nc.sbuf_tensor(px + "sprob", [128, T], BF16)).ap()
            sneg = en2(nc.sbuf_tensor(px + "sneg", [128, 1], F32)).ap()
            ssume = en2(nc.sbuf_tensor(px + "ssume", [128, 1], F32)).ap()
            srcpo = en2(nc.sbuf_tensor(px + "srcpo", [128, 1], F32)).ap()
            psc = [en2(nc.psum_tensor(f"{px}psc{j}", [128, 512], F32)).ap() for j in range(2)]
            ptr = [en2(nc.psum_tensor(f"{px}ptr{j}", [128, 128], BF16)).ap() for j in range(2)]
            pav = en2(nc.psum_tensor(px + "pav", [128, 1024], F32)).ap()

            s_q = [sc.sem(f"q{j}") for j in range(2)]
            s_x0c = [sc.sem(f"x0c{j}") for j in range(2)]
            s_so = [sc.sem(f"so{j}") for j in range(2)]

            free_xq = [None, None]
            free_x0b = [None, None]
            free_out = [None, None]
            free_psc = [None, None]
            free_ptr = [None, None]
            free_prT = [None, None]
            free_pav = None
            prev_prob = None
            prev_last_tr = None
            for qi in range(TT):
                b = qi % 2
                r0 = qi * 128
                nk = qi + 1
                L = nk * 128
                nch = (L + 511) // 512
                tok_q = sc.op("sync", lambda e, b=b, qi=qi: e.dma_start(
                    out=sxq[b].rearrange("p (c t) -> p c t", c=8),
                    in_=xnq_d[qi].rearrange("c p t -> p c t")),
                    waits=[free_xq[b]], inc=s_q[b], dma=True)
                tok_x0 = sc.op("sync", lambda e, b=b, r0=r0: e.dma_start(out=sx0[b][:], in_=x0xn_d[r0:r0 + 128, :]),
                               waits=[free_x0b[b]], inc=s_x0c[b], dma=True)
                ev_toks = []
                last_score_mm = None
                for ch in range(nch):
                    w = min(512, L - ch * 512)
                    pb = ch % 2
                    mm = None
                    for c in range(8):
                        mm = sc.op("tensor", lambda e, b=b, ch=ch, c=c, pb=pb, w=w: e.matmul(
                            psc[pb][:, 0:w], sxq[b][:, c * 128:(c + 1) * 128],
                            K_fm[:, c * T + ch * 512: c * T + ch * 512 + w],
                            start=(c == 0), stop=(c == 7)),
                            waits=[tok_q, free_psc[pb]] if c == 0 else [],
                            inc=s_t if c == 7 else None)
                    last_score_mm = mm
                    if ch == nch - 1:
                        off = r0 - ch * 512
                        if off > 0:
                            sc.op("vector", lambda e, ch=ch, pb=pb, off=off: e.tensor_copy(
                                ssc[:, ch * 512: ch * 512 + off], psc[pb][:, 0:off]), waits=[mm])
                        tok_ev = sc.op("vector", lambda e, ch=ch, pb=pb, off=off: e.tensor_add(
                            ssc[:, ch * 512 + off: ch * 512 + off + 128], psc[pb][:, off:off + 128], scm[:]),
                            waits=[mm, t_c])
                    else:
                        tok_ev = sc.op("vector", lambda e, ch=ch, pb=pb, w=w: e.tensor_copy(
                            ssc[:, ch * 512: ch * 512 + w], psc[pb][:, 0:w]), waits=[mm])
                    free_psc[pb] = tok_ev
                    ev_toks.append(tok_ev)
                tok_ngs = sc.op("vector", lambda e, L=L: e.reduce_max(sneg[:], ssc[:, 0:L], axis=AX.X, negate=True),
                                waits=[ev_toks[-1]])
                tok_prob = sc.op("scalar", lambda e, L=L: e.activation(out=sprob[:, 0:L], in_=ssc[:, 0:L],
                                                                      func=AF.Exp, bias=sneg[:, 0:1],
                                                                      accum_out=ssume[:]),
                                 waits=[tok_ngs, prev_prob, prev_last_tr])
                prev_prob = tok_prob
                tok_rcpo = sc.op("vector", lambda e: e.reciprocal(srcpo[:], ssume[:]), waits=[tok_prob])
                av_last = None
                tok_tr = None
                for k in range(nk):
                    tb = k % 2
                    tok_tr = sc.op("tensor", lambda e, k=k, tb=tb: e.transpose(ptr[tb][:],
                                                                               sprob[:, k * 128:(k + 1) * 128],
                                                                               sidb[:]),
                                   waits=([tok_prob] if k == 0 else []) + [free_ptr[tb]], inc=s_t)
                    tok_cp = sc.op("vector", lambda e, tb=tb: e.tensor_copy(sprT[tb][:], ptr[tb][:]),
                                   waits=[tok_tr, free_prT[tb]])
                    free_ptr[tb] = tok_cp
                    mm2 = None
                    for half in range(2):
                        mm2 = sc.op("tensor", lambda e, k=k, tb=tb, half=half, nk=nk: e.matmul(
                            pav[:, half * 512:(half + 1) * 512], sprT[tb][:],
                            V_tm[:, k * 1024 + half * 512: k * 1024 + half * 512 + 512],
                            start=(k == 0), stop=(k == nk - 1)),
                            waits=[tok_cp, free_pav] if (k == 0 and half == 0) else [tok_cp],
                            inc=s_t if (k == nk - 1 and half == 1) else None)
                    free_prT[tb] = mm2 if mm2 is not None else free_prT[tb]
                    if k == nk - 1:
                        av_last = mm2
                prev_last_tr = tok_tr
                out_toks = []
                for half in range(2):
                    ob = half
                    tok_o = sc.op("vector", lambda e, b=b, half=half, ob=ob: e.scalar_tensor_tensor(
                        out=sout[ob][:], in0=pav[:, half * 512:(half + 1) * 512], scalar=srcpo[:, 0:1],
                        in1=sx0[b][:, half * 512:(half + 1) * 512], op0=ALU.mult, op1=ALU.add),
                        waits=[av_last, tok_rcpo, tok_x0, free_out[ob]])
                    tok_os = sc.op("sync", lambda e, r0=r0, half=half, ob=ob: e.dma_start(
                        out=x1p_d[r0:r0 + 128, half * 512:(half + 1) * 512], in_=sout[ob][:]),
                        waits=[tok_o], inc=s_so[ob], dma=True)
                    free_out[ob] = tok_os
                    out_toks.append(tok_o)
                free_pav = out_toks
                free_xq[b] = last_score_mm
                free_x0b[b] = out_toks[-1]
            sc.barrier()
            sc.emit_phase("C_attn")


def build_k1(debug=False):
    nc = bass.Bass("TRN2", target_bir_lowering=False)
    x_d = nc.dram_tensor("x", [T, D], F32, kind="ExternalInput").ap()
    wt_d = nc.dram_tensor("wt", [D, 2 * D], F32, kind="ExternalInput").ap()
    gw_d = nc.dram_tensor("gw", [D, E0], F32, kind="ExternalInput").ap()
    gb_d = nc.dram_tensor("gb", [1, E0], F32, kind="ExternalInput").ap()
    idb_d = nc.dram_tensor("idb", [128, 128], BF16, kind="ExternalInput").ap()
    idf_d = nc.dram_tensor("idf", [128, 128], F32, kind="ExternalInput").ap()
    cm_d = nc.dram_tensor("cm", [128, 128], F32, kind="ExternalInput").ap()
    x1p_d = nc.dram_tensor("x1p", [T, D], F32, kind="ExternalOutput").ap()
    xnq_d = nc.dram_tensor("xnq", [TT, 8, 128, 128], BF16, kind="ExternalOutput").ap()
    wmf_d = nc.dram_tensor("wmf", [E0, T], F32, kind="ExternalOutput").ap()
    wmt_d = nc.dram_tensor("wmt", [T, E0], F32, kind="ExternalOutput").ap()
    x0xn_d = nc.dram_tensor("x0xn", [T, D], F32,
                            kind="ExternalOutput" if debug else "Internal").ap()
    sc = Sched(nc)
    emit_k1(nc, sc, "a_", x_d, wt_d, gw_d, gb_d, idb_d, idf_d, cm_d,
            x1p_d, xnq_d, wmf_d, wmt_d, x0xn_d)
    return nc



def emit_k2(nc, sc, px, xnq_d, wmt_d, wgf_d, x1p_d, ew1_d, ew3_d, ew2_d,
            sw1_d, sw3_d, sw2_d, ssel_d, stsel_d, rsel_d, ones_d, idb_d, idf_d,
            x1_d, wfin_d, ew1b_d, ew3b_d, ew2b_d, sw1b_d, sw3b_d, sw2b_d,
            experts=None, dump_acc=False, use_scratch=True, acc_dbg=None,
            wgf_wait=None):
    with ExitStack() as ctx:
        en = ctx.enter_context
        sidb = en(nc.sbuf_tensor(px + "sidb", [128, 128], BF16)).ap()
        sidf = en(nc.sbuf_tensor(px + "sidf", [128, 128], F32)).ap()

        s_t = sc.sem("t")
        s_c = sc.sem("c")
        s_cast = sc.sem("cast")
        s_st = sc.sem("st")

        sc.op("sync", lambda e: e.dma_start(out=sidb[:], in_=idb_d[:]), inc=s_c, dma=True)
        sc.op("sync", lambda e: e.dma_start(out=sidf[:], in_=idf_d[:]), inc=s_c, dma=True)

        if use_scratch:
            for e_i in range(E0):
                for src, dst in ((ew1_d, ew1b_d), (ew3_d, ew3b_d), (ew2_d, ew2b_d)):
                    sc.op("gpsimd", lambda e, s=src, d_=dst, e_i=e_i: e.dma_start(out=d_[e_i], in_=s[e_i]),
                          inc=s_cast, dma=True)
            for s_i in range(2):
                for src, dst in ((sw1_d, sw1b_d), (sw3_d, sw3b_d), (sw2_d, sw2b_d)):
                    sc.op("gpsimd", lambda e, s=src, d_=dst, s_i=s_i: e.dma_start(out=d_[s_i], in_=s[s_i]),
                          inc=s_cast, dma=True)
        tok_cast_all = (s_cast, s_cast.c) if use_scratch else None

        with ExitStack() as cd:
            en2 = cd.enter_context
            swg = en2(nc.sbuf_tensor(px + "swg", [128, T], F32)).ap()
            sgnscr = en2(nc.sbuf_tensor(px + "sgnscr", [128, T], BF16)).ap()
            sS = en2(nc.sbuf_tensor(px + "sS", [128, E0], F32)).ap()
            sST = en2(nc.sbuf_tensor(px + "sST", [E0, 128], F32)).ap()
            sR = en2(nc.sbuf_tensor(px + "sR", [128, E0], F32)).ap()
            sone = en2(nc.sbuf_tensor(px + "sone", [1, 128], F32)).ap()
            sb29 = en2(nc.sbuf_tensor(px + "sb29", [128, 1], F32)).ap()
            ssgn = en2(nc.sbuf_tensor(px + "ssgn", [128, 1], F32)).ap()
            scnt = en2(nc.sbuf_tensor(px + "scnt", [128, 1], F32)).ap()
            scntE = en2(nc.sbuf_tensor(px + "scntE", [E0, 1], F32)).ap()
            stgt16 = en2(nc.sbuf_tensor(px + "stgt16", [E0, 1], F32)).ap()
            slo = en2(nc.sbuf_tensor(px + "slo", [128, 1], F32)).ap()
            shi = en2(nc.sbuf_tensor(px + "shi", [128, 1], F32)).ap()
            stmp = en2(nc.sbuf_tensor(px + "stmp", [128, 1], F32)).ap()
            smid = en2(nc.sbuf_tensor(px + "smid", [128, 1], F32)).ap()
            snmid = en2(nc.sbuf_tensor(px + "snmid", [128, 1], F32)).ap()
            spredE = en2(nc.sbuf_tensor(px + "spredE", [E0, 1], F32)).ap()
            spred = en2(nc.sbuf_tensor(px + "spred", [128, 1], F32)).ap()
            sd1 = en2(nc.sbuf_tensor(px + "sd1", [128, 1], F32)).ap()
            sd2 = en2(nc.sbuf_tensor(px + "sd2", [128, 1], F32)).ap()
            strow = en2(nc.sbuf_tensor(px + "strow", [1, E0], F32)).ap()
            sthr = en2(nc.sbuf_tensor(px + "sthr", [128, E0], F32)).ap()
            swmtt = [en2(nc.sbuf_tensor(f"{px}swmtt{j}", [128, E0], F32)).ap() for j in range(2)]
            sselm = en2(nc.sbuf_tensor(px + "sselm", [128, E0], F32)).ap()
            swfin = [en2(nc.sbuf_tensor(f"{px}swfin{j}", [128, E0], F32)).ap() for j in range(2)]
            swf = en2(nc.sbuf_tensor(px + "swf", [128, T], F32)).ap()
            pcnt = en2(nc.psum_tensor(px + "pcnt", [E0, 1], F32)).ap()
            ppred = en2(nc.psum_tensor(px + "ppred", [128, 1], F32)).ap()
            ptrow = en2(nc.psum_tensor(px + "ptrow", [1, E0], F32)).ap()
            ptbc = en2(nc.psum_tensor(px + "ptbc", [128, E0], F32)).ap()
            pwt = [en2(nc.psum_tensor(f"{px}pwt{j}", [E0, 128], F32)).ap() for j in range(2)]

            s_wm = [sc.sem(f"wm{j}") for j in range(2)]
            s_wf = sc.sem("wfst")

            sc.op("sync", lambda e: e.dma_start(out=swg[:], in_=wgf_d[:]), waits=[wgf_wait], inc=s_c, dma=True)
            sc.op("sync", lambda e: e.dma_start(out=sS[:], in_=ssel_d[:]), inc=s_c, dma=True)
            sc.op("sync", lambda e: e.dma_start(out=sST[:], in_=stsel_d[:]), inc=s_c, dma=True)
            sc.op("sync", lambda e: e.dma_start(out=sR[:], in_=rsel_d[:]), inc=s_c, dma=True)
            sc.op("sync", lambda e: e.dma_start(out=sone[:], in_=ones_d[:]), inc=s_c, dma=True)
            t_c = (s_c, s_c.c)

            sc.op("vector", lambda e: e.memset(sb29[:], 1e29))
            tok_mx0 = None
            for hfi in range(2):
                sc.op("vector", lambda e, hfi=hfi: e.tensor_scalar(
                    out=sgnscr.bitcast(F32)[:, 0:T // 2], in0=swg[:, hfi * (T // 2):(hfi + 1) * (T // 2)],
                    scalar1=-1e29, scalar2=None, op0=ALU.is_gt), waits=[t_c])
                sc.op("vector", lambda e, hfi=hfi: e.tensor_mul(
                    sgnscr.bitcast(F32)[:, 0:T // 2], sgnscr.bitcast(F32)[:, 0:T // 2],
                    swg[:, hfi * (T // 2):(hfi + 1) * (T // 2)]))
                tok_mx0 = sc.op("vector", lambda e, hfi=hfi: e.reduce_max(
                    smid[:] if hfi == 0 else snmid[:], sgnscr.bitcast(F32)[:, 0:T // 2], axis=AX.X,
                    apply_absolute_value=True))
            tok_mx0 = sc.op("vector", lambda e: e.tensor_max(shi[:], smid[:], snmid[:]))
            mmx = sc.op("tensor", lambda e: e.matmul(pcnt[:], sS[:], shi[:], start=True, stop=True),
                        waits=[t_c, tok_mx0], inc=s_t)
            tok_hx = sc.op("vector", lambda e: e.tensor_copy(scntE[:], pcnt[:]), waits=[mmx])
            tok_hadd = sc.op("vector", lambda e: e.tensor_scalar_add(scntE[:], scntE[:], 1.0))
            mmx2 = sc.op("tensor", lambda e: e.matmul(ppred[:], sST[:], scntE[:], start=True, stop=True),
                         waits=[tok_hadd], inc=s_t)
            tok_hb = sc.op("vector", lambda e: e.tensor_copy(shi[:], ppred[:]), waits=[mmx2])
            tok_ms = sc.op("vector", lambda e: e.tensor_scalar_mul(slo[:], shi[:], -1.0))
            tok_sgn0 = sc.op("scalar", lambda e: e.activation(out=sgnscr[:], in_=swg[:], func=AF.Sign,
                                                             bias=sb29[:, 0:1], accum_out=ssgn[:]),
                             waits=[t_c, tok_ms])
            tok_cnt0 = sc.op("vector", lambda e: e.tensor_scalar(out=scnt[:], in0=ssgn[:], scalar1=float(T),
                                                                scalar2=0.5, op0=ALU.add, op1=ALU.mult),
                             waits=[tok_sgn0])
            mmc = sc.op("tensor", lambda e: e.matmul(pcnt[:], sS[:], scnt[:], start=True, stop=True),
                        waits=[t_c, tok_cnt0, tok_hx], inc=s_t)
            tok_cE = sc.op("vector", lambda e: e.tensor_copy(scntE[:], pcnt[:]), waits=[mmc])
            tok_tgt = sc.op("vector", lambda e: e.tensor_scalar_min(stgt16[:], scntE[:], float(CAP)))
            free_pcnt = tok_cE
            tok_upd = tok_tgt
            free_ppred = None
            for it in range(N_BIS):
                sc.op("vector", lambda e: e.tensor_add(stmp[:], slo[:], shi[:]), waits=[tok_upd])
                sc.op("vector", lambda e: e.tensor_scalar_mul(smid[:], stmp[:], 0.5))
                tok_nm = sc.op("vector", lambda e: e.tensor_scalar_mul(snmid[:], stmp[:], -0.5))
                tok_sgn = sc.op("scalar", lambda e: e.activation(out=sgnscr[:], in_=swg[:], func=AF.Sign,
                                                                bias=snmid[:, 0:1], accum_out=ssgn[:]),
                                waits=[tok_nm])
                tok_cnt = sc.op("vector", lambda e: e.tensor_scalar(out=scnt[:], in0=ssgn[:], scalar1=float(T),
                                                                   scalar2=0.5, op0=ALU.add, op1=ALU.mult),
                               waits=[tok_sgn])
                mmc = sc.op("tensor", lambda e: e.matmul(pcnt[:], sS[:], scnt[:], start=True, stop=True),
                            waits=[tok_cnt, free_pcnt], inc=s_t)
                tok_cE = sc.op("vector", lambda e: e.tensor_copy(scntE[:], pcnt[:]), waits=[mmc])
                free_pcnt = tok_cE
                tok_pE = sc.op("vector", lambda e: e.tensor_tensor(spredE[:], scntE[:], stgt16[:], ALU.is_ge))
                mmp = sc.op("tensor", lambda e: e.matmul(ppred[:], sST[:], spredE[:], start=True, stop=True),
                            waits=[tok_pE, free_ppred if free_ppred is not None else tok_hb], inc=s_t)
                tok_pr = sc.op("vector", lambda e: e.tensor_copy(spred[:], ppred[:]), waits=[mmp])
                free_ppred = tok_pr
                sc.op("vector", lambda e: e.tensor_sub(sd1[:], smid[:], slo[:]))
                sc.op("vector", lambda e: e.tensor_mul(sd1[:], spred[:], sd1[:]))
                sc.op("vector", lambda e: e.tensor_add(slo[:], slo[:], sd1[:]))
                sc.op("vector", lambda e: e.tensor_sub(sd2[:], shi[:], smid[:]))
                sc.op("vector", lambda e: e.tensor_mul(sd2[:], spred[:], sd2[:]))
                tok_upd = sc.op("vector", lambda e: e.tensor_add(shi[:], smid[:], sd2[:]))
            mmr = sc.op("tensor", lambda e: e.matmul(ptrow[:], slo[:], sR[:], start=True, stop=True),
                        waits=[tok_upd], inc=s_t)
            tok_tr = sc.op("vector", lambda e: e.tensor_copy(strow[:], ptrow[:]), waits=[mmr])
            mmb = sc.op("tensor", lambda e: e.matmul(ptbc[:], sone[:], strow[:], start=True, stop=True),
                        waits=[tok_tr], inc=s_t)
            tok_thr = sc.op("vector", lambda e: e.tensor_copy(sthr[:], ptbc[:]), waits=[mmb])
            free_wmtt = [None, None]
            free_wfin = [None, None]
            free_pwt = [None, None]
            last_wf = None
            for i in range(TT):
                b = i % 2
                tok_w = sc.op("sync", lambda e, b=b, i=i: e.dma_start(out=swmtt[b][:],
                                                                     in_=wmt_d[i * 128:(i + 1) * 128, :]),
                              waits=[free_wmtt[b]], inc=s_wm[b], dma=True)
                sc.op("vector", lambda e, b=b: e.tensor_tensor(sselm[:], swmtt[b][:], sthr[:], ALU.is_ge),
                      waits=[tok_w, tok_thr])
                tok_wf = sc.op("vector", lambda e, b=b: e.tensor_mul(swfin[b][:], sselm[:], swmtt[b][:]),
                               waits=[free_wfin[b]])
                free_wmtt[b] = tok_wf
                tok_T = sc.op("tensor", lambda e, b=b: e.transpose(pwt[b][:], swfin[b][:], sidf[:]),
                              waits=[tok_wf, t_c, free_pwt[b]], inc=s_t)
                free_wfin[b] = tok_T
                tok_cp = sc.op("vector", lambda e, i=i, b=b: e.tensor_copy(swf[0:E0, i * 128:(i + 1) * 128],
                                                                           pwt[b][:]),
                               waits=[tok_T])
                free_pwt[b] = tok_cp
                last_wf = tok_cp
            sc.op("sync", lambda e: e.dma_start(out=wfin_d[:], in_=swf[0:E0, :]), waits=[last_wf],
                  inc=s_wf, dma=True)
            sc.barrier()
            sc.emit_phase("D_bisect")

        with ExitStack() as cf:
            en2 = cf.enter_context
            sacc = en2(nc.sbuf_tensor(px + "sacc", [128, 8 * T], BF16)).ap()
            cf2 = en2(ExitStack())
            en3 = cf2.enter_context
            sw1s = en3(nc.sbuf_tensor(px + "sw1s", [128, 8 * H0], BF16)).ap()
            sw3s = en3(nc.sbuf_tensor(px + "sw3s", [128, 8 * H0], BF16)).ap()
            sw2s = en3(nc.sbuf_tensor(px + "sw2s", [128, 16 * D], BF16)).ap()
            sxnb = [en3(nc.sbuf_tensor(f"{px}sxnb{j}", [128, 8 * 512], BF16)).ap() for j in range(2)]
            sbc = [en3(nc.sbuf_tensor(f"{px}sbc{j}", [128, 512], F32)).ap() for j in range(2)]
            sbc05 = en3(nc.sbuf_tensor(px + "sbc05", [128, 512], F32)).ap()
            sG = en3(nc.sbuf_tensor(px + "sG", [128, 16 * 512], BF16)).ap()
            sg1 = en3(nc.sbuf_tensor(px + "sg1", [128, 512], BF16)).ap()
            sg3 = en3(nc.sbuf_tensor(px + "sg3", [128, 512], BF16)).ap()
            ph = [en3(nc.psum_tensor(f"{px}ph{j}", [128, 512], F32)).ap() for j in range(2)]
            po = [en3(nc.psum_tensor(f"{px}po{j}", [128, 512], F32)).ap() for j in range(2)]

            s_nb = [sc.sem(f"nb{j}") for j in range(2)]
            s_bc = [sc.sem(f"bc{j}") for j in range(2)]
            s_w1 = sc.sem("w1")
            s_w3 = sc.sem("w3")
            s_w2 = sc.sem("w2")

            tok_acc0 = sc.op("vector", lambda e: e.memset(sacc[:], 0.0))
            sc.op("vector", lambda e: e.memset(sbc05[:], 0.5))
            free_xnb = [None, None]
            free_bc = [None, None]
            free_ph = [None, None]
            free_po = [None, None]
            free_w = None
            acc_tok = tok_acc0
            phi = 0
            expert_list = list(range(E0 + 2)) if experts is None else experts
            for ei in expert_list:
                routed = ei < E0
                if use_scratch:
                    w1src = ew1b_d[ei] if routed else sw1b_d[ei - E0]
                    w3src = ew3b_d[ei] if routed else sw3b_d[ei - E0]
                    w2src = ew2b_d[ei] if routed else sw2b_d[ei - E0]
                else:
                    w1src = ew1_d[ei] if routed else sw1_d[ei - E0]
                    w3src = ew3_d[ei] if routed else sw3_d[ei - E0]
                    w2src = ew2_d[ei] if routed else sw2_d[ei - E0]
                t_w1 = sc.op("gpsimd", lambda e, s=w1src: e.dma_start(
                    out=sw1s.rearrange("p (c h) -> p c h", c=8), in_=s.rearrange("(c p) h -> p c h", p=128)),
                    waits=[tok_cast_all, free_w], inc=s_w1, dma=True)
                t_w3 = sc.op("gpsimd", lambda e, s=w3src: e.dma_start(
                    out=sw3s.rearrange("p (c h) -> p c h", c=8), in_=s.rearrange("(c p) h -> p c h", p=128)),
                    inc=s_w3, dma=True)
                t_w2 = sc.op("gpsimd", lambda e, s=w2src: e.dma_start(
                    out=sw2s.rearrange("p (h o) -> p h o", h=16), in_=s.rearrange("(h p) o -> p h o", p=128)),
                    inc=s_w2, dma=True)
                last_mm_e = None
                for tc in range(8):
                    bx = tc % 2
                    tok_xn = None
                    for q4 in range(4):
                        tok_xn = sc.op("sync", lambda e, bx=bx, tc=tc, q4=q4: e.dma_start(
                            out=sxnb[bx].rearrange("p (c q t) -> p c q t", c=8, q=4)[:, :, q4, :],
                            in_=xnq_d[tc * 4 + q4].rearrange("c p t -> p c t")),
                            waits=[free_xnb[bx]] if q4 == 0 else [], inc=s_nb[bx], dma=True)
                    if routed:
                        tok_bc = sc.op("sync", lambda e, bx=bx, ei=ei, tc=tc: e.dma_start(
                            out=sbc[bx][:],
                            in_=wfin_d[ei:ei + 1, tc * 512:(tc + 1) * 512].broadcast_to([128, 512])),
                            waits=[free_bc[bx]], inc=s_bc[bx], dma=True)
                        bc_ap = sbc[bx]
                    else:
                        tok_bc = None
                        bc_ap = sbc05
                    g_toks = []
                    prev_G = None
                    for h in range(16):
                        pb = phi % 2
                        phi += 1
                        mm1 = None
                        for c in range(8):
                            mm1 = sc.op("tensor", lambda e, h=h, c=c, pb=pb, bx=bx: e.matmul(
                                ph[pb][:], sw1s[:, c * H0 + h * 128: c * H0 + h * 128 + 128],
                                sxnb[bx][:, c * 512:(c + 1) * 512], start=(c == 0), stop=(c == 7)),
                                waits=[t_w1, tok_xn, free_ph[pb]] if c == 0 else [],
                                inc=s_t if c == 7 else None)
                        tok_g1 = sc.op("scalar", lambda e, pb=pb: e.activation(out=sg1[:], in_=ph[pb][:],
                                                                              func=AF.Silu),
                                       waits=[mm1, prev_G])
                        free_ph[pb] = tok_g1
                        pb3 = phi % 2
                        phi += 1
                        mm3 = None
                        for c in range(8):
                            mm3 = sc.op("tensor", lambda e, h=h, c=c, pb3=pb3, bx=bx: e.matmul(
                                ph[pb3][:], sw3s[:, c * H0 + h * 128: c * H0 + h * 128 + 128],
                                sxnb[bx][:, c * 512:(c + 1) * 512], start=(c == 0), stop=(c == 7)),
                                waits=[t_w3, free_ph[pb3]] if c == 0 else [],
                                inc=s_t if c == 7 else None)
                        tok_g3 = sc.op("vector", lambda e, pb3=pb3, bc_ap=bc_ap: e.tensor_tensor(
                            sg3[:], ph[pb3][:], bc_ap[:], ALU.mult),
                            waits=[mm3, tok_bc])
                        free_ph[pb3] = tok_g3
                        tok_G = sc.op("vector", lambda e, h=h: e.tensor_mul(sG[:, h * 512:(h + 1) * 512],
                                                                            sg1[:], sg3[:]),
                                      waits=[tok_g1])
                        prev_G = tok_G
                        g_toks.append(tok_G)
                    for oc in range(8):
                        pb = phi % 2
                        phi += 1
                        mm2 = None
                        for h in range(16):
                            mm2 = sc.op("tensor", lambda e, oc=oc, h=h, pb=pb: e.matmul(
                                po[pb][:], sw2s[:, h * D + oc * 128: h * D + oc * 128 + 128],
                                sG[:, h * 512:(h + 1) * 512], start=(h == 0), stop=(h == 15)),
                                waits=[t_w2, g_toks[-1], free_po[pb]] if h == 0 else [],
                                inc=s_t if h == 15 else None)
                        acc_tok = sc.op("vector", lambda e, oc=oc, tc=tc, pb=pb: e.tensor_tensor(
                            sacc[:, oc * T + tc * 512: oc * T + (tc + 1) * 512],
                            sacc[:, oc * T + tc * 512: oc * T + (tc + 1) * 512], po[pb][:], ALU.add),
                            waits=[mm2, acc_tok])
                        free_po[pb] = acc_tok
                        last_mm_e = mm2
                    free_xnb[bx] = last_mm_e
                    free_bc[bx] = acc_tok
                free_w = last_mm_e
            if dump_acc:
                sc.op("sync", lambda e: e.dma_start(out=acc_dbg[:], in_=sacc[:]), waits=[acc_tok],
                      inc=s_st, dma=True)
            sc.barrier()
            sc.emit_phase("F_experts")
            cf2.close()

            with ExitStack() as cg:
                en4 = cg.enter_context
                sx1p = [en4(nc.sbuf_tensor(f"{px}sx1p{j}", [128, D], F32)).ap() for j in range(2)]
                sx1o = [en4(nc.sbuf_tensor(f"{px}sx1o{j}", [128, D], F32)).ap() for j in range(2)]
                pacc = [en4(nc.psum_tensor(f"{px}pacc{j}", [128, 128], BF16)).ap() for j in range(2)]
                s_xp = [sc.sem(f"xp{j}") for j in range(2)]
                s_xo = [sc.sem(f"xo{j}") for j in range(2)]
                free_p = [None, None]
                free_o = [None, None]
                free_i = [None, None]
                for i in range(TT):
                    b = i % 2
                    tok_x1p = sc.op("sync", lambda e, b=b, i=i: e.dma_start(out=sx1p[b][:],
                                                                           in_=x1p_d[i * 128:(i + 1) * 128, :]),
                                    waits=[free_i[b]], inc=s_xp[b], dma=True)
                    add_toks = []
                    for c in range(8):
                        pb = c % 2
                        tok_T = sc.op("tensor", lambda e, i=i, c=c, pb=pb: e.transpose(
                            pacc[pb][:], sacc[:, c * T + i * 128: c * T + i * 128 + 128], sidb[:]),
                            waits=[(s_c, 32), free_p[pb]] + ([acc_tok] if i == 0 and c == 0 else []), inc=s_t)
                        tok_add = sc.op("vector", lambda e, b=b, c=c, pb=pb: e.tensor_tensor(
                            sx1o[b][:, c * 128:(c + 1) * 128], sx1p[b][:, c * 128:(c + 1) * 128],
                            pacc[pb][:], ALU.add),
                            waits=[tok_T, tok_x1p] + ([free_o[b]] if c == 0 else []))
                        free_p[pb] = tok_add
                        add_toks.append(tok_add)
                    tok_st2 = sc.op("sync", lambda e, b=b, i=i: e.dma_start(out=x1_d[i * 128:(i + 1) * 128, :],
                                                                           in_=sx1o[b][:]),
                                    waits=[add_toks[-1]], inc=s_xo[b], dma=True)
                    free_o[b] = tok_st2
                    free_i[b] = add_toks[-1]
                sc.barrier()
                sc.emit_phase("G_final")


def build_k2(experts=None, dump_acc=False, use_scratch=True):
    nc = bass.Bass("TRN2", target_bir_lowering=False)
    xnq_d = nc.dram_tensor("xnq", [TT, 8, 128, 128], BF16, kind="ExternalInput").ap()
    wmt_d = nc.dram_tensor("wmtd", [T, E0], F32, kind="ExternalInput").ap()
    wgf_d = nc.dram_tensor("wgf", [128, T], F32, kind="ExternalInput").ap()
    x1p_d = nc.dram_tensor("x1p", [T, D], F32, kind="ExternalInput").ap()
    ew1_d = nc.dram_tensor("ew1", [E0, D, H0], F32, kind="ExternalInput").ap()
    ew3_d = nc.dram_tensor("ew3", [E0, D, H0], F32, kind="ExternalInput").ap()
    ew2_d = nc.dram_tensor("ew2", [E0, H0, D], F32, kind="ExternalInput").ap()
    sw1_d = nc.dram_tensor("sw1", [2, D, H0], F32, kind="ExternalInput").ap()
    sw3_d = nc.dram_tensor("sw3", [2, D, H0], F32, kind="ExternalInput").ap()
    sw2_d = nc.dram_tensor("sw2", [2, H0, D], F32, kind="ExternalInput").ap()
    ssel_d = nc.dram_tensor("Ssel", [128, E0], F32, kind="ExternalInput").ap()
    stsel_d = nc.dram_tensor("STsel", [E0, 128], F32, kind="ExternalInput").ap()
    rsel_d = nc.dram_tensor("Rsel", [128, E0], F32, kind="ExternalInput").ap()
    ones_d = nc.dram_tensor("ones_f", [1, 128], F32, kind="ExternalInput").ap()
    idb_d = nc.dram_tensor("idb", [128, 128], BF16, kind="ExternalInput").ap()
    idf_d = nc.dram_tensor("idf", [128, 128], F32, kind="ExternalInput").ap()
    x1_d = nc.dram_tensor("x1", [T, D], F32, kind="ExternalOutput").ap()
    wfin_d = nc.dram_tensor("wfin", [E0, T], F32, kind="ExternalOutput").ap()
    acc_dbg = nc.dram_tensor("accdbg", [128, 8 * T], BF16, kind="ExternalOutput").ap() if dump_acc else None
    ew1b_d = nc.dram_tensor("ew1b", [E0, D, H0], BF16, kind="Internal").ap()
    ew3b_d = nc.dram_tensor("ew3b", [E0, D, H0], BF16, kind="Internal").ap()
    ew2b_d = nc.dram_tensor("ew2b", [E0, H0, D], BF16, kind="Internal").ap()
    sw1b_d = nc.dram_tensor("sw1b", [2, D, H0], BF16, kind="Internal").ap()
    sw3b_d = nc.dram_tensor("sw3b", [2, D, H0], BF16, kind="Internal").ap()
    sw2b_d = nc.dram_tensor("sw2b", [2, H0, D], BF16, kind="Internal").ap()
    sc = Sched(nc)
    emit_k2(nc, sc, "b_", xnq_d, wmt_d, wgf_d, x1p_d, ew1_d, ew3_d, ew2_d,
            sw1_d, sw3_d, sw2_d, ssel_d, stsel_d, rsel_d, ones_d, idb_d, idf_d,
            x1_d, wfin_d, ew1b_d, ew3b_d, ew2b_d, sw1b_d, sw3b_d, sw2b_d,
            experts=experts, dump_acc=dump_acc, use_scratch=use_scratch,
            acc_dbg=acc_dbg)
    return nc


def k2_inputs(inputs, res1):
    idb, idf, cm = _host_consts()
    S = np.zeros((128, E0), np.float32)
    for r in range(NCORE):
        for e_i in range(E0):
            S[r * E0 + e_i, e_i] = 1.0
    ST = np.ascontiguousarray(S.T)
    ones_f = np.ones((1, 128), np.float32)
    wgf = np.concatenate([np.asarray(res1.results[b]["wmf"]) for b in range(NCORE)], axis=0)
    maps = []
    for b in range(NCORE):
        R = np.zeros((128, E0), np.float32)
        for e_i in range(E0):
            R[b * E0 + e_i, e_i] = 1.0
        maps.append({
            "xnq": np.asarray(res1.results[b]["xnq"]),
            "wmtd": np.asarray(res1.results[b]["wmt"]),
            "wgf": wgf,
            "x1p": np.asarray(res1.results[b]["x1p"]),
            "ew1": inputs["L0_ffn_ew1"], "ew3": inputs["L0_ffn_ew3"], "ew2": inputs["L0_ffn_ew2"],
            "sw1": inputs["L0_ffn_sw1"], "sw3": inputs["L0_ffn_sw3"], "sw2": inputs["L0_ffn_sw2"],
            "Ssel": S, "STsel": ST, "Rsel": R, "ones_f": ones_f, "idb": idb, "idf": idf,
        })
    return maps



JTOT = 1367
JPAD = 1408
NH = 16
DH = 64
SCALE = float(D) ** -0.5


def _lq(qi):
    win0 = (qi * 128) // 3 + 2
    return min(JTOT, win0 + 64), win0


def emit_k3(nc, sc, px, x1_d, wr_d, wqT_d, wcp_d, bcv_d, nlk_d, nlv_d,
            woT_d, bo_d, cml_d, idb_d, idf_d, x2p_d, xnq1_d, rsc_d, qq_d, attn_d,
            xn1tm_d=None, pmax_d=None):
    with ExitStack() as ctx:
        en = ctx.enter_context
        sidb = en(nc.sbuf_tensor(px + "sidb", [128, 128], BF16)).ap()
        sidf = en(nc.sbuf_tensor(px + "sidf", [128, 128], F32)).ap()
        kc_fm = en(nc.sbuf_tensor(px + "kc_fm", [128, 8 * JPAD], BF16)).ap()
        vc_tm = en(nc.sbuf_tensor(px + "vc_tm", [128, 11 * 1024], BF16)).ap()
        swcp = en(nc.sbuf_tensor(px + "swcp", [128, 3 * 8 * DH], BF16)).ap()
        sbcv = en(nc.sbuf_tensor(px + "sbcv", [128, 8], F32)).ap()
        snlk = en(nc.sbuf_tensor(px + "snlk", [128, 8], BF16)).ap()
        snlv = en(nc.sbuf_tensor(px + "snlv", [128, 8], BF16)).ap()

        s_t = sc.sem("t")
        s_c = sc.sem("c")

        sc.op("sync", lambda e: e.dma_start(out=sidb[:], in_=idb_d[:]), inc=s_c, dma=True)
        sc.op("sync", lambda e: e.dma_start(out=sidf[:], in_=idf_d[:]), inc=s_c, dma=True)
        sc.op("sync", lambda e: e.dma_start(out=sbcv[:], in_=bcv_d[:]), inc=s_c, dma=True)
        sc.op("gpsimd", lambda e: e.dma_start(out=swcp[:], in_=wcp_d[:]), inc=s_c, dma=True)
        sc.op("gpsimd", lambda e: e.dma_start(out=snlk[:], in_=nlk_d[:]), inc=s_c, dma=True)
        sc.op("gpsimd", lambda e: e.dma_start(out=snlv[:], in_=nlv_d[:]), inc=s_c, dma=True)
        t_c = (s_c, s_c.c)

        with ExitStack() as ch:
            en2 = ch.enter_context
            sx = [en2(nc.sbuf_tensor(f"{px}sx{j}", [128, D], F32)).ap() for j in range(2)]
            sxh = [en2(nc.sbuf_tensor(f"{px}sxh{j}", [128, D], BF16)).ap() for j in range(2)]
            stg = [en2(nc.sbuf_tensor(f"{px}stg{j}", [128, D], BF16)).ap() for j in range(2)]
            srt = [en2(nc.sbuf_tensor(f"{px}srt{j}", [128, E1], F32)).ap() for j in range(2)]
            swr = en2(nc.sbuf_tensor(px + "swr", [128, 8 * E1], BF16)).ap()
            srf = en2(nc.sbuf_tensor(px + "srf", [128, T], F32)).ap()
            sxn = en2(nc.sbuf_tensor(px + "sxn", [128, D], F32)).ap()
            sqscr = en2(nc.sbuf_tensor(px + "sqscr", [128, D], F32)).ap()
            seps = en2(nc.sbuf_tensor(px + "seps", [128, 1], F32)).ap()
            ssum = en2(nc.sbuf_tensor(px + "ssum", [128, 1], F32)).ap()
            ssumsq = en2(nc.sbuf_tensor(px + "ssumsq", [128, 1], F32)).ap()
            smean = en2(nc.sbuf_tensor(px + "smean", [128, 1], F32)).ap()
            sm2t = en2(nc.sbuf_tensor(px + "sm2t", [128, 1], F32)).ap()
            svar = en2(nc.sbuf_tensor(px + "svar", [128, 1], F32)).ap()
            srstd = en2(nc.sbuf_tensor(px + "srstd", [128, 1], F32)).ap()
            slog = en2(nc.sbuf_tensor(px + "slog", [128, E1], F32)).ap()
            smr = en2(nc.sbuf_tensor(px + "smr", [128, 1], F32)).ap()
            sexp = en2(nc.sbuf_tensor(px + "sexp", [128, E1], F32)).ap()
            sesum = en2(nc.sbuf_tensor(px + "sesum", [128, 1], F32)).ap()
            srcp = en2(nc.sbuf_tensor(px + "srcp", [128, 1], F32)).ap()
            seqr = en2(nc.sbuf_tensor(px + "seqr", [128, E1], F32)).ap()
            ptp = [en2(nc.psum_tensor(f"{px}ptp{j}", [128, 128], BF16)).ap() for j in range(2)]
            plog = en2(nc.psum_tensor(px + "plog", [128, E1], F32)).ap()
            pwm = en2(nc.psum_tensor(px + "pwm", [E1, 128], F32)).ap()

            s_x = [sc.sem(f"x{j}") for j in range(2)]
            s_sq = [sc.sem(f"sq{j}") for j in range(2)]
            s_rf = sc.sem("rf")
            s_wrl = sc.sem("wrl")
            s_xt = [sc.sem(f"xt{j}") for j in range(2)]
            s_pms = sc.sem("pms")
            prev_pm = None

            t_wr = sc.op("gpsimd", lambda e: e.dma_start(out=swr.rearrange("p (c e) -> p c e", c=8),
                                                         in_=wr_d.rearrange("(c p) e -> p c e", p=128)),
                         inc=s_wrl, dma=True)
            t_eps = sc.op("vector", lambda e: e.memset(seps[:], 1e-5))
            free_sx = [None, None]
            free_xh = [None, None]
            free_stg = [None, None]
            free_rt = [None, None]
            free_ptp = [None, None]
            free_plog = None
            free_pwm = None
            last_rf = None
            prev_var_read = None
            prev_xn = None
            for i in range(TT):
                b = i % 2
                r0 = i * 128
                tok_x = sc.op("sync", lambda e, b=b, r0=r0: e.dma_start(out=sx[b][:], in_=x1_d[r0:r0 + 128, :]),
                              waits=[free_sx[b]], inc=s_x[b], dma=True)
                sc.op("vector", lambda e, b=b: e.reduce_sum(ssum[:], sx[b][:], axis=AX.X), waits=[tok_x])
                tok_sq = sc.op("scalar", lambda e, b=b: e.activation(out=sqscr[:], in_=sx[b][:], func=AF.Square,
                                                                    accum_out=ssumsq[:]),
                               waits=[tok_x, prev_var_read])
                tok_mean = sc.op("vector", lambda e: e.tensor_scalar_mul(smean[:], ssum[:], 1.0 / D))
                sc.op("vector", lambda e: e.tensor_mul(sm2t[:], smean[:], smean[:]))
                prev_var_read = sc.op("vector", lambda e: e.tensor_scalar_mul(svar[:], ssumsq[:], 1.0 / D),
                                      waits=[tok_sq])
                tok_var = sc.op("vector", lambda e: e.tensor_sub(svar[:], svar[:], sm2t[:]))
                sc.op("scalar", lambda e: e.activation(out=srstd[:], in_=svar[:], func=AF.Ln, bias=seps[:, 0:1]),
                      waits=[tok_var, t_eps, prev_xn])
                tok_rstd = sc.op("scalar", lambda e: e.activation(out=srstd[:], in_=srstd[:], func=AF.Exp, scale=-0.5))
                tok_xn = sc.op("vector", lambda e, b=b: e.tensor_scalar(out=sxn[:], in0=sx[b][:],
                                                                       scalar1=smean[:, 0:1],
                                                                       scalar2=srstd[:, 0:1], op0=ALU.subtract,
                                                                       op1=ALU.mult),
                               waits=[tok_mean, tok_rstd])
                prev_xn = tok_xn
                free_sx[b] = tok_xn
                tok_xh = sc.op("vector", lambda e, b=b: e.tensor_copy(sxh[b][:], sxn[:]), waits=[free_xh[b]])
                if xn1tm_d is not None:
                    tok_xnt = sc.op("sync", lambda e, b=b, r0=r0: e.dma_start(
                        out=xn1tm_d[r0:r0 + 128, :], in_=sxh[b][:]),
                        waits=[tok_xh], inc=s_xt[b], dma=True)
                else:
                    tok_xnt = None
                cp_toks = []
                wstg = free_stg[b]
                tok_tr = None
                for c in range(8):
                    tok_tr = sc.op("tensor", lambda e, b=b, c=c: e.transpose(ptp[c % 2][:],
                                                                             sxh[b][:, c * 128:(c + 1) * 128],
                                                                             sidb[:]),
                                   waits=[tok_xh, t_c, free_ptp[c % 2]], inc=s_t)
                    tok_cp = sc.op("vector", lambda e, b=b, c=c: e.tensor_copy(stg[b][:, c * 128:(c + 1) * 128],
                                                                               ptp[c % 2][:]),
                                   waits=[tok_tr, wstg])
                    free_ptp[c % 2] = tok_cp
                    cp_toks.append(tok_cp)
                free_xh[b] = [tok_tr, tok_xnt]
                tok_stg = cp_toks[-1]
                tok_xnq = sc.op("sync", lambda e, b=b, i=i: e.dma_start(
                    out=xnq1_d[i].rearrange("c p t -> p c t"),
                    in_=stg[b].rearrange("p (c t) -> p c t", c=8)),
                    waits=[tok_stg], inc=s_sq[b], dma=True)
                mm_tok = None
                for c in range(8):
                    mm_tok = sc.op("tensor", lambda e, b=b, c=c: e.matmul(plog[:], stg[b][:, c * 128:(c + 1) * 128],
                                                                          swr[:, c * E1:(c + 1) * E1],
                                                                          start=(c == 0), stop=(c == 7)),
                                   waits=[tok_stg, t_wr, free_plog] if c == 0 else [],
                                   inc=s_t if c == 7 else None)
                free_stg[b] = [tok_xnq, mm_tok]
                tok_logcp = sc.op("vector", lambda e: e.tensor_copy(slog[:], plog[:]), waits=[mm_tok])
                free_plog = tok_logcp
                tok_mr = sc.op("vector", lambda e: e.reduce_max(smr[:], slog[:], axis=AX.X, negate=True))
                tok_exp = sc.op("scalar", lambda e: e.activation(out=sexp[:], in_=slog[:], func=AF.Exp,
                                                                bias=smr[:, 0:1], accum_out=sesum[:]),
                                waits=[tok_mr, tok_logcp])
                tok_rcp = sc.op("vector", lambda e: e.reciprocal(srcp[:], sesum[:]),
                                waits=[tok_exp, prev_pm])
                if pmax_d is not None:
                    prev_pm = sc.op("sync", lambda e, b=b, r0=r0: e.dma_start(
                        out=pmax_d[r0:r0 + 128, :], in_=srcp[:]),
                        waits=[tok_rcp, prev_pm], inc=s_pms, dma=True)
                sc.op("vector", lambda e: e.tensor_scalar(out=seqr[:], in0=sexp[:], scalar1=1.0, scalar2=None,
                                                          op0=ALU.is_ge))
                tok_rt = sc.op("vector", lambda e, b=b: e.tensor_scalar(out=srt[b][:], in0=seqr[:],
                                                                       scalar1=srcp[:, 0:1], scalar2=None,
                                                                       op0=ALU.mult),
                               waits=[tok_rcp, free_rt[b]])
                tok_rT = sc.op("tensor", lambda e, b=b: e.transpose(pwm[:], srt[b][:], sidf[:]),
                               waits=[tok_rt, t_c, free_pwm], inc=s_t)
                free_rt[b] = tok_rT
                tok_rcpc = sc.op("vector", lambda e, i=i: e.tensor_copy(srf[0:E1, i * 128:(i + 1) * 128], pwm[:]),
                                 waits=[tok_rT])
                free_pwm = tok_rcpc
                last_rf = tok_rcpc
            sc.op("sync", lambda e: e.dma_start(out=rsc_d[:], in_=srf[0:E1, :]), waits=[last_rf],
                  inc=s_rf, dma=True)
            sc.barrier()
            sc.emit_phase("H_ln1")

        for pno, part in enumerate(("q", "k", "v")):
            with ExitStack() as ci:
                en2 = ci.enter_context
                swq = en2(nc.sbuf_tensor(f"{px}swq_{part}", [128, 8 * D], BF16)).ap()
                sxnb = [en2(nc.sbuf_tensor(f"{px}sxnb_{part}{j}", [128, 8 * 512], BF16)).ap() for j in range(2)]
                sev = [en2(nc.sbuf_tensor(f"{px}sev_{part}{j}", [128, 512], BF16)).ap() for j in range(2)]
                pkv = [en2(nc.psum_tensor(f"{px}pkv_{part}{j}", [128, 512], F32)).ap() for j in range(2)]
                if part != "q":
                    kvf = en2(nc.sbuf_tensor(f"{px}kvf_{part}", [128, 8 * 4104], BF16)).ap()

                s_nb = [sc.sem(f"nb{j}") for j in range(2)]
                s_ev = [sc.sem(f"ev{j}") for j in range(2)]
                s_wq = sc.sem("wql")
                t_wq = sc.op("gpsimd", lambda e, pno=pno: e.dma_start(
                    out=swq.rearrange("p (c k) -> p c k", c=8),
                    in_=wqT_d[:, pno * D:(pno + 1) * D].rearrange("(c p) k -> p c k", p=128)),
                    inc=s_wq, dma=True)
                if part != "q":
                    tok_pad = sc.op("vector", lambda e: e.memset(kvf[:], 0.0))
                free_xnb = [None, None]
                free_ev = [None, None]
                free_pkv = [None, None]
                pi = 0
                last_proj_v = None
                for tc in range(8):
                    b = tc % 2
                    tok_xn = None
                    for q4 in range(4):
                        tok_xn = sc.op("sync", lambda e, b=b, tc=tc, q4=q4: e.dma_start(
                            out=sxnb[b].rearrange("p (c q t) -> p c q t", c=8, q=4)[:, :, q4, :],
                            in_=xnq1_d[tc * 4 + q4].rearrange("c p t -> p c t")),
                            waits=[free_xnb[b]] if q4 == 0 else [], inc=s_nb[b], dma=True)
                    last_mm = None
                    for kk in range(8):
                        pb = pi % 2
                        pi += 1
                        mm = None
                        for c in range(8):
                            mm = sc.op("tensor", lambda e, b=b, kk=kk, c=c, pb=pb: e.matmul(
                                pkv[pb][:], swq[:, c * D + kk * 128: c * D + kk * 128 + 128],
                                sxnb[b][:, c * 512:(c + 1) * 512], start=(c == 0), stop=(c == 7)),
                                waits=[tok_xn, t_wq, free_pkv[pb]] if c == 0 else [],
                                inc=s_t if c == 7 else None)
                        last_mm = mm
                        if part == "q":
                            eb = pb
                            tok_e = sc.op("vector", lambda e, pb=pb, eb=eb: e.tensor_copy(sev[eb][:], pkv[pb][:]),
                                          waits=[mm, free_ev[eb]])
                            free_pkv[pb] = tok_e
                            tok_st = sc.op("sync", lambda e, tc=tc, kk=kk, eb=eb: e.dma_start(
                                out=qq_d[tc * 4:(tc + 1) * 4, kk].rearrange("q p t -> p q t"),
                                in_=sev[eb].rearrange("p (q t) -> p q t", q=4)),
                                waits=[tok_e], inc=s_ev[eb], dma=True)
                            free_ev[eb] = tok_st
                        else:
                            tok_e = sc.op("vector", lambda e, kk=kk, tc=tc, pb=pb: e.tensor_copy(
                                kvf[:, kk * 4104 + 2 + tc * 512: kk * 4104 + 2 + (tc + 1) * 512], pkv[pb][:]),
                                waits=[mm] + ([tok_pad] if tc == 0 and kk == 0 else []))
                            free_pkv[pb] = tok_e
                            last_proj_v = tok_e
                    free_xnb[b] = last_mm
                if part != "q":
                    with ExitStack() as cj:
                        en3 = cj.enter_context
                        svt = en3(nc.sbuf_tensor(f"{px}svt_{part}", [128, 128], BF16)).ap()
                        pcm = [en3(nc.psum_tensor(f"{px}pcm_{part}{j}", [64, 128], F32)).ap() for j in range(2)]
                        ptv = [en3(nc.psum_tensor(f"{px}ptv_{part}{j}", [128, 64], BF16)).ap() for j in range(2)]
                        free_pcm = [None, None]
                        free_ptv = [None, None]
                        free_svt = None
                        if part == "k":
                            tok_nl = sc.op("vector", lambda e: e.tensor_copy(
                                kc_fm.rearrange("p (cc j) -> p cc j", cc=8)[:, :, 0:1].squeeze(2), snlk[:]),
                                waits=[t_c])
                        else:
                            tok_nl = None
                        cidx = 0
                        for h in range(NH):
                            base = (h % 2) * 64
                            hp = h // 2
                            for jc in range(11):
                                jlo = jc * 128 - 1
                                wfull = min(128, 1366 - jlo)
                                o0 = 1 if jc == 0 else 0
                                jj0 = max(jlo, 0)
                                wv = wfull - o0
                                pb = cidx % 2
                                mmc_ = None
                                for tap in range(3):
                                    col0 = 3 * jj0 + tap
                                    mmc_ = sc.op("tensor", lambda e, h=h, hp=hp, base=base, tap=tap, jj0=jj0,
                                                 wv=wv, o0=o0, pb=pb, col0=col0: e.matmul(
                                        pcm[pb][:, o0:o0 + wv],
                                        swcp[base:base + 64, tap * 512 + hp * 64: tap * 512 + hp * 64 + 64],
                                        kvf[base:base + 64, hp * 4104 + col0: hp * 4104 + col0 + 3 * wv: 3],
                                        start=(tap == 0), stop=(tap == 2)),
                                        waits=[last_proj_v, t_c, free_pcm[pb]] if tap == 0 else [],
                                        inc=s_t if tap == 2 else None)
                                if part == "k":
                                    tok_ev2 = sc.op("vector", lambda e, h=h, hp=hp, base=base, jc=jc, o0=o0,
                                                    wv=wv, pb=pb: e.tensor_scalar(
                                        out=kc_fm[base:base + 64,
                                                  hp * JPAD + jc * 128 + o0: hp * JPAD + jc * 128 + o0 + wv],
                                        in0=pcm[pb][:, o0:o0 + wv],
                                        scalar1=sbcv[base:base + 64, hp:hp + 1], scalar2=None, op0=ALU.add),
                                        waits=[mmc_])
                                    free_pcm[pb] = tok_ev2
                                else:
                                    tok_b = sc.op("vector", lambda e, base=base, hp=hp, o0=o0, wv=wv, pb=pb: e.tensor_scalar(
                                        out=svt[base:base + 64, o0:o0 + wv], in0=pcm[pb][:, o0:o0 + wv],
                                        scalar1=sbcv[base:base + 64, hp:hp + 1], scalar2=None, op0=ALU.add),
                                        waits=[mmc_, free_svt])
                                    free_pcm[pb] = tok_b
                                    if jc == 0:
                                        tok_b = sc.op("vector", lambda e, base=base, hp=hp: e.tensor_copy(
                                            svt[base:base + 64, 0:1], snlv[base:base + 64, hp:hp + 1]),
                                            waits=[tok_b, t_c])
                                    tb = cidx % 2
                                    wtot = o0 + wv if jc == 0 else wv
                                    tok_T = sc.op("tensor", lambda e, base=base, tb=tb, wtot=wtot: e.transpose(
                                        ptv[tb][0:wtot, :], svt[base:base + 64, 0:wtot],
                                        sidb[base:base + 64, base:base + 64]),
                                        waits=[tok_b, free_ptv[tb]], inc=s_t)
                                    free_svt = tok_T
                                    tok_vc = sc.op("vector", lambda e, h=h, jc=jc, tb=tb, wtot=wtot: e.tensor_copy(
                                        vc_tm[0:wtot, jc * 1024 + h * 64: jc * 1024 + h * 64 + 64]
                                        if False else
                                        vc_tm.rearrange("p (jcb d) -> p jcb d", jcb=11)[0:wtot, jc, h * 64:h * 64 + 64],
                                        ptv[tb][0:wtot, :]),
                                        waits=[tok_T])
                                    free_ptv[tb] = tok_vc
                                cidx += 1
                sc.barrier()
                sc.emit_phase(f"I_{part}")

        with ExitStack() as cj2:
            en2 = cj2.enter_context
            sq = [en2(nc.sbuf_tensor(f"{px}sq{j}", [128, D], BF16)).ap() for j in range(2)]
            ssc1 = [en2(nc.sbuf_tensor(f"{px}ssc1{j}", [128, JPAD], F32)).ap() for j in range(2)]
            sexp1 = [en2(nc.sbuf_tensor(f"{px}sexp1{j}", [128, JPAD], F32)).ap() for j in range(2)]
            sprob1 = [en2(nc.sbuf_tensor(f"{px}sprob1{j}", [128, JPAD], BF16)).ap() for j in range(2)]
            sprT = [en2(nc.sbuf_tensor(f"{px}sprT{j}", [128, 128], BF16)).ap() for j in range(2)]
            scml = [en2(nc.sbuf_tensor(f"{px}scml{j}", [128, 64], F32)).ap() for j in range(2)]
            sneg = [en2(nc.sbuf_tensor(f"{px}sneg{j}", [128, 1], F32)).ap() for j in range(2)]
            snegs = [en2(nc.sbuf_tensor(f"{px}snegs{j}", [128, 1], F32)).ap() for j in range(2)]
            ssume = [en2(nc.sbuf_tensor(f"{px}ssume{j}", [128, 1], F32)).ap() for j in range(2)]
            srcpo = [en2(nc.sbuf_tensor(f"{px}srcpo{j}", [128, 1], F32)).ap() for j in range(2)]
            psc = [en2(nc.psum_tensor(f"{px}psc{j}", [128, 512], F32)).ap() for j in range(3)]
            ptr = [en2(nc.psum_tensor(f"{px}ptr{j}", [128, 128], BF16)).ap() for j in range(2)]
            pout = [en2(nc.psum_tensor(f"{px}pout{j}", [64, 128], F32)).ap() for j in range(2)]

            satt = [en2(nc.sbuf_tensor(f"{px}satt{j}", [128, 8 * 128], BF16)).ap() for j in range(2)]
            s_q = [sc.sem(f"qj{j}") for j in range(2)]
            s_cm = [sc.sem(f"cm{j}") for j in range(2)]
            s_at = [sc.sem(f"at{j}") for j in range(2)]

            free_q = [None, None]
            free_cm = [None, None]
            free_att = [None, None]
            free_psc = [None, None, None]
            free_ptr = [None, None]
            free_prT = [None, None]
            free_pout = [None, None]
            prev_norm = [None, None]
            tr_done = [None, None]
            exp_done = [None, None]
            psci = 0
            oidx = 0
            for qi in range(TT):
                bq = qi % 2
                Lq, win0 = _lq(qi)
                nblk = (Lq + 127) // 128
                nch = (Lq + 511) // 512
                tok_q = sc.op("sync", lambda e, bq=bq, qi=qi: e.dma_start(
                    out=sq[bq].rearrange("p (c t) -> p c t", c=8),
                    in_=qq_d[qi].rearrange("c p t -> p c t")),
                    waits=[free_q[bq]], inc=s_q[bq], dma=True)
                tok_cmq = sc.op("sync", lambda e, bq=bq, qi=qi: e.dma_start(out=scml[bq][:], in_=cml_d[qi]),
                                waits=[free_cm[bq]], inc=s_cm[bq], dma=True)
                last_cm_read = None
                last_oc = None

                def consume(h, tok_pr, qi=qi, bq=bq, Lq=Lq, nblk=nblk):
                    nonlocal oidx, last_oc
                    hb = h % 2
                    base = (h % 2) * 64
                    hp = h // 2
                    ob = oidx % 2
                    out_last = None
                    tok_tr = None
                    for jc in range(nblk):
                        wj = min(128, Lq - jc * 128)
                        tb = jc % 2
                        tok_tr = sc.op("tensor", lambda e, hb=hb, jc=jc, tb=tb, wj=wj: e.transpose(
                            ptr[tb][0:wj, :], sprob1[hb][:, jc * 128: jc * 128 + wj], sidb[:]),
                            waits=([tok_pr] if jc == 0 else []) + [free_ptr[tb]], inc=s_t)
                        tok_cp = sc.op("vector", lambda e, tb=tb, wj=wj: e.tensor_copy(sprT[tb][0:wj, :],
                                                                                      ptr[tb][0:wj, :]),
                                       waits=[tok_tr, free_prT[tb]])
                        free_ptr[tb] = tok_cp
                        mm2 = sc.op("tensor", lambda e, jc=jc, tb=tb, h=h, wj=wj, ob=ob, nblk=nblk: e.matmul(
                            pout[ob][:],
                            vc_tm[0:wj, jc * 1024 + h * 64: jc * 1024 + h * 64 + 64],
                            sprT[tb][0:wj, :], start=(jc == 0), stop=(jc == nblk - 1)),
                            waits=[tok_cp] + ([free_pout[ob]] if jc == 0 else []),
                            inc=s_t if jc == nblk - 1 else None)
                        free_prT[tb] = mm2
                        out_last = mm2
                    tr_done[hb] = tok_tr
                    tok_oc = sc.op("vector", lambda e, base=base, hp=hp, bq=bq, ob=ob: e.tensor_copy(
                        satt[bq][base:base + 64, hp * 128:(hp + 1) * 128], pout[ob][:]),
                        waits=[out_last] + ([free_att[bq]] if h == 0 else []))
                    free_pout[ob] = tok_oc
                    last_oc = tok_oc
                    oidx += 1

                pending = None
                for h in range(NH):
                    hb = h % 2
                    base = (h % 2) * 64
                    hp = h // 2
                    ev_toks = []
                    for chk in range(nch):
                        wch = min(512, Lq - chk * 512)
                        pb = psci % 3
                        psci += 1
                        mm = sc.op("tensor", lambda e, base=base, hp=hp, bq=bq, chk=chk, wch=wch, pb=pb: e.matmul(
                            psc[pb][:, 0:wch],
                            sq[bq].rearrange("p (c t) -> p c t", c=8)[base:base + 64, hp, :],
                            kc_fm[base:base + 64, hp * JPAD + chk * 512: hp * JPAD + chk * 512 + wch],
                            start=True, stop=True),
                            waits=[tok_q, free_psc[pb]], inc=s_t)
                        lo = chk * 512
                        hi = lo + wch
                        if lo < win0:
                            w0 = min(hi, win0) - lo
                            sc.op("vector", lambda e, hb=hb, lo=lo, w0=w0, pb=pb: e.tensor_copy(
                                ssc1[hb][:, lo:lo + w0], psc[pb][:, 0:w0]), waits=[mm])
                        if hi > win0:
                            m0 = max(lo, win0)
                            wmm = hi - m0
                            tok_ev = sc.op("vector", lambda e, hb=hb, m0=m0, wmm=wmm, lo=lo, pb=pb, bq=bq,
                                           win0=win0: e.tensor_add(
                                ssc1[hb][:, m0:m0 + wmm], psc[pb][:, m0 - lo:m0 - lo + wmm],
                                scml[bq][:, m0 - win0:m0 - win0 + wmm]),
                                waits=[mm, tok_cmq])
                            last_cm_read = tok_ev
                        else:
                            tok_ev = sc.op("vector", lambda e: e.engine_nop(), waits=[mm])
                        free_psc[pb] = tok_ev
                        ev_toks.append(tok_ev)
                    tok_ngs = sc.op("vector", lambda e, hb=hb, Lq=Lq: e.reduce_max(sneg[hb][:], ssc1[hb][:, 0:Lq],
                                                                                  axis=AX.X, negate=True),
                                    waits=[ev_toks[-1]])
                    tok_ngss = sc.op("vector", lambda e, hb=hb: e.tensor_scalar_mul(snegs[hb][:], sneg[hb][:],
                                                                                   SCALE))
                    tok_exp = sc.op("scalar", lambda e, hb=hb, Lq=Lq: e.activation(
                        out=sexp1[hb][:, 0:Lq], in_=ssc1[hb][:, 0:Lq], func=AF.Exp, bias=snegs[hb][:, 0:1],
                        scale=SCALE, accum_out=ssume[hb][:]),
                        waits=[tok_ngss, prev_norm[hb]])
                    exp_done[hb] = tok_exp
                    tok_rcp = sc.op("vector", lambda e, hb=hb: e.reciprocal(srcpo[hb][:], ssume[hb][:]),
                                    waits=[tok_exp])
                    tok_pr = sc.op("vector", lambda e, hb=hb, Lq=Lq: e.tensor_scalar(
                        out=sprob1[hb][:, 0:Lq], in0=sexp1[hb][:, 0:Lq], scalar1=srcpo[hb][:, 0:1],
                        scalar2=None, op0=ALU.mult),
                        waits=[tok_rcp, tr_done[hb]])
                    prev_norm[hb] = tok_pr
                    if pending is not None:
                        consume(*pending)
                    pending = (h, tok_pr)
                consume(*pending)
                free_q[bq] = prev_norm[pending[0] % 2]
                free_cm[bq] = last_cm_read
                free_att[bq] = sc.op("sync", lambda e, bq=bq, qi=qi: e.dma_start(
                    out=attn_d[qi].rearrange("c p t -> p c t"),
                    in_=satt[bq].rearrange("p (c t) -> p c t", c=8)),
                    waits=[last_oc], inc=s_at[bq], dma=True)
            sc.barrier()
            sc.emit_phase("J_attn")

        with ExitStack() as ck:
            en2 = ck.enter_context
            swo = en2(nc.sbuf_tensor(px + "swo", [128, 8 * D], BF16)).ap()
            sbo = en2(nc.sbuf_tensor(px + "sbo", [128, D], F32)).ap()
            sx1r = [en2(nc.sbuf_tensor(f"{px}sx1r{j}", [128, D], F32)).ap() for j in range(2)]
            sat2 = [en2(nc.sbuf_tensor(f"{px}sat2{j}", [128, 8 * 128], BF16)).ap() for j in range(2)]
            so2 = [en2(nc.sbuf_tensor(f"{px}so2{j}", [128, 512], F32)).ap() for j in range(2)]
            pwo = [en2(nc.psum_tensor(f"{px}pwo{j}", [128, 512], F32)).ap() for j in range(2)]

            s_wo = sc.sem("wo")
            s_bo = sc.sem("bol")
            s_x1r = [sc.sem(f"x1r{j}") for j in range(2)]
            s_o2 = [sc.sem(f"o2{j}") for j in range(2)]
            s_at2 = [sc.sem(f"at2{j}") for j in range(2)]

            t_wo = sc.op("gpsimd", lambda e: e.dma_start(out=swo.rearrange("p (c k) -> p c k", c=8),
                                                         in_=woT_d.rearrange("(c p) k -> p c k", p=128)),
                         inc=s_wo, dma=True)
            t_bo = sc.op("sync", lambda e: e.dma_start(out=sbo[:], in_=bo_d.broadcast_to([128, D])),
                         inc=s_bo, dma=True)
            free_x1r = [None, None]
            free_o2 = [None, None]
            free_pwo = [None, None]
            free_at2 = [None, None]
            for qi in range(TT):
                b = qi % 2
                r0 = qi * 128
                tok_x1 = sc.op("sync", lambda e, b=b, r0=r0: e.dma_start(out=sx1r[b][:], in_=x1_d[r0:r0 + 128, :]),
                               waits=[free_x1r[b]], inc=s_x1r[b], dma=True)
                tok_at = sc.op("sync", lambda e, b=b, qi=qi: e.dma_start(
                    out=sat2[b].rearrange("p (c t) -> p c t", c=8),
                    in_=attn_d[qi].rearrange("c p t -> p c t")),
                    waits=[free_at2[b]], inc=s_at2[b], dma=True)
                st_toks = []
                for half in range(2):
                    pb = half
                    mm = None
                    for c in range(8):
                        mm = sc.op("tensor", lambda e, b=b, half=half, c=c, pb=pb: e.matmul(
                            pwo[pb][:], sat2[b][:, c * 128:(c + 1) * 128],
                            swo[:, c * D + half * 512: c * D + half * 512 + 512],
                            start=(c == 0), stop=(c == 7)),
                            waits=[t_wo, tok_at, free_pwo[pb]] if c == 0 else [],
                            inc=s_t if c == 7 else None)
                    ob = half
                    tok_o = sc.op("vector", lambda e, b=b, half=half, ob=ob, pb=pb: e.tensor_add(
                        so2[ob][:], pwo[pb][:], sbo[:, half * 512:(half + 1) * 512]),
                        waits=[mm, t_bo, free_o2[ob]])
                    tok_o2 = sc.op("vector", lambda e, b=b, half=half, ob=ob: e.tensor_add(
                        so2[ob][:], so2[ob][:], sx1r[b][:, half * 512:(half + 1) * 512]),
                        waits=[tok_x1])
                    tok_st = sc.op("sync", lambda e, r0=r0, half=half, ob=ob: e.dma_start(
                        out=x2p_d[r0:r0 + 128, half * 512:(half + 1) * 512], in_=so2[ob][:]),
                        waits=[tok_o2], inc=s_o2[ob], dma=True)
                    free_o2[ob] = tok_st
                    free_pwo[pb] = tok_o
                    st_toks.append(tok_o2)
                free_x1r[b] = st_toks[-1]
                free_at2[b] = mm
            sc.barrier()
            sc.emit_phase("K_wout")


def build_k3():
    nc = bass.Bass("TRN2", target_bir_lowering=False)
    x1_d = nc.dram_tensor("x1", [T, D], F32, kind="ExternalInput").ap()
    wr_d = nc.dram_tensor("wr", [D, E1], F32, kind="ExternalInput").ap()
    wqT_d = nc.dram_tensor("wqT", [D, 3 * D], F32, kind="ExternalInput").ap()
    wcp_d = nc.dram_tensor("wcp", [128, 3 * 8 * DH], F32, kind="ExternalInput").ap()
    bcv_d = nc.dram_tensor("bcv", [128, 8], F32, kind="ExternalInput").ap()
    nlk_d = nc.dram_tensor("nlk", [128, 8], F32, kind="ExternalInput").ap()
    nlv_d = nc.dram_tensor("nlv", [128, 8], F32, kind="ExternalInput").ap()
    woT_d = nc.dram_tensor("woT", [D, D], F32, kind="ExternalInput").ap()
    bo_d = nc.dram_tensor("bo", [1, D], F32, kind="ExternalInput").ap()
    cml_d = nc.dram_tensor("cml", [TT, 128, 64], F32, kind="ExternalInput").ap()
    idb_d = nc.dram_tensor("idb", [128, 128], BF16, kind="ExternalInput").ap()
    idf_d = nc.dram_tensor("idf", [128, 128], F32, kind="ExternalInput").ap()
    x2p_d = nc.dram_tensor("x2p", [T, D], F32, kind="ExternalOutput").ap()
    xnq1_d = nc.dram_tensor("xnq1", [TT, 8, 128, 128], BF16, kind="ExternalOutput").ap()
    rsc_d = nc.dram_tensor("rsc", [E1, T], F32, kind="ExternalOutput").ap()
    qq_d = nc.dram_tensor("qq", [TT, 8, 128, 128], BF16, kind="Internal").ap()
    attn_d = nc.dram_tensor("attn", [TT, 8, 128, 128], BF16, kind="Internal").ap()
    sc = Sched(nc)
    emit_k3(nc, sc, "c_", x1_d, wr_d, wqT_d, wcp_d, bcv_d, nlk_d, nlv_d,
            woT_d, bo_d, cml_d, idb_d, idf_d, x2p_d, xnq1_d, rsc_d, qq_d, attn_d)
    return nc


def k3_inputs(inputs, res2):
    idb, idf, cm = _host_consts()
    wqT = np.ascontiguousarray(inputs["L1_attn_w_qkv"].T)
    wc = inputs["L1_attn_w_conv"]
    wcp = np.zeros((128, 3 * 8 * DH), np.float32)
    for h in range(NH):
        base = (h % 2) * 64
        hp = h // 2
        blk = wc[h * DH:(h + 1) * DH]
        for tap in range(3):
            wcp[base:base + 64, tap * 512 + hp * 64: tap * 512 + (hp + 1) * 64] = blk[:, :, tap].T
    bcv = np.ascontiguousarray(inputs["L1_attn_b_conv"].reshape(8, 128).T)
    nlk = np.ascontiguousarray(inputs["L1_attn_null_k"].reshape(8, 128).T)
    nlv = np.ascontiguousarray(inputs["L1_attn_null_v"].reshape(8, 128).T)
    woT = np.ascontiguousarray(inputs["L1_attn_w_out"].T)
    bo = np.ascontiguousarray(inputs["L1_attn_b_out"].reshape(1, D))
    wr = np.ascontiguousarray(inputs["L1_ffn_Wr"])
    cml = np.zeros((TT, 128, 64), np.float32)
    for qi in range(TT):
        Lq, win0 = _lq(qi)
        for r in range(128):
            i = qi * 128 + r
            fm = i // 3 + 2
            for c in range(64):
                j = win0 + c
                if j >= fm:
                    cml[qi, r, c] = NEG
    maps = []
    for b in range(NCORE):
        maps.append({
            "x1": np.asarray(res2.results[b]["x1"]),
            "wr": wr, "wqT": wqT, "wcp": wcp, "bcv": bcv, "nlk": nlk, "nlv": nlv,
            "woT": woT, "bo": bo, "cml": cml, "idb": idb, "idf": idf,
        })
    return maps



def emit_k4_dense(nc, sc, px, xnq1_d, rsc_d, x2p_d, w1_d, b1_d, w2_d, b2_d,
            idb_d, out_d, w1b_d, w2b_d):
    with ExitStack() as ctx:
        en = ctx.enter_context
        sidb = en(nc.sbuf_tensor(px + "sidb", [128, 128], BF16)).ap()
        sacc = en(nc.sbuf_tensor(px + "sacc", [128, 8 * T], BF16)).ap()

        s_t = sc.sem("t")
        s_c = sc.sem("c")
        s_cast = sc.sem("cast")

        sc.op("sync", lambda e: e.dma_start(out=sidb[:], in_=idb_d[:]), inc=s_c, dma=True)
        for e_i in range(E1):
            sc.op("gpsimd", lambda e, e_i=e_i: e.dma_start(out=w1b_d[e_i], in_=w1_d[e_i]), inc=s_cast, dma=True)
            sc.op("gpsimd", lambda e, e_i=e_i: e.dma_start(out=w2b_d[e_i], in_=w2_d[e_i]), inc=s_cast, dma=True)
        tok_cast = (s_cast, s_cast.c)

        with ExitStack() as cf:
            en2 = cf.enter_context
            sw2s = en2(nc.sbuf_tensor(px + "sw2s", [128, 32 * D], BF16)).ap()
            sw1f = [en2(nc.sbuf_tensor(f"{px}sw1f{j}", [128, 8 * 128], BF16)).ap() for j in range(3)]
            sxnb = [en2(nc.sbuf_tensor(f"{px}sxnb{j}", [128, 8 * 512], BF16)).ap() for j in range(2)]
            sbc = [en2(nc.sbuf_tensor(f"{px}sbc{j}", [128, 512], F32)).ap() for j in range(2)]
            sG = en2(nc.sbuf_tensor(px + "sG", [128, 32 * 512], BF16)).ap()
            sb1f = en2(nc.sbuf_tensor(px + "sb1f", [128, 32], F32)).ap()
            stmp5 = en2(nc.sbuf_tensor(px + "stmp5", [128, 512], F32)).ap()
            cb2 = en2(ExitStack())
            sb2r = cb2.enter_context(nc.sbuf_tensor(px + "sb2r", [E1, D], F32)).ap()
            srt = cb2.enter_context(nc.sbuf_tensor(px + "srt", [E1, 512], F32)).ap()
            ph = [en2(nc.psum_tensor(f"{px}ph{j}", [128, 512], F32)).ap() for j in range(2)]
            po = [en2(nc.psum_tensor(f"{px}po{j}", [128, 512], F32)).ap() for j in range(2)]

            s_nb = [sc.sem(f"nb{j}") for j in range(2)]
            s_bc = [sc.sem(f"bc{j}") for j in range(2)]
            s_w1f = [sc.sem(f"w1f{j}") for j in range(3)]
            s_w2l = sc.sem("w2l")
            s_b1l = sc.sem("b1l")

            sc.op("sync", lambda e: e.dma_start(out=sb2r[:], in_=b2_d[:]), inc=s_c, dma=True)
            t_c = (s_c, s_c.c)
            s_rtl = sc.sem("rtl")

            acc_tok = None
            free_pb2 = [None, None]
            free_rt = None
            for tc in range(8):
                tok_rt = sc.op("sync", lambda e, tc=tc: e.dma_start(out=srt[:], in_=rsc_d[:, tc * 512:(tc + 1) * 512]),
                               waits=[free_rt], inc=s_rtl, dma=True)
                last_mm_rt = None
                for oc in range(8):
                    pbi = oc % 2
                    mm = sc.op("tensor", lambda e, oc=oc, tc=tc, pbi=pbi: e.matmul(
                        po[pbi][:], sb2r[:, oc * 128:(oc + 1) * 128], srt[:],
                        start=True, stop=True),
                        waits=[t_c, tok_rt, free_pb2[pbi]], inc=s_t)
                    acc_tok = sc.op("vector", lambda e, oc=oc, tc=tc, pbi=pbi: e.tensor_copy(
                        sacc[:, oc * T + tc * 512: oc * T + (tc + 1) * 512], po[pbi][:]), waits=[mm])
                    free_pb2[pbi] = acc_tok
                    last_mm_rt = mm
                free_rt = last_mm_rt

            sc.barrier()
            sc.emit_phase("L_b2term")
            cb2.close()
            free_xnb = [None, None]
            free_bc = [None, None]
            free_ph = [None, None]
            free_po = [None, None]
            free_w1f = [None, None, None]
            free_w2 = None
            free_b1 = None
            phi = 0
            w1i = 0
            for ei in range(E1):
                t_w2 = sc.op("gpsimd", lambda e, ei=ei: e.dma_start(
                    out=sw2s.rearrange("p (f o) -> p f o", f=32),
                    in_=w2b_d[ei].rearrange("(f p) o -> p f o", p=128)),
                    waits=[tok_cast, free_w2], inc=s_w2l, dma=True)
                t_b1 = sc.op("sync", lambda e, ei=ei: e.dma_start(out=sb1f[:], in_=b1_d[ei]),
                             waits=[free_b1], inc=s_b1l, dma=True)
                last_mm_e = None
                for tc in range(8):
                    bx = tc % 2
                    tok_xn = None
                    for q4 in range(4):
                        tok_xn = sc.op("sync", lambda e, bx=bx, tc=tc, q4=q4: e.dma_start(
                            out=sxnb[bx].rearrange("p (c q t) -> p c q t", c=8, q=4)[:, :, q4, :],
                            in_=xnq1_d[tc * 4 + q4].rearrange("c p t -> p c t")),
                            waits=[free_xnb[bx]] if q4 == 0 else [], inc=s_nb[bx], dma=True)
                    tok_bc = sc.op("sync", lambda e, bx=bx, ei=ei, tc=tc: e.dma_start(
                        out=sbc[bx][:], in_=rsc_d[ei:ei + 1, tc * 512:(tc + 1) * 512].broadcast_to([128, 512])),
                        waits=[free_bc[bx]], inc=s_bc[bx], dma=True)
                    g_toks = []
                    prev_G = None
                    for f in range(32):
                        wslot = w1i % 3
                        w1i += 1
                        t_w1f = sc.op("gpsimd", lambda e, ei=ei, f=f, wslot=wslot: e.dma_start(
                            out=sw1f[wslot].rearrange("p (c t) -> p c t", c=8),
                            in_=w1b_d[ei].rearrange("(c p) f -> p c f", p=128)[:, :, f * 128:(f + 1) * 128]),
                            waits=[tok_cast, free_w1f[wslot]], inc=s_w1f[wslot], dma=True)
                        pb = phi % 2
                        phi += 1
                        mm1 = None
                        for c in range(8):
                            mm1 = sc.op("tensor", lambda e, f=f, c=c, pb=pb, bx=bx, wslot=wslot: e.matmul(
                                ph[pb][:], sw1f[wslot][:, c * 128:(c + 1) * 128],
                                sxnb[bx][:, c * 512:(c + 1) * 512], start=(c == 0), stop=(c == 7)),
                                waits=[t_w1f, tok_xn, free_ph[pb]] if c == 0 else [],
                                inc=s_t if c == 7 else None)
                        free_w1f[wslot] = mm1
                        tok_G = sc.op("scalar", lambda e, f=f, pb=pb: e.activation(
                            out=sG[:, f * 512:(f + 1) * 512], in_=ph[pb][:], func=AF.Gelu,
                            bias=sb1f[:, f:f + 1]),
                            waits=[mm1, t_b1, prev_G])
                        prev_G = tok_G
                        free_ph[pb] = tok_G
                        g_toks.append(tok_G)
                    for oc in range(8):
                        pb = phi % 2
                        phi += 1
                        mm2 = None
                        for f in range(32):
                            mm2 = sc.op("tensor", lambda e, oc=oc, f=f, pb=pb: e.matmul(
                                po[pb][:], sw2s[:, f * D + oc * 128: f * D + oc * 128 + 128],
                                sG[:, f * 512:(f + 1) * 512], start=(f == 0), stop=(f == 31)),
                                waits=[t_w2, g_toks[-1], free_po[pb]] if f == 0 else [],
                                inc=s_t if f == 31 else None)
                        tok_t5 = sc.op("vector", lambda e, pb=pb, bx=bx: e.tensor_tensor(
                            stmp5[:], po[pb][:], sbc[bx][:], ALU.mult), waits=[mm2, tok_bc])
                        acc_tok = sc.op("vector", lambda e, oc=oc, tc=tc: e.tensor_tensor(
                            sacc[:, oc * T + tc * 512: oc * T + (tc + 1) * 512],
                            sacc[:, oc * T + tc * 512: oc * T + (tc + 1) * 512], stmp5[:], ALU.add),
                            waits=[acc_tok])
                        free_po[pb] = tok_t5
                        last_mm_e = mm2
                    free_xnb[bx] = last_mm_e
                    free_bc[bx] = acc_tok
                free_w2 = last_mm_e
                free_b1 = prev_G
            sc.barrier()
            sc.emit_phase("L_ffn")

            with ExitStack() as cg:
                en4 = cg.enter_context
                sx2p = [en4(nc.sbuf_tensor(f"{px}sx2p{j}", [128, D], F32)).ap() for j in range(2)]
                sxo = [en4(nc.sbuf_tensor(f"{px}sxo{j}", [128, D], F32)).ap() for j in range(2)]
                pacc = [en4(nc.psum_tensor(f"{px}pacc{j}", [128, 128], BF16)).ap() for j in range(2)]
                s_xp = [sc.sem(f"xp{j}") for j in range(2)]
                s_xo = [sc.sem(f"xoo{j}") for j in range(2)]
                free_p = [None, None]
                free_o = [None, None]
                free_i = [None, None]
                for i in range(TT):
                    b = i % 2
                    tok_x2p = sc.op("sync", lambda e, b=b, i=i: e.dma_start(out=sx2p[b][:],
                                                                           in_=x2p_d[i * 128:(i + 1) * 128, :]),
                                    waits=[free_i[b]], inc=s_xp[b], dma=True)
                    add_toks = []
                    for c in range(8):
                        pb = c % 2
                        tok_T = sc.op("tensor", lambda e, i=i, c=c, pb=pb: e.transpose(
                            pacc[pb][:], sacc[:, c * T + i * 128: c * T + i * 128 + 128], sidb[:]),
                            waits=[(s_c, 16), free_p[pb]] + ([acc_tok] if i == 0 and c == 0 else []), inc=s_t)
                        tok_add = sc.op("vector", lambda e, b=b, c=c, pb=pb: e.tensor_tensor(
                            sxo[b][:, c * 128:(c + 1) * 128], sx2p[b][:, c * 128:(c + 1) * 128],
                            pacc[pb][:], ALU.add),
                            waits=[tok_T, tok_x2p] + ([free_o[b]] if c == 0 else []))
                        free_p[pb] = tok_add
                        add_toks.append(tok_add)
                    tok_st2 = sc.op("sync", lambda e, b=b, i=i: e.dma_start(out=out_d[i * 128:(i + 1) * 128, :],
                                                                           in_=sxo[b][:]),
                                    waits=[add_toks[-1]], inc=s_xo[b], dma=True)
                    free_o[b] = tok_st2
                    free_i[b] = add_toks[-1]
                sc.barrier()
                sc.emit_phase("M_final")


NT1 = 15


def emit_k4(nc, sc, px, xnq1_d, rsc_d, x2p_d, w1_d, b1_d, w2_d, b2_d,
            idb_d, out_d, w1b_d, w2b_d, xn1tm_d, pmax_d, consts, stop_after=None, dbg=False):
    I32 = mybir.dt.int32
    IOA = bass.IndirectOffsetOnAxis
    KD = "ExternalOutput" if dbg else "Internal"
    tok_d = nc.dram_tensor(px + "tok", [NT1 * 512, 1], I32, kind=KD).ap()
    pos_d = nc.dram_tensor(px + "pos", [T, 1], I32, kind=KD).ap()
    texp_d = nc.dram_tensor(px + "texp", [NT1, 1], I32, kind=KD).ap()
    texpf_d = nc.dram_tensor(px + "texpf", [NT1, 1], F32, kind="Internal").ap()
    posb_d = nc.dram_tensor(px + "posb", [TT, E1], F32, kind=KD).ap()
    base_d = nc.dram_tensor(px + "based", [1, E1], F32, kind=KD).ap()
    yc_d = nc.dram_tensor(px + "yc", [NT1 * 512, D], BF16, kind=KD).ap()
    (tri8s_d, tri128_d, tri32s_d, evec8_d, itile1_d, i8_d, iota128_d, iota32_d, idxoff_d, iota128f_d) = consts

    with ExitStack() as ctx:
        en = ctx.enter_context
        sidb = en(nc.sbuf_tensor(px + "sidb", [128, 128], BF16)).ap()
        s_t = sc.sem("t")
        s_c = sc.sem("c")
        s_cast = sc.sem("cast")
        sc.op("sync", lambda e: e.dma_start(out=sidb[:], in_=idb_d[:]), inc=s_c, dma=True)
        for e_i in range(E1):
            for f in range(32):
                sc.op("gpsimd", lambda e, e_i=e_i, f=f: e.dma_start(
                    out=w1b_d[e_i, f],
                    in_=w1_d[e_i].rearrange("(c p) (f t) -> f p c t", p=128, t=128)[f]),
                    inc=s_cast, dma=True)
                sc.op("gpsimd", lambda e, e_i=e_i, f=f: e.dma_start(
                    out=w2b_d[e_i, f],
                    in_=w2_d[e_i].rearrange("(f p) (oc o) -> f p oc o", p=128, o=128)[f]),
                    inc=s_cast, dma=True)
        tok_cast = (s_cast, s_cast.c)

        with ExitStack() as cp:
            en2 = cp.enter_context
            srt8 = en2(nc.sbuf_tensor(px + "srt8", [E1, T], F32)).ap()
            ssel8 = en2(nc.sbuf_tensor(px + "ssel8", [E1, T], F32)).ap()
            stri8 = en2(nc.sbuf_tensor(px + "stri8", [E1, E1], F32)).ap()
            stri128 = en2(nc.sbuf_tensor(px + "stri128", [128, 128], F32)).ap()
            stri32 = en2(nc.sbuf_tensor(px + "stri32", [TT, TT], F32)).ap()
            sevec = en2(nc.sbuf_tensor(px + "sevec", [E1, 1], F32)).ap()
            sit1 = en2(nc.sbuf_tensor(px + "sit1", [E1, NT1], F32)).ap()
            si8 = en2(nc.sbuf_tensor(px + "si8", [E1, E1], F32)).ap()
            siota = en2(nc.sbuf_tensor(px + "siota", [128, 1], I32)).ap()
            scnts = en2(nc.sbuf_tensor(px + "scnts", [E1, TT], F32)).ap()
            stot = en2(nc.sbuf_tensor(px + "stot", [E1, 1], F32)).ap()
            sr1 = en2(nc.sbuf_tensor(px + "sr1", [E1, 1], F32)).ap()
            spadi = en2(nc.sbuf_tensor(px + "spadi", [E1, 1], mybir.dt.int32)).ap()
            spad = en2(nc.sbuf_tensor(px + "spad", [E1, 1], F32)).ap()
            sbase = en2(nc.sbuf_tensor(px + "sbase", [E1, 1], F32)).ap()
            send = en2(nc.sbuf_tensor(px + "send", [E1, 1], F32)).ap()
            sge1 = en2(nc.sbuf_tensor(px + "sge1", [E1, NT1], F32)).ap()
            sge2 = en2(nc.sbuf_tensor(px + "sge2", [E1, NT1], F32)).ap()
            setile = en2(nc.sbuf_tensor(px + "setile", [NT1, 1], F32)).ap()
            setili = en2(nc.sbuf_tensor(px + "setili", [NT1, 1], I32)).ap()
            scntT = en2(nc.sbuf_tensor(px + "scntT", [TT, E1], F32)).ap()
            sbrow = en2(nc.sbuf_tensor(px + "sbrow", [1, E1], F32)).ap()
            sposb = en2(nc.sbuf_tensor(px + "sposb", [TT, E1], F32)).ap()
            szero = en2(nc.sbuf_tensor(px + "szero", [128, 60], I32)).ap()
            sselt = en2(nc.sbuf_tensor(px + "sselt", [128, E1], F32)).ap()
            sbasebc = en2(nc.sbuf_tensor(px + "sbasebc", [128, E1], F32)).ap()
            spb = [en2(nc.sbuf_tensor(f"{px}spb{j}", [128, E1], F32)).ap() for j in range(2)]
            spos = en2(nc.sbuf_tensor(px + "spos", [128, E1], F32)).ap()
            spos1 = en2(nc.sbuf_tensor(px + "spos1", [128, 1], F32)).ap()
            sposi = [en2(nc.sbuf_tensor(f"{px}sposi{j}", [128, 1], I32)).ap() for j in range(2)]
            stok = [en2(nc.sbuf_tensor(f"{px}stok{j}", [128, 1], I32)).ap() for j in range(2)]
            ptmp = en2(nc.psum_tensor(px + "ptmp", [128, 32], F32)).ap()
            ptr8 = [en2(nc.psum_tensor(f"{px}ptr8_{j}", [128, E1], F32)).ap() for j in range(2)]

            s_ld = sc.sem("pld")
            s_pb = [sc.sem(f"pb{j}") for j in range(2)]
            s_sc1 = [sc.sem(f"sc1_{j}") for j in range(2)]
            s_sc2 = [sc.sem(f"sc2_{j}") for j in range(2)]
            s_zf = sc.sem("zf")
            s_tx = sc.sem("tx")
            s_bb = sc.sem("bb")

            sc.op("sync", lambda e: e.dma_start(out=srt8[:], in_=rsc_d[:]), inc=s_ld, dma=True)
            sc.op("sync", lambda e: e.dma_start(out=stri8[:], in_=tri8s_d[:]), inc=s_ld, dma=True)
            sc.op("sync", lambda e: e.dma_start(out=stri128[:], in_=tri128_d[:]), inc=s_ld, dma=True)
            sc.op("sync", lambda e: e.dma_start(out=stri32[:], in_=tri32s_d[:]), inc=s_ld, dma=True)
            sc.op("sync", lambda e: e.dma_start(out=sevec[:], in_=evec8_d[:]), inc=s_ld, dma=True)
            sc.op("sync", lambda e: e.dma_start(out=sit1[:], in_=itile1_d.broadcast_to([E1, NT1])),
                  inc=s_ld, dma=True)
            sc.op("sync", lambda e: e.dma_start(out=si8[:], in_=i8_d[:]), inc=s_ld, dma=True)
            sc.op("sync", lambda e: e.dma_start(out=siota[:], in_=iota128_d[:]), inc=s_ld, dma=True)
            t_c2 = (s_ld, s_ld.c)
            tok_z0 = sc.op("vector", lambda e: e.memset(szero[:], 0))
            tok_zf = sc.op("gpsimd", lambda e: e.dma_start(
                out=tok_d.rearrange("(p a) one -> p (a one)", p=128), in_=szero[:]),
                waits=[tok_z0], inc=s_zf, dma=True)
            tok_sel = sc.op("vector", lambda e: e.tensor_scalar(out=ssel8[:], in0=srt8[:], scalar1=0.0,
                                                               scalar2=None, op0=ALU.not_equal), waits=[t_c2])
            for i in range(TT):
                sc.op("vector", lambda e, i=i: e.reduce_sum(
                    scnts[:, i:i + 1], ssel8[:, i * 128:(i + 1) * 128], axis=AX.X))
            tok_tot = sc.op("vector", lambda e: e.reduce_sum(stot[:], scnts[:], axis=AX.X))
            sc.op("vector", lambda e: e.tensor_copy(spadi[:], stot[:]))
            sc.op("vector", lambda e: e.tensor_single_scalar(spadi[:], spadi[:], 511, ALU.add))
            sc.op("vector", lambda e: e.tensor_single_scalar(spadi[:], spadi[:], 9, ALU.arith_shift_right))
            sc.op("vector", lambda e: e.tensor_single_scalar(spadi[:], spadi[:], 9, ALU.logical_shift_left))
            tok_pad = sc.op("vector", lambda e: e.tensor_copy(spad[:], spadi[:]))
            mmb = sc.op("tensor", lambda e: e.matmul(ptr8[0][0:E1, 0:1], stri8[:], spad[:],
                                                     start=True, stop=True),
                        waits=[t_c2, tok_pad], inc=s_t)
            tok_base = sc.op("vector", lambda e: e.tensor_copy(sbase[:], ptr8[0][0:E1, 0:1]), waits=[mmb])
            tok_end = sc.op("vector", lambda e: e.tensor_add(send[:], sbase[:], spad[:]))
            sc.op("vector", lambda e: e.tensor_scalar(out=sge1[:], in0=sit1[:], scalar1=sbase[:, 0:1],
                                                      scalar2=None, op0=ALU.is_ge))
            sc.op("vector", lambda e: e.tensor_scalar(out=sge2[:], in0=sit1[:], scalar1=send[:, 0:1],
                                                      scalar2=None, op0=ALU.is_lt))
            tok_ind = sc.op("vector", lambda e: e.tensor_mul(sge1[:], sge1[:], sge2[:]))
            mme = sc.op("tensor", lambda e: e.matmul(ptmp[0:NT1, 0:1], sge1[:], sevec[:], start=True, stop=True),
                        waits=[tok_ind], inc=s_t)
            tok_et = sc.op("vector", lambda e: e.tensor_copy(setile[:], ptmp[0:NT1, 0:1]), waits=[mme])
            tok_eti = sc.op("vector", lambda e: e.tensor_copy(setili[:], setile[:]))
            sc.op("sync", lambda e: e.dma_start(out=texp_d[:], in_=setili[:]), waits=[tok_eti],
                  inc=s_tx, dma=True)
            sc.op("sync", lambda e: e.dma_start(out=texpf_d[:], in_=setile[:]), waits=[tok_eti],
                  inc=s_tx, dma=True)
            mmt = sc.op("tensor", lambda e: e.transpose(ptmp[0:TT, 0:E1], scnts[:], si8[:]),
                        waits=[tok_tot, t_c2, tok_et], inc=s_t)
            tok_ct = sc.op("vector", lambda e: e.tensor_copy(scntT[:], ptmp[0:TT, 0:E1]), waits=[mmt])
            mmc2 = sc.op("tensor", lambda e: e.matmul(ptmp[0:TT, 0:E1], stri32[:], scntT[:],
                                                      start=True, stop=True),
                         waits=[tok_ct], inc=s_t)
            tok_cum = sc.op("vector", lambda e: e.tensor_copy(sposb[:], ptmp[0:TT, 0:E1]), waits=[mmc2])
            tok_pbd = sc.op("sync", lambda e: e.dma_start(out=posb_d[:], in_=sposb[:]), waits=[tok_cum],
                            inc=s_tx, dma=True)
            mmr2 = sc.op("tensor", lambda e: e.matmul(ptr8[1][0:1, 0:E1], sbase[:], si8[:],
                                                      start=True, stop=True),
                         waits=[tok_base], inc=s_t)
            tok_br = sc.op("vector", lambda e: e.tensor_copy(sbrow[:], ptr8[1][0:1, 0:E1]), waits=[mmr2])
            tok_brd = sc.op("sync", lambda e: e.dma_start(out=base_d[:], in_=sbrow[:]), waits=[tok_br],
                            inc=s_tx, dma=True)
            tok_bbc = sc.op("sync", lambda e: e.dma_start(out=sbasebc[:], in_=base_d.broadcast_to([128, E1])),
                            waits=[tok_brd], inc=s_bb, dma=True)
            prev_sca = [None, None]
            for i in range(TT):
                b = i % 2
                tok_pb = sc.op("sync", lambda e, b=b, i=i: e.dma_start(
                    out=spb[b][:], in_=posb_d[i:i + 1, :].broadcast_to([128, E1])),
                    waits=[prev_sca[b], tok_pbd], inc=s_pb[b], dma=True)
                tok_tt = sc.op("tensor", lambda e, i=i, b=b: e.transpose(
                    ptr8[b][:], ssel8[:, i * 128:(i + 1) * 128], si8[:]),
                    waits=[tok_sel, prev_sca[b]], inc=s_t)
                tok_st2 = sc.op("vector", lambda e, b=b: e.tensor_copy(sselt[:], ptr8[b][:]), waits=[tok_tt])
                mmrk = sc.op("tensor", lambda e, b=b: e.matmul(ptr8[b][:], stri128[:], sselt[:],
                                                               start=True, stop=True),
                             waits=[tok_st2], inc=s_t)
                tok_pos = sc.op("vector", lambda e, b=b: e.tensor_tensor(spos[:], ptr8[b][:], spb[b][:], ALU.add),
                                waits=[mmrk, tok_pb])
                sc.op("vector", lambda e: e.tensor_add(spos[:], spos[:], sbasebc[:]), waits=[tok_bbc])
                sc.op("vector", lambda e: e.tensor_scalar_add(spos[:], spos[:], -1.0))
                sc.op("vector", lambda e: e.tensor_mul(spos[:], spos[:], sselt[:]))
                tok_p1 = sc.op("vector", lambda e: e.reduce_sum(spos1[:], spos[:], axis=AX.X))
                tok_pi = sc.op("vector", lambda e, b=b: e.tensor_copy(sposi[b][:], spos1[:]))
                tok_pd = sc.op("sync", lambda e, b=b, i=i: e.dma_start(
                    out=pos_d[i * 128:(i + 1) * 128, :], in_=sposi[b][:]),
                    waits=[tok_pi], inc=s_sc1[b], dma=True)
                tok_tk = sc.op("vector", lambda e, b=b, i=i: e.tensor_scalar_add(stok[b][:], siota[:], i * 128),
                               waits=[t_c2])
                tok_sca = sc.op("gpsimd", lambda e, b=b: e.indirect_dma_start(
                    out=tok_d[:], out_offset=IOA(sposi[b][:, 0:1], 0),
                    in_=stok[b][:, 0:1], in_offset=None),
                    waits=[tok_pi, tok_tk, tok_zf], inc=s_sc2[b], dma=True)
                prev_sca[b] = [tok_sca, tok_pd]
            sc.barrier()
            sc.emit_phase(px + "P_cmp")
        if stop_after == "P":
            return

        with ExitStack() as cq:
            en2 = cq.enter_context
            sxg = [en2(nc.sbuf_tensor(f"{px}sxg{j}", [128, D], BF16)).ap() for j in range(2)]
            xcf = en2(nc.sbuf_tensor(px + "xcf", [128, 8 * 512], BF16)).ap()
            sw1f = [en2(nc.sbuf_tensor(f"{px}sw1f{j}", [128, 8 * 128], BF16)).ap() for j in range(3)]
            sw2s = en2(nc.sbuf_tensor(px + "sw2s", [128, 32 * D], BF16)).ap()
            sG = en2(nc.sbuf_tensor(px + "sG", [128, 32 * 512], BF16)).ap()
            sb1s = [en2(nc.sbuf_tensor(f"{px}sb1s{j}", [128, 32], F32)).ap() for j in range(2)]
            sy = en2(nc.sbuf_tensor(px + "sy", [128, 512], BF16)).ap()
            sty = en2(nc.sbuf_tensor(px + "sty", [128, 4 * D], BF16)).ap()
            stexb = [en2(nc.sbuf_tensor(f"{px}stexb{j}", [128, 1], F32)).ap() for j in range(2)]
            sidxall_f = [en2(nc.sbuf_tensor(f"{px}sidxallf{j}", [128, 32], F32)).ap() for j in range(2)]
            sidxall = [en2(nc.sbuf_tensor(f"{px}sidxall{j}", [128, 32], I32)).ap() for j in range(2)]
            sidxb1f = [en2(nc.sbuf_tensor(f"{px}sidxb1f{j}", [128, 1], F32)).ap() for j in range(2)]
            sidxb1 = [en2(nc.sbuf_tensor(f"{px}sidxb1_{j}", [128, 1], I32)).ap() for j in range(2)]
            sioff = en2(nc.sbuf_tensor(px + "sioff", [128, 32], F32)).ap()
            siof = en2(nc.sbuf_tensor(px + "siof", [128, 1], F32)).ap()
            sidxt = [en2(nc.sbuf_tensor(f"{px}sidxt{j}", [128, 1], I32)).ap() for j in range(2)]
            siot32 = en2(nc.sbuf_tensor(px + "siot32", [TT, 1], I32)).ap()
            ph = [en2(nc.psum_tensor(f"{px}ph{j}", [128, 512], F32)).ap() for j in range(2)]
            po = [en2(nc.psum_tensor(f"{px}po{j}", [128, 512], F32)).ap() for j in range(2)]
            ptt = [en2(nc.psum_tensor(f"{px}ptt{j}", [128, 128], BF16)).ap() for j in range(2)]

            s_xg = [sc.sem(f"xg{j}") for j in range(2)]
            s_wg = [sc.sem(f"wg{j}") for j in range(2)]
            s_w1l = [sc.sem(f"w1l{j}") for j in range(3)]
            s_w2l = sc.sem("w2l")
            s_b1l = [sc.sem(f"b1l{j}") for j in range(2)]
            s_it = sc.sem("it")
            s_ti = [sc.sem(f"ti{j}") for j in range(2)]
            s_yc = sc.sem("ycs")

            t_io = sc.op("sync", lambda e: e.dma_start(out=siot32[:], in_=iota32_d[0:TT, :]), inc=s_it, dma=True)
            sc.op("sync", lambda e: e.dma_start(out=sioff[:], in_=idxoff_d[:]), inc=s_it, dma=True)
            sc.op("sync", lambda e: e.dma_start(out=siof[:], in_=iota128f_d[:]), inc=s_it, dma=True)
            free_xg = [None, None]
            free_w1f = [None, None, None]
            free_w2 = None
            free_b1 = [None, None]
            free_idxw = [None, None]
            free_idxt = [None, None]
            free_ph = [None, None]
            free_po = [None, None]
            free_ptt = [None, None]
            free_xcf = None
            free_G = None
            free_sy = None
            free_sty = None
            w1i = 0
            phi = 0
            for ct in range(NT1):
                cb = ct % 2
                tok_te = sc.op("sync", lambda e, ct=ct, cb=cb: e.dma_start(
                    out=stexb[cb][:], in_=texpf_d[ct:ct + 1, :].broadcast_to([128, 1])),
                    waits=[(None if free_idxw[cb] is None else free_idxw[cb])], inc=s_wg[cb], dma=True)
                tok_i1 = sc.op("vector", lambda e, cb=cb: e.tensor_scalar(
                    out=sidxb1f[cb][:], in0=stexb[cb][:], scalar1=128.0, scalar2=None, op0=ALU.mult),
                    waits=[tok_te, (s_it, s_it.c)])
                sc.op("vector", lambda e, cb=cb: e.tensor_add(sidxb1f[cb][:], sidxb1f[cb][:], siof[:]))
                tok_i2 = sc.op("vector", lambda e, cb=cb: e.tensor_copy(sidxb1[cb][:], sidxb1f[cb][:]))
                sc.op("vector", lambda e, cb=cb: e.tensor_scalar(
                    out=sidxb1f[cb][:], in0=stexb[cb][:], scalar1=4096.0, scalar2=None,
                    op0=ALU.mult))
                sc.op("vector", lambda e, cb=cb: e.tensor_scalar(
                    out=sidxall_f[cb][:], in0=sioff[:], scalar1=sidxb1f[cb][:, 0:1],
                    scalar2=None, op0=ALU.add))
                tok_ix2 = sc.op("vector", lambda e, cb=cb: e.tensor_copy(sidxall[cb][:], sidxall_f[cb][:]))
                tok_b1 = sc.op("gpsimd", lambda e, ct=ct, cb=cb: e.indirect_dma_start(
                    out=sb1s[cb][:], out_offset=None,
                    in_=b1_d.rearrange("e p f -> (e p) f"),
                    in_offset=IOA(sidxb1[cb][:, 0:1], 0)),
                    waits=[tok_i2, free_b1[cb]], inc=s_b1l[cb], dma=True)
                tp_toks = []
                for q4 in range(4):
                    gb = q4 % 2
                    tok_ti = sc.op("sync", lambda e, ct=ct, q4=q4, gb=gb: e.dma_start(
                        out=sidxt[gb][:], in_=tok_d[ct * 512 + q4 * 128: ct * 512 + (q4 + 1) * 128, :]),
                        waits=[free_idxt[gb]], inc=s_ti[gb], dma=True)
                    tok_xga = sc.op("gpsimd", lambda e, gb=gb: e.indirect_dma_start(
                        out=sxg[gb][:], out_offset=None,
                        in_=xn1tm_d[:], in_offset=IOA(sidxt[gb][:, 0:1], 0)),
                        waits=[tok_ti, free_xg[gb]], inc=s_xg[gb], dma=True)
                    free_idxt[gb] = tok_xga
                    for c in range(8):
                        tb = (q4 * 8 + c) % 2
                        tok_T = sc.op("tensor", lambda e, gb=gb, c=c, tb=tb: e.transpose(
                            ptt[tb][:], sxg[gb][:, c * 128:(c + 1) * 128], sidb[:]),
                            waits=[tok_xga, (s_c, 16), free_ptt[tb], free_xcf], inc=s_t)
                        tok_cp = sc.op("vector", lambda e, q4=q4, c=c, tb=tb: e.tensor_copy(
                            xcf[:, c * 512 + q4 * 128: c * 512 + q4 * 128 + 128], ptt[tb][:]),
                            waits=[tok_T])
                        free_ptt[tb] = tok_cp
                        tp_toks.append(tok_cp)
                    free_xg[gb] = tp_toks[-1]
                tok_xcf = tp_toks[-1]
                t_w2 = None
                for f2 in range(32):
                    t_w2 = sc.op("gpsimd", lambda e, cb=cb, f2=f2: e.indirect_dma_start(
                        out=sw2s.rearrange("p (f o) -> p f o", f=32)[:, f2, :], out_offset=None,
                        in_=w2b_d.rearrange("e f p c t -> (e f p) (c t)"),
                        in_offset=IOA(sidxall[cb][:, f2:f2 + 1], 0)),
                        waits=[tok_ix2, tok_cast, free_w2] if f2 == 0 else [], inc=s_w2l, dma=True)
                g_toks = []
                prev_G = None
                for f in range(32):
                    ws = w1i % 3
                    w1i += 1
                    t_w1f = sc.op("gpsimd", lambda e, cb=cb, f=f, ws=ws: e.indirect_dma_start(
                        out=sw1f[ws][:], out_offset=None,
                        in_=w1b_d.rearrange("e f p c t -> (e f p) (c t)"),
                        in_offset=IOA(sidxall[cb][:, f:f + 1], 0)),
                        waits=[tok_ix2, tok_cast, free_w1f[ws]], inc=s_w1l[ws], dma=True)
                    pb = phi % 2
                    phi += 1
                    mm1 = None
                    for c in range(8):
                        mm1 = sc.op("tensor", lambda e, f=f, c=c, pb=pb, ws=ws: e.matmul(
                            ph[pb][:], sw1f[ws][:, c * 128:(c + 1) * 128],
                            xcf[:, c * 512:(c + 1) * 512], start=(c == 0), stop=(c == 7)),
                            waits=[t_w1f, tok_xcf, free_ph[pb], free_G] if c == 0 else [],
                            inc=s_t if c == 7 else None)
                    free_w1f[ws] = mm1
                    tok_Gf = sc.op("scalar", lambda e, f=f, pb=pb, cb=cb: e.activation(
                        out=sG[:, f * 512:(f + 1) * 512], in_=ph[pb][:], func=AF.Gelu,
                        bias=sb1s[cb][:, f:f + 1]),
                        waits=[mm1, tok_b1, prev_G])
                    prev_G = tok_Gf
                    free_ph[pb] = tok_Gf
                    g_toks.append(tok_Gf)
                free_b1[cb] = prev_G

                ty_toks = []
                last_mm2 = None
                for oc in range(8):
                    pb = phi % 2
                    phi += 1
                    mm2 = None
                    for f in range(32):
                        mm2 = sc.op("tensor", lambda e, oc=oc, f=f, pb=pb: e.matmul(
                            po[pb][:], sw2s[:, f * D + oc * 128: f * D + oc * 128 + 128],
                            sG[:, f * 512:(f + 1) * 512], start=(f == 0), stop=(f == 31)),
                            waits=[t_w2, g_toks[-1], free_po[pb], free_sty] if f == 0 else [],
                            inc=s_t if f == 31 else None)
                    last_mm2 = mm2
                    tok_sy = sc.op("vector", lambda e, pb=pb: e.tensor_copy(sy[:], po[pb][:]),
                                   waits=[mm2, free_sy])
                    free_po[pb] = tok_sy
                    for q4 in range(4):
                        tb = (oc * 4 + q4) % 2
                        tok_T2 = sc.op("tensor", lambda e, q4=q4, tb=tb: e.transpose(
                            ptt[tb][:], sy[:, q4 * 128:(q4 + 1) * 128], sidb[:]),
                            waits=[tok_sy, free_ptt[tb]], inc=s_t)
                        tok_cp2 = sc.op("vector", lambda e, oc=oc, q4=q4, tb=tb: e.tensor_copy(
                            sty[:, q4 * D + oc * 128: q4 * D + oc * 128 + 128], ptt[tb][:]),
                            waits=[tok_T2] + ([free_sty] if oc == 0 and q4 == 0 else []))
                        free_ptt[tb] = tok_cp2
                        ty_toks.append(tok_cp2)
                    free_sy = tok_T2
                tok_yst = sc.op("sync", lambda e, ct=ct: e.dma_start(
                    out=yc_d[ct * 512:(ct + 1) * 512, :].rearrange("(q p) d -> p q d", p=128),
                    in_=sty.rearrange("p (q d) -> p q d", q=4)),
                    waits=[ty_toks[-1]], inc=s_yc, dma=True)
                free_sty = tok_yst
                free_G = last_mm2
                free_xcf = mm1
                free_w2 = last_mm2
                free_idxw[cb] = [mm1, last_mm2]
            sc.barrier()
            sc.emit_phase(px + "Q_cffn")
        if stop_after == "Q":
            return

        with ExitStack() as cr:
            en2 = cr.enter_context
            syg = [en2(nc.sbuf_tensor(f"{px}syg{j}", [128, D], BF16)).ap() for j in range(2)]
            sx2p = [en2(nc.sbuf_tensor(f"{px}sx2p{j}", [128, D], F32)).ap() for j in range(2)]
            sxo = [en2(nc.sbuf_tensor(f"{px}sxo{j}", [128, D], F32)).ap() for j in range(2)]
            spm = [en2(nc.sbuf_tensor(f"{px}spm{j}", [128, 1], F32)).ap() for j in range(2)]
            sidx2 = [en2(nc.sbuf_tensor(f"{px}sidx2{j}", [128, 1], I32)).ap() for j in range(2)]
            srt8b = en2(nc.sbuf_tensor(px + "srt8b", [E1, T], F32)).ap()
            sb2r = en2(nc.sbuf_tensor(px + "sb2r", [E1, D], F32)).ap()
            pb2 = [en2(nc.psum_tensor(f"{px}pb2_{j}", [128, 512], F32)).ap() for j in range(2)]

            s_ld2 = sc.sem("ld2")
            s_g2 = [sc.sem(f"g2{j}") for j in range(2)]
            s_i2 = [sc.sem(f"i2{j}") for j in range(2)]
            s_pm = [sc.sem(f"pm{j}") for j in range(2)]
            s_x2 = [sc.sem(f"x2{j}") for j in range(2)]
            s_oo = [sc.sem(f"oo{j}") for j in range(2)]

            sc.op("sync", lambda e: e.dma_start(out=srt8b[:], in_=rsc_d[:]), inc=s_ld2, dma=True)
            sc.op("sync", lambda e: e.dma_start(out=sb2r[:], in_=b2_d[:]), inc=s_ld2, dma=True)
            t_c3 = (s_ld2, s_ld2.c)
            free_yg = [None, None]
            free_i2 = [None, None]
            free_pm = [None, None]
            free_x2 = [None, None]
            free_oo = [None, None]
            free_pb2 = [None, None]
            for i in range(TT):
                b = i % 2
                r0 = i * 128
                tok_i2 = sc.op("sync", lambda e, b=b, r0=r0: e.dma_start(out=sidx2[b][:],
                                                                        in_=pos_d[r0:r0 + 128, :]),
                               waits=[free_i2[b]], inc=s_i2[b], dma=True)
                tok_pm = sc.op("sync", lambda e, b=b, r0=r0: e.dma_start(out=spm[b][:],
                                                                        in_=pmax_d[r0:r0 + 128, :]),
                               waits=[free_pm[b]], inc=s_pm[b], dma=True)
                tok_x2 = sc.op("sync", lambda e, b=b, r0=r0: e.dma_start(out=sx2p[b][:],
                                                                        in_=x2p_d[r0:r0 + 128, :]),
                               waits=[free_x2[b]], inc=s_x2[b], dma=True)
                tok_yg = sc.op("gpsimd", lambda e, b=b: e.indirect_dma_start(
                    out=syg[b][:], out_offset=None, in_=yc_d[:], in_offset=IOA(sidx2[b][:, 0:1], 0)),
                    waits=[tok_i2, free_yg[b], (s_yc, s_yc.c)], inc=s_g2[b], dma=True)
                out_toks = []
                for half in range(2):
                    pbi = half
                    mmb2 = sc.op("tensor", lambda e, i=i, half=half, pbi=pbi: e.matmul(
                        pb2[pbi][:], srt8b[:, i * 128:(i + 1) * 128],
                        sb2r[:, half * 512:(half + 1) * 512], start=True, stop=True),
                        waits=[t_c3, free_pb2[pbi]], inc=s_t)
                    tok_o = sc.op("vector", lambda e, b=b, half=half, pbi=pbi: e.scalar_tensor_tensor(
                        out=sxo[b][:, half * 512:(half + 1) * 512], in0=syg[b][:, half * 512:(half + 1) * 512],
                        scalar=spm[b][:, 0:1], in1=pb2[pbi][:], op0=ALU.mult, op1=ALU.add),
                        waits=[tok_yg, tok_pm, mmb2, free_oo[b] if half == 0 else None])
                    tok_o2 = sc.op("vector", lambda e, b=b, half=half: e.tensor_add(
                        sxo[b][:, half * 512:(half + 1) * 512], sxo[b][:, half * 512:(half + 1) * 512],
                        sx2p[b][:, half * 512:(half + 1) * 512]),
                        waits=[tok_x2])
                    free_pb2[pbi] = tok_o
                    out_toks.append(tok_o2)
                tok_st = sc.op("sync", lambda e, b=b, r0=r0: e.dma_start(out=out_d[r0:r0 + 128, :],
                                                                        in_=sxo[b][:]),
                               waits=[out_toks[-1]], inc=s_oo[b], dma=True)
                free_oo[b] = tok_st
                free_i2[b] = tok_yg
                free_pm[b] = out_toks[0]
                free_x2[b] = out_toks[-1]
                free_yg[b] = out_toks[-1]
            sc.barrier()
            sc.emit_phase(px + "R_fin")


def build_k4():
    nc = bass.Bass("TRN2", target_bir_lowering=False)
    xnq1_d = nc.dram_tensor("xnq1", [TT, 8, 128, 128], BF16, kind="ExternalInput").ap()
    rsc_d = nc.dram_tensor("rsc", [E1, T], F32, kind="ExternalInput").ap()
    x2p_d = nc.dram_tensor("x2p", [T, D], F32, kind="ExternalInput").ap()
    w1_d = nc.dram_tensor("w1", [E1, D, F1], F32, kind="ExternalInput").ap()
    b1_d = nc.dram_tensor("b1r", [E1, 128, 32], F32, kind="ExternalInput").ap()
    w2_d = nc.dram_tensor("w2", [E1, F1, D], F32, kind="ExternalInput").ap()
    b2_d = nc.dram_tensor("b2", [E1, D], F32, kind="ExternalInput").ap()
    idb_d = nc.dram_tensor("idb", [128, 128], BF16, kind="ExternalInput").ap()
    out_d = nc.dram_tensor("outp", [T, D], F32, kind="ExternalOutput").ap()
    w1b_d = nc.dram_tensor("w1b", [E1, D, F1], BF16, kind="Internal").ap()
    w2b_d = nc.dram_tensor("w2b", [E1, F1, D], BF16, kind="Internal").ap()
    sc = Sched(nc)
    emit_k4_dense(nc, sc, "d_", xnq1_d, rsc_d, x2p_d, w1_d, b1_d, w2_d, b2_d,
                  idb_d, out_d, w1b_d, w2b_d)
    return nc


def build_k4_sparse_test(stop_after=None, dbg=False):
    nc = bass.Bass("TRN2", target_bir_lowering=False)
    I, N = "ExternalInput", "Internal"
    dt = nc.dram_tensor
    I32_ = mybir.dt.int32
    xnq1_d = None
    rsc_d = dt("rsc", [E1, T], F32, kind=I).ap()
    x2p_d = dt("x2p", [T, D], F32, kind=I).ap()
    w1_d = dt("w1", [E1, D, F1], F32, kind=I).ap()
    b1_d = dt("b1r", [E1, 128, 32], F32, kind=I).ap()
    w2_d = dt("w2", [E1, F1, D], F32, kind=I).ap()
    b2_d = dt("b2", [E1, D], F32, kind=I).ap()
    idb_d = dt("idb", [128, 128], BF16, kind=I).ap()
    out_d = dt("outp", [T, D], F32, kind="ExternalOutput").ap()
    w1b_d = dt("w1b", [E1, 32, 128, 8, 128], BF16, kind=N).ap()
    w2b_d = dt("w2b", [E1, 32, 128, 8, 128], BF16, kind=N).ap()
    xn1tm_d = dt("xn1tm", [T, D], BF16, kind=I).ap()
    pmax_d = dt("pmax", [T, 1], F32, kind=I).ap()
    tri8s_d = dt("tri8s", [E1, E1], F32, kind=I).ap()
    tri128_d = dt("tri128", [128, 128], F32, kind=I).ap()
    tri32s_d = dt("tri32s", [TT, TT], F32, kind=I).ap()
    evec8_d = dt("evec8", [E1, 1], F32, kind=I).ap()
    itile1_d = dt("itile1", [1, NT1], F32, kind=I).ap()
    i8_d = dt("i8c", [E1, E1], F32, kind=I).ap()
    iota128_d = dt("iota128", [128, 1], I32_, kind=I).ap()
    iota32_d = dt("iota32", [TT, 1], I32_, kind=I).ap()
    idxoff_d = dt("idxoff", [128, TT], F32, kind=I).ap()
    iota128f_d = dt("iota128f", [128, 1], F32, kind=I).ap()
    k4consts = (tri8s_d, tri128_d, tri32s_d, evec8_d, itile1_d, i8_d, iota128_d, iota32_d, idxoff_d, iota128f_d)
    sc = Sched(nc)
    emit_k4(nc, sc, "d_", xnq1_d, rsc_d, x2p_d, w1_d, b1_d, w2_d, b2_d,
            idb_d, out_d, w1b_d, w2b_d, xn1tm_d, pmax_d, k4consts,
            stop_after=stop_after, dbg=dbg)
    return nc


def k4_inputs(inputs, res3):
    idb, idf, cm = _host_consts()
    b1r = np.ascontiguousarray(inputs["L1_ffn_b1"].reshape(E1, 32, 128).transpose(0, 2, 1))
    maps = []
    for b in range(NCORE):
        maps.append({
            "xnq1": np.asarray(res3.results[b]["xnq1"]),
            "rsc": np.asarray(res3.results[b]["rsc"]),
            "x2p": np.asarray(res3.results[b]["x2p"]),
            "w1": inputs["L1_ffn_W1"], "b1r": b1r,
            "w2": inputs["L1_ffn_W2"], "b2": inputs["L1_ffn_b2"],
            "idb": idb,
        })
    return maps



def _host_consts():
    idb = np.eye(128, dtype=np.float32).astype(BF)
    idf = np.eye(128, dtype=np.float32)
    cm = np.triu(np.full((128, 128), NEG, np.float32), 1)
    return idb, idf, cm


def k1_inputs(inputs):
    idb, idf, cm = _host_consts()
    x = np.ascontiguousarray(inputs["L0_attn_x"].astype(np.float32))
    wt = np.ascontiguousarray(inputs["L0_attn_W"].T)
    gw = np.ascontiguousarray(inputs["L0_ffn_gate_w"])
    gb = np.ascontiguousarray(inputs["L0_ffn_gate_b"].reshape(1, E0))
    return [{"x": x[b], "wt": wt, "gw": gw, "gb": gb, "idb": idb, "idf": idf, "cm": cm}
            for b in range(NCORE)]


def build_fused():
    nc = bass.Bass("TRN2", target_bir_lowering=False, num_devices=NCORE)
    I, O, N = "ExternalInput", "ExternalOutput", "Internal"
    dt = nc.dram_tensor
    x_d = dt("x", [T, D], F32, kind=I).ap()
    wt_d = dt("wt", [D, 2 * D], F32, kind=I).ap()
    gw_d = dt("gw", [D, E0], F32, kind=I).ap()
    gb_d = dt("gb", [1, E0], F32, kind=I).ap()
    idb_d = dt("idb", [128, 128], BF16, kind=I).ap()
    idf_d = dt("idf", [128, 128], F32, kind=I).ap()
    cm_d = dt("cm", [128, 128], F32, kind=I).ap()
    ew1_d = dt("ew1", [E0, D, H0], F32, kind=I).ap()
    ew3_d = dt("ew3", [E0, D, H0], F32, kind=I).ap()
    ew2_d = dt("ew2", [E0, H0, D], F32, kind=I).ap()
    sw1_d = dt("sw1", [2, D, H0], F32, kind=I).ap()
    sw3_d = dt("sw3", [2, D, H0], F32, kind=I).ap()
    sw2_d = dt("sw2", [2, H0, D], F32, kind=I).ap()
    ssel_d = dt("Ssel", [128, E0], F32, kind=I).ap()
    stsel_d = dt("STsel", [E0, 128], F32, kind=I).ap()
    rsel_d = dt("Rsel", [128, E0], F32, kind=I).ap()
    ones_d = dt("ones_f", [1, 128], F32, kind=I).ap()
    wr_d = dt("wr", [D, E1], F32, kind=I).ap()
    wqT_d = dt("wqT", [D, 3 * D], F32, kind=I).ap()
    wcp_d = dt("wcp", [128, 3 * 8 * DH], F32, kind=I).ap()
    bcv_d = dt("bcv", [128, 8], F32, kind=I).ap()
    nlk_d = dt("nlk", [128, 8], F32, kind=I).ap()
    nlv_d = dt("nlv", [128, 8], F32, kind=I).ap()
    woT_d = dt("woT", [D, D], F32, kind=I).ap()
    bo_d = dt("bo", [1, D], F32, kind=I).ap()
    cml_d = dt("cml", [TT, 128, 64], F32, kind=I).ap()
    w1_d = dt("w1", [E1, D, F1], F32, kind=I).ap()
    b1_d = dt("b1r", [E1, 128, 32], F32, kind=I).ap()
    w2_d = dt("w2", [E1, F1, D], F32, kind=I).ap()
    b2_d = dt("b2", [E1, D], F32, kind=I).ap()
    out_d = dt("outp", [T, D], F32, kind=O).ap()
    x1p_d = dt("x1p", [T, D], F32, kind=N).ap()
    xnq_d = dt("xnq", [TT, 8, 128, 128], BF16, kind=N).ap()
    wmf_d = dt("wmf", [E0, T], F32, kind=N).ap()
    wmt_d = dt("wmt", [T, E0], F32, kind=N).ap()
    x0xn_d = dt("x0xn", [T, D], F32, kind=N).ap()
    wgf_d = dt("wgf", [128, T], F32, kind=N).ap()
    x1_d = dt("x1", [T, D], F32, kind=N).ap()
    wfin_d = dt("wfin", [E0, T], F32, kind=N).ap()
    ew1b_d = dt("ew1b", [E0, D, H0], BF16, kind=N).ap()
    ew3b_d = dt("ew3b", [E0, D, H0], BF16, kind=N).ap()
    ew2b_d = dt("ew2b", [E0, H0, D], BF16, kind=N).ap()
    sw1b_d = dt("sw1b", [2, D, H0], BF16, kind=N).ap()
    sw3b_d = dt("sw3b", [2, D, H0], BF16, kind=N).ap()
    sw2b_d = dt("sw2b", [2, H0, D], BF16, kind=N).ap()
    xnq1_d = dt("xnq1", [TT, 8, 128, 128], BF16, kind=N).ap()
    rsc_d = dt("rsc", [E1, T], F32, kind=N).ap()
    x2p_d = dt("x2p", [T, D], F32, kind=N).ap()
    qq_d = dt("qq", [TT, 8, 128, 128], BF16, kind=N).ap()
    attn_d = dt("attn", [TT, 8, 128, 128], BF16, kind=N).ap()
    w1b_d = dt("w1b", [E1, 32, 128, 8, 128], BF16, kind=N).ap()
    w2b_d = dt("w2b", [E1, 32, 128, 8, 128], BF16, kind=N).ap()
    xn1tm_d = dt("xn1tm", [T, D], BF16, kind=N).ap()
    pmax_d = dt("pmax", [T, 1], F32, kind=N).ap()
    I32_ = mybir.dt.int32
    tri8s_d = dt("tri8s", [E1, E1], F32, kind=I).ap()
    tri128_d = dt("tri128", [128, 128], F32, kind=I).ap()
    tri32s_d = dt("tri32s", [TT, TT], F32, kind=I).ap()
    evec8_d = dt("evec8", [E1, 1], F32, kind=I).ap()
    itile1_d = dt("itile1", [1, NT1], F32, kind=I).ap()
    i8_d = dt("i8c", [E1, E1], F32, kind=I).ap()
    iota128_d = dt("iota128", [128, 1], I32_, kind=I).ap()
    iota32_d = dt("iota32", [TT, 1], I32_, kind=I).ap()
    idxoff_d = dt("idxoff", [128, TT], F32, kind=I).ap()
    iota128f_d = dt("iota128f", [128, 1], F32, kind=I).ap()
    k4consts = (tri8s_d, tri128_d, tri32s_d, evec8_d, itile1_d, i8_d, iota128_d, iota32_d, idxoff_d, iota128f_d)

    sc = Sched(nc)
    emit_k1(nc, sc, "a_", x_d, wt_d, gw_d, gb_d, idb_d, idf_d, cm_d,
            x1p_d, xnq_d, wmf_d, wmt_d, x0xn_d)
    tok_ag = sc.op("gpsimd", lambda e: e.collective_compute(
        "AllGather", mybir.AluOpType.bypass,
        replica_groups=[list(range(NCORE))],
        ins=[wmf_d.opt()], outs=[wgf_d.opt()]))
    emit_k2(nc, sc, "b_", xnq_d, wmt_d, wgf_d, x1p_d, ew1_d, ew3_d, ew2_d,
            sw1_d, sw3_d, sw2_d, ssel_d, stsel_d, rsel_d, ones_d, idb_d, idf_d,
            x1_d, wfin_d, ew1b_d, ew3b_d, ew2b_d, sw1b_d, sw3b_d, sw2b_d,
            wgf_wait=tok_ag)
    emit_k3(nc, sc, "c_", x1_d, wr_d, wqT_d, wcp_d, bcv_d, nlk_d, nlv_d,
            woT_d, bo_d, cml_d, idb_d, idf_d, x2p_d, xnq1_d, rsc_d, qq_d, attn_d,
            xn1tm_d=xn1tm_d, pmax_d=pmax_d)
    emit_k4(nc, sc, "d_", xnq1_d, rsc_d, x2p_d, w1_d, b1_d, w2_d, b2_d,
            idb_d, out_d, w1b_d, w2b_d, xn1tm_d, pmax_d, k4consts)
    return nc


def fused_inputs(inputs):
    idb, idf, cm = _host_consts()
    x = np.ascontiguousarray(inputs["L0_attn_x"].astype(np.float32))
    wt = np.ascontiguousarray(inputs["L0_attn_W"].T)
    gw = np.ascontiguousarray(inputs["L0_ffn_gate_w"])
    gb = np.ascontiguousarray(inputs["L0_ffn_gate_b"].reshape(1, E0))
    S = np.zeros((128, E0), np.float32)
    for r in range(NCORE):
        for e_i in range(E0):
            S[r * E0 + e_i, e_i] = 1.0
    ST = np.ascontiguousarray(S.T)
    ones_f = np.ones((1, 128), np.float32)
    wqT = np.ascontiguousarray(inputs["L1_attn_w_qkv"].T)
    wc = inputs["L1_attn_w_conv"]
    wcp = np.zeros((128, 3 * 8 * DH), np.float32)
    for h in range(NH):
        base = (h % 2) * 64
        hp = h // 2
        blk = wc[h * DH:(h + 1) * DH]
        for tap in range(3):
            wcp[base:base + 64, tap * 512 + hp * 64: tap * 512 + (hp + 1) * 64] = blk[:, :, tap].T
    bcv = np.ascontiguousarray(inputs["L1_attn_b_conv"].reshape(8, 128).T)
    nlk = np.ascontiguousarray(inputs["L1_attn_null_k"].reshape(8, 128).T)
    nlv = np.ascontiguousarray(inputs["L1_attn_null_v"].reshape(8, 128).T)
    woT = np.ascontiguousarray(inputs["L1_attn_w_out"].T)
    bo = np.ascontiguousarray(inputs["L1_attn_b_out"].reshape(1, D))
    wr = np.ascontiguousarray(inputs["L1_ffn_Wr"])
    cml = np.zeros((TT, 128, 64), np.float32)
    for qi in range(TT):
        Lq, win0 = _lq(qi)
        for r in range(128):
            i = qi * 128 + r
            fm = i // 3 + 2
            for c in range(64):
                if win0 + c >= fm:
                    cml[qi, r, c] = NEG
    b1r = np.ascontiguousarray(inputs["L1_ffn_b1"].reshape(E1, 32, 128).transpose(0, 2, 1))
    tri8s = np.tril(np.ones((E1, E1), np.float32), -1).T
    tri128 = np.triu(np.ones((128, 128), np.float32))
    tri32s = np.tril(np.ones((TT, TT), np.float32), -1).T
    evec8 = np.arange(E1, dtype=np.float32).reshape(E1, 1)
    itile1 = (np.arange(NT1, dtype=np.float32) * 512).reshape(1, NT1)
    i8c = np.eye(E1, dtype=np.float32)
    iota128 = np.arange(128, dtype=np.int32).reshape(128, 1)
    iota32 = np.arange(TT, dtype=np.int32).reshape(TT, 1)
    idxoff = (np.arange(TT, dtype=np.float32)[None, :] * 128.0
              + np.arange(128, dtype=np.float32)[:, None]).astype(np.float32)
    iota128f = np.arange(128, dtype=np.float32).reshape(128, 1)
    maps = []
    for b in range(NCORE):
        R = np.zeros((128, E0), np.float32)
        for e_i in range(E0):
            R[b * E0 + e_i, e_i] = 1.0
        maps.append({
            "x": x[b], "wt": wt, "gw": gw, "gb": gb, "idb": idb, "idf": idf, "cm": cm,
            "tri8s": tri8s, "tri128": tri128, "tri32s": tri32s, "evec8": evec8,
            "itile1": itile1, "i8c": i8c, "iota128": iota128, "iota32": iota32,
            "idxoff": idxoff, "iota128f": iota128f,
            "ew1": inputs["L0_ffn_ew1"], "ew3": inputs["L0_ffn_ew3"], "ew2": inputs["L0_ffn_ew2"],
            "sw1": inputs["L0_ffn_sw1"], "sw3": inputs["L0_ffn_sw3"], "sw2": inputs["L0_ffn_sw2"],
            "Ssel": S, "STsel": ST, "Rsel": R, "ones_f": ones_f,
            "wr": wr, "wqT": wqT, "wcp": wcp, "bcv": bcv, "nlk": nlk, "nlv": nlv,
            "woT": woT, "bo": bo, "cml": cml,
            "w1": inputs["L1_ffn_W1"], "b1r": b1r, "w2": inputs["L1_ffn_W2"],
            "b2": inputs["L1_ffn_b2"],
        })
    return maps


def kernel(**inputs):
    nc = build_fused()
    res = run_bass_kernel_spmd(nc, fused_inputs(inputs), core_ids=list(range(NCORE)))
    return np.stack([np.asarray(res.results[b]["outp"]) for b in range(NCORE)])


def kernel_multilaunch(**inputs):
    nc1 = build_k1()
    res1 = run_bass_kernel_spmd(nc1, k1_inputs(inputs), core_ids=list(range(NCORE)))
    nc2 = build_k2()
    res2 = run_bass_kernel_spmd(nc2, k2_inputs(inputs, res1), core_ids=list(range(NCORE)))
    nc3 = build_k3()
    res3 = run_bass_kernel_spmd(nc3, k3_inputs(inputs, res2), core_ids=list(range(NCORE)))
    nc4 = build_k4()
    res4 = run_bass_kernel_spmd(nc4, k4_inputs(inputs, res3), core_ids=list(range(NCORE)))
    out = np.stack([np.asarray(res4.results[b]["outp"]) for b in range(NCORE)])
    return out
